# version 02
# baseline (speedup 1.0000x reference)
import contextlib
import numpy as np
import ml_dtypes
import concourse.bass as bass
import concourse.mybir as mybir
from concourse.bass_utils import run_bass_kernel_spmd

F32 = mybir.dt.float32
BF16 = mybir.dt.bfloat16
AF = mybir.ActivationFunctionType
ALU = mybir.AluOpType
AX = mybir.AxisListType

D = 1024
DK = 8
DIN = 512
DBR = 256
HEADS = 8
HD = 32
PW = 2080
EW = 2084
QK_SCALE = float(HD) ** -0.5
BR = [(10, 5), (20, 10)]
H_MOE = 4096
HB = 512
N_EXP = 7
FFN_H = 2048

TILES = [(0, 512), (512, 512), (1024, 512), (1536, 512), (2048, 32)]

def _groups(w, s):
    lead = s + ((120 - s) // w) * w
    gs = [(0, lead)]
    pos = lead
    while pos + 120 <= EW - 30:
        gs.append((pos, 120))
        pos += 120
    gs.append((pos, EW - pos))
    return gs

GROUPS = {j: _groups(*BR[j]) for j in (0, 1)}

def _slot(j, g):
    return 3 * j + (0 if g == 0 else (2 if g == len(GROUPS[j]) - 1 else 1))

NSLOT = 6
BIAS_W = 8 * 120

_vc = {}
_vn = 0
def _vcol(name, n):
    global _vn
    _vc[name] = _vn
    _vn += n
for _nm, _n in [("down_b", 4), ("up_b", 8), ("norm1_w", 4), ("norm2_w", 8),
                ("ffn_b1", 16), ("ffn_b2", 8), ("proj_b", 4),
                ("qkv_b", 24), ("eps6", 1), ("eps5", 1)]:
    _vcol(_nm, _n)
NV = _vn

ENGINES = ("te", "act", "dve", "sp", "pool")

class Reg:
    __slots__ = ("w", "r")
    def __init__(self):
        self.w = None
        self.r = []

class _Op:
    __slots__ = ("fn", "inc", "waits", "needed", "sem", "gidx")
    def __init__(self, fn, inc, waits, sem, gidx):
        self.fn = fn; self.inc = inc; self.waits = waits
        self.needed = False; self.sem = sem; self.gidx = gidx

class Prog:
    def __init__(self):
        self.ops = {e: [] for e in ENGINES}
        self.wm = {e: {} for e in ENGINES}
        self.gidx = 0

    def emit(self, eng, fn, reads=(), writes=(), dma=False, boot=False, sem=None):
        deps = {}
        def add(dep):
            deng, didx = dep
            if deng == eng == "te":
                return
            if deng == eng and dma and self.ops[deng][didx].inc == 16:
                return
            key = (deng, self.ops[deng][didx].sem)
            cur = deps.get(key)
            if cur is None or cur[1] < didx:
                deps[key] = dep
        for r in reads:
            if r.w is not None:
                add(r.w)
        for wreg in writes:
            if wreg.w is not None:
                add(wreg.w)
            for rd in wreg.r:
                add(rd)
        waits = []
        wm = self.wm[eng]
        for (deng, didx) in deps.values():
            key = (deng, self.ops[deng][didx].sem)
            if wm.get(key, -1) < didx:
                waits.append((deng, didx))
                wm[key] = didx
        if sem is None:
            sem = "boot" if boot else ("poolc" if (eng == "pool" and not dma) else eng)
        op = _Op(fn, 16 if dma else 1, waits, sem, self.gidx)
        self.gidx += 1
        self.ops[eng].append(op)
        me = (eng, len(self.ops[eng]) - 1)
        for wreg in writes:
            wreg.w = me
            wreg.r = []
        for r in reads:
            r.r.append(me)
        return me

    @staticmethod
    def barrier(newreg, olds):
        for o in olds:
            if o.w is not None:
                newreg.r.append(o.w)
            newreg.r.extend(o.r)

    def finalize(self, block, sems, trunc=None):
        if trunc is not None:
            for e in ENGINES:
                self.ops[e] = [op for op in self.ops[e] if op.gidx < trunc]
        for e in ENGINES:
            for op in self.ops[e]:
                if op.sem == "boot" or op.inc == 16:
                    op.needed = True
                for (deng, didx) in op.waits:
                    self.ops[deng][didx].needed = True
        val = {}
        cum = {}
        for e in ENGINES:
            v = []
            for op in self.ops[e]:
                if op.needed:
                    cum[op.sem] = cum.get(op.sem, 0) + op.inc
                v.append(cum.get(op.sem, 0))
            val[e] = v
        boot_total = cum.get("boot", 0)
        def body(e):
            mysems = {}
            for op in self.ops[e]:
                if op.needed and op.inc == 16:
                    mysems[op.sem] = cum.get(op.sem, 0)
            def run(eng):
                for op in self.ops[e]:
                    for (deng, didx) in op.waits:
                        t = self.ops[deng][didx]
                        if t.sem == "boot":
                            eng.wait_ge(sems["boot"], boot_total)
                        else:
                            eng.wait_ge(sems[t.sem], val[deng][didx])
                    inst = op.fn(eng)
                    if op.needed:
                        inst.then_inc(sems[op.sem], op.inc)
                for s, tot in sorted(mysems.items()):
                    eng.wait_ge(sems[s], tot)
            return run
        if self.ops["te"]:
            block.tensor(body("te"))
        if self.ops["act"]:
            block.scalar(body("act"))
        if self.ops["dve"]:
            block.vector(body("dve"))
        if self.ops["sp"]:
            block.sync(body("sp"))
        if self.ops["pool"]:
            block.gpsimd(body("pool"))


def build_program(taps=(), trunc=None):
    nc = bass.Bass("TRN2", target_bir_lowering=False)
    ap = {}
    def din(name, shape, dt=F32):
        ap[name] = nc.dram_tensor(name, list(shape), dt, kind="ExternalInput").ap()
        return ap[name]
    din("xT", [D, PW])
    din("bias6", [NSLOT, 128, BIAS_W])
    din("w_down", [D, DIN]); din("w_up", [DIN, D])
    din("w_qkv1", [DBR, 3 * DBR]); din("w_qkv2", [DBR, 3 * DBR])
    din("w_projp", [2, HEADS * HD, DBR])
    din("w_ffn1", [D, FFN_H]); din("w_ffn2", [FFN_H, D])
    din("w_router", [D, 8])
    din("w_moe_up", [N_EXP, D, H_MOE]); din("w_moe_gate", [N_EXP, D, H_MOE])
    din("w_moe_down", [N_EXP, H_MOE, D])
    din("w_sh_up", [D, H_MOE]); din("w_sh_gate", [D, H_MOE]); din("w_sh_down", [H_MOE, D])
    din("vecs", [NV, 128])
    din("ident_f", [128, 128]); din("ident_b", [128, 128], BF16)
    din("ones_r", [1, 128]); din("ones_c", [128, 1], BF16)
    din("ident2", [64, 32], BF16)
    din("tri_u", [8, 8])
    din("sel8", [8, N_EXP * 128])
    out_ap = nc.dram_tensor("out", [PW, D], F32, kind="ExternalOutput").ap()
    tap_aps = {t: nc.dram_tensor("tap_" + t, list(s), F32, kind="ExternalOutput").ap()
               for (t, s) in taps}

    p = Prog()
    cm = contextlib.ExitStack()
    with cm:
        sb = lambda name, shape, dt=F32: cm.enter_context(nc.sbuf_tensor(name, shape, dt))
        R2 = sb("R2", [128, DK * PW])
        XBA = sb("XBA", [128, 2 * 4 * EW], BF16)
        XIB_OFF, O12_OFF, HBF_OFF = 0, 4 * EW, 0
        SCF = sb("SCF", [128, 2560])
        wup = [sb(f"wup{i}", [128, 8 * HB], BF16) for i in range(2)]
        wgt = [sb(f"wgt{i}", [128, 8 * HB], BF16) for i in range(2)]
        wdn = [sb(f"wdn{i}", [128, 4 * D], BF16) for i in range(2)]
        xbf = sb("xbf", [128, 4 * 512], BF16)
        hid = sb("hid", [128, 4 * 512], BF16)
        rW = sb("rW", [128, 8 * 8], BF16)
        vec = sb("vec", [128, NV])
        idf = sb("idf", [128, 128]); idb = sb("idb", [128, 128], BF16)
        onr = sb("onr", [1, 128]); onc = sb("onc", [128, 1], BF16)
        id2 = sb("id2_sb", [64, 32], BF16)
        triu = sb("triu", [8, 8])
        sel8 = sb("sel8_sb", [8, N_EXP * 128])
        qp = [sb(f"qp{i}", [64, 128], BF16) for i in range(4)]
        kp = [sb(f"kp{i}", [64, 128], BF16) for i in range(4)]
        vp = [sb(f"vp{i}", [64, 128], BF16) for i in range(4)]
        oh = [sb(f"oh{i}", [32, 128], BF16) for i in range(8)]
        sct = [sb("sct0", [128, 120]), sb("sct1", [128, 120])]
        et = [sb("et0", [128, 120]), sb("et1", [128, 120])]
        atb = [sb("atb0", [128, 120], BF16), sb("atb1", [128, 120], BF16)]
        atT = [sb("atT0", [128, 120], BF16), sb("atT1", [128, 120], BF16)]
        vT = [sb("vT0", [128, 32], BF16), sb("vT1", [128, 32], BF16)]
        nmax = [sb("nmax0", [128, 4]), sb("nmax1", [128, 4])]
        sc0 = sb("sc0", [128, 512]); sc1 = sb("sc1", [128, 512])
        sbf = [sb("sbf0", [128, 512], BF16), sb("sbf1", [128, 512], BF16)]
        rowst = sb("rowst", [1, 1536])
        m2all = sb("m2all", [8, PW])
        lgF = sb("lgF", [8, 512])
        ltm = sb("ltm", [128, 32])
        eqtm = sb("eqtm", [128, 32])
        mxp = sb("mxp", [128, 8])
        eqT = sb("eqT", [8, 512])
        v2r = sb("v2r", [1, 512])
        ttile = sb("ttile", [128, 1024])
        pb = [cm.enter_context(nc.psum_tensor(f"pb{i}", [128, 512], F32)) for i in range(6)]
        pT = [cm.enter_context(nc.psum_tensor(f"pT{i}", [128, 1024], BF16)) for i in range(2)]

        RG = Reg
        r_xT = RG()
        r_R2 = RG(); r_XIB = RG(); r_O12 = RG(); r_HBF = RG()
        r_SCF_xipre = RG(); r_SCF_xf = RG(); r_SCF_m2b = [RG() for _ in range(5)]
        r_SCF_lnsq = RG()
        r_wup = [RG(), RG()]; r_wgt = [RG(), RG()]; r_wdn = [RG(), RG()]
        r_xbf = RG(); r_hid = [RG() for _ in range(4)]; r_rW = RG(); r_vec = RG()
        r_idf = RG(); r_idb = RG(); r_id2 = RG(); r_onr = RG(); r_onc = RG(); r_triu = RG(); r_sel8 = RG()
        r_qp = [RG() for _ in range(4)]; r_kp = [RG() for _ in range(4)]; r_vp = [RG() for _ in range(4)]
        r_oh = [RG() for _ in range(8)]
        r_sct = [RG(), RG()]; r_et = [RG(), RG()]; r_atb = [RG(), RG()]
        r_atT = [RG(), RG()]; r_vT = [RG(), RG()]; r_nmax = [RG(), RG()]
        r_sc0 = RG(); r_sc1 = RG(); r_sbf = [RG(), RG()]
        r_rowst = RG(); r_m2all = RG(); r_lgF = RG(); r_ltm = RG(); r_eqtm = RG()
        r_mxp = RG(); r_eqT = RG(); r_v2r = RG(); r_ttile = RG()
        r_pb = [RG() for _ in range(6)]; r_pT = [RG(), RG()]
        r_bias = RG()
        r_proj = RG()
        r_out = RG()
        r_taps = {t: RG() for t in tap_aps}

        bank_i = [0]
        def nbank():
            b = bank_i[0] % 6
            bank_i[0] += 1
            return pb[b], r_pb[b]

        E = p.emit

        for vi in range(NV):
            E("pool", lambda g, vi=vi: g.dma_start(
                out=vec[:, vi:vi + 1],
                in_=ap["vecs"][vi].rearrange("(p o) -> p o", o=1)),
              writes=[r_vec], dma=True, boot=True)
        E("pool", lambda g: g.dma_start(out=idf[:], in_=ap["ident_f"]), writes=[r_idf], dma=True, boot=True)
        E("pool", lambda g: g.dma_start(out=idb[:], in_=ap["ident_b"]), writes=[r_idb], dma=True, boot=True)
        E("pool", lambda g: g.dma_start(out=onr[:], in_=ap["ones_r"]), writes=[r_onr], dma=True, boot=True)
        E("pool", lambda g: g.dma_start(out=onc[:], in_=ap["ones_c"]), writes=[r_onc], dma=True, boot=True)
        E("pool", lambda g: g.dma_start(out=id2[:], in_=ap["ident2"]), writes=[r_id2], dma=True, boot=True)
        E("pool", lambda g: g.dma_start(out=triu[:], in_=ap["tri_u"]), writes=[r_triu], dma=True, boot=True)
        E("pool", lambda g: g.dma_start(out=sel8[:], in_=ap["sel8"]), writes=[r_sel8], dma=True, boot=True)

        def bias_dst(sl):
            if sl < 4:
                return wup[1][:, 960 * sl:960 * (sl + 1)]
            return wdn[1][:, 960 * (sl - 4):960 * (sl - 3)]
        for sl in range(NSLOT):
            E("pool", lambda g, sl=sl: g.dma_start(out=bias_dst(sl), in_=ap["bias6"][sl]),
              writes=[r_bias], dma=True, boot=True)
        E("pool", lambda g: g.dma_start(out=wup[0][:].rearrange("p (k m) -> p k m", k=8),
                                        in_=ap["w_down"].rearrange("(k p) m -> p k m", p=128)),
          writes=[r_wup[0]], dma=True, boot=True)
        E("pool", lambda g: g.dma_start(out=wdn[0][:].rearrange("p (k m) -> p k m", k=4),
                                        in_=ap["w_up"].rearrange("(k p) m -> p k m", p=128)),
          writes=[r_wdn[0]], dma=True, boot=True)
        for j in range(2):
            E("pool", lambda g, j=j: g.dma_start(
                out=wgt[j][:, 0:1536].rearrange("p (k m) -> p k m", k=2),
                in_=ap["w_qkv" + str(j + 1)].rearrange("(k p) m -> p k m", p=128)),
              writes=[r_wgt[j]], dma=True, boot=True)
            E("pool", lambda g, j=j: g.dma_start(
                out=wgt[j][0:32, 1536:3584].rearrange("p (h m) -> p h m", h=8),
                in_=ap["w_projp"][j].rearrange("(h p) m -> p h m", p=32)),
              writes=[r_proj], dma=True, boot=True)
        E("pool", lambda g: g.dma_start(out=rW[:].rearrange("p (k e) -> p k e", k=8),
                                        in_=ap["w_router"].rearrange("(k p) e -> p k e", p=128)),
          writes=[r_rW], dma=True, boot=True)

        vslice = lambda name, i: vec[:, _vc[name] + i:_vc[name] + i + 1]
        v64 = lambda i: vec[0:64, _vc["qkv_b"] + i:_vc["qkv_b"] + i + 1]

        xT_v = ap["xT"].rearrange("(k p) t -> p k t", p=128)
        marks = {"boot": p.gidx}

        def tap_R2(t):
            if t in tap_aps:
                for mo in range(DK):
                    E("sp", lambda s_, mo=mo, t=t: s_.dma_start(
                        out=tap_aps[t][mo * 128:(mo + 1) * 128, :], in_=R2[:, mo * PW:(mo + 1) * PW]),
                      reads=[r_R2], writes=[r_taps[t]], dma=True)

        def tap_XBA(t, off, rr, nch, width):
            if t in tap_aps:
                for mo in range(nch):
                    for seg in range(0, width, 512):
                        w = min(512, width - seg)
                        E("act", lambda a, mo=mo, off=off, seg=seg, w=w: a.copy(
                            out=sc0[:, 0:w], in_=XBA[:, off + mo * width + seg:off + mo * width + seg + w]),
                          reads=[rr], writes=[r_sc0])
                        E("sp", lambda s_, mo=mo, seg=seg, w=w, t=t: s_.dma_start(
                            out=tap_aps[t][mo * 128:(mo + 1) * 128, seg:seg + w], in_=sc0[:, 0:w]),
                          reads=[r_sc0], writes=[r_taps[t]], dma=True)

        for (t0, tl) in TILES:
            dbanks = [nbank() for _ in range(4)]
            for kh in range(2):
                E("pool", lambda g, kh=kh, t0=t0, tl=tl: g.dma_start(
                    out=xbf[:, 0:4 * tl].rearrange("p (k t) -> p k t", k=4),
                    in_=xT_v[:, 4 * kh:4 * kh + 4, t0:t0 + tl]),
                  reads=[r_xT], writes=[r_xbf], dma=True, sem="sxbf")
                for mo in range(4):
                    bk, rb = dbanks[mo]
                    for k4 in range(4):
                        k = 4 * kh + k4
                        E("te", lambda t, k=k, k4=k4, mo=mo, tl=tl, bk=bk: t.matmul(
                            out=bk[:, 0:tl],
                            lhsT=wup[0][:, (k * 4 + mo) * 128:(k * 4 + mo + 1) * 128],
                            rhs=xbf[:, k4 * tl:(k4 + 1) * tl],
                            start=(k == 0), stop=(k == 7)),
                          reads=[r_wup[0], r_xbf], writes=[rb])
            sqb, sqr = nbank()
            for mo in range(4):
                bk, rb = dbanks[mo]
                E("act", lambda a, mo=mo, tl=tl, bk=bk: a.add(
                    out=SCF[:, mo * 512:mo * 512 + tl], in_=bk[:, 0:tl],
                    add=vslice("down_b", mo)),
                  reads=[rb, r_vec], writes=[r_SCF_xipre])
                E("act", lambda a, mo=mo, tl=tl, bk=bk: a.activation(
                    out=sbf[mo % 2][:, 0:tl], in_=bk[:, 0:tl], func=AF.Square,
                    bias=vslice("down_b", mo), scale=1.0),
                  reads=[rb, r_vec], writes=[r_sbf[mo % 2]])
                E("te", lambda t, mo=mo, tl=tl, sqb=sqb: t.matmul(
                    out=sqb[0:1, 0:tl], lhsT=onc[:], rhs=sbf[mo % 2][:, 0:tl],
                    start=(mo == 0), stop=(mo == 3)),
                  reads=[r_sbf[mo % 2], r_onc], writes=[sqr])
            E("act", lambda a, sqb=sqb, tl=tl: a.activation(
                out=rowst[:, 0:tl], in_=sqb[0:1, 0:tl], func=AF.Sqrt,
                bias=vec[0:1, _vc["eps6"]:_vc["eps6"] + 1], scale=1.0 / 512.0),
              reads=[sqr], writes=[r_rowst])
            E("dve", lambda v, tl=tl: v.reciprocal(out=rowst[:, 512:512 + tl], in_=rowst[:, 0:tl]),
              reads=[r_rowst], writes=[r_rowst])
            bb, rbb = nbank()
            E("te", lambda t, tl=tl, bb=bb: t.matmul(
                out=bb[:, 0:tl], lhsT=onr[:], rhs=rowst[:, 512:512 + tl], start=True, stop=True),
              reads=[r_rowst, r_onr], writes=[rbb])
            for mo in range(4):
                E("dve", lambda v, mo=mo, tl=tl, bb=bb: v.tensor_tensor(
                    out=sc0[:, 0:tl], in0=SCF[:, mo * 512:mo * 512 + tl],
                    in1=bb[:, 0:tl], op=ALU.mult),
                  reads=[r_SCF_xipre, rbb], writes=[r_sc0])
                E("dve", lambda v, mo=mo, t0=t0, tl=tl: v.tensor_scalar(
                    out=XBA[:, XIB_OFF + mo * EW + t0:XIB_OFF + mo * EW + t0 + tl],
                    in0=sc0[:, 0:tl], scalar1=vslice("norm1_w", mo), scalar2=None, op0=ALU.mult),
                  reads=[r_sc0, r_vec], writes=[r_XIB])
        for mo in range(4):
            E("pool", lambda g, mo=mo: g.memset(
                XBA[:, XIB_OFF + mo * EW + 2076:XIB_OFF + mo * EW + 2084], 0.0),
              writes=[r_XIB])

        marks["p1"] = p.gidx
        tap_XBA("xi", XIB_OFF, r_XIB, 4, EW)
        for j in (0, 1):
            for g, (g0, gl) in enumerate(GROUPS[j]):
                sl = _slot(j, g)
                for mc in range(6):
                    bk, rb = nbank()
                    for k in range(2):
                        kc = 2 * j + k
                        E("te", lambda t, mc=mc, k=k, kc=kc, g0=g0, gl=gl, j=j, bk=bk: t.matmul(
                            out=bk[:, 0:gl],
                            lhsT=wgt[j][:, (k * 6 + mc) * 128:(k * 6 + mc + 1) * 128],
                            rhs=XBA[:, XIB_OFF + kc * EW + g0:XIB_OFF + kc * EW + g0 + gl],
                            start=(k == 0), stop=(k == 1)),
                          reads=[r_wgt[j], r_XIB], writes=[rb])
                    kind, sub = divmod(mc, 2)
                    tgt = (qp, kp, vp)[kind]
                    rtg = (r_qp, r_kp, r_vp)[kind]
                    for half in range(2):
                        pi = sub * 2 + half
                        E("dve", lambda v, half=half, pi=pi, gl=gl, j=j, mc=mc, tgt=tgt, bk=bk: v.tensor_scalar(
                            out=tgt[pi][:, 0:gl], in0=bk[64 * half:64 * (half + 1), 0:gl],
                            scalar1=v64(j * 12 + mc * 2 + half), scalar2=None, op0=ALU.add),
                          reads=[rb, r_vec], writes=[rtg[pi]])
                sbanks = []
                for h in range(HEADS):
                    pi, su = divmod(h, 2)
                    bk, rb = nbank()
                    E("te", lambda t, pi=pi, su=su, gl=gl, bk=bk: t.matmul(
                        out=bk[0:gl, 0:gl], lhsT=qp[pi][32 * su:32 * (su + 1), 0:gl],
                        rhs=kp[pi][32 * su:32 * (su + 1), 0:gl], start=True, stop=True),
                      reads=[r_qp[pi], r_kp[pi]], writes=[rb])
                    sbanks.append((bk, rb))
                for h in range(HEADS):
                    pi, su = divmod(h, 2)
                    pa = h % 2
                    bk, rb = sbanks[h]
                    E("dve", lambda v, gl=gl, sl=sl, h=h, bk=bk, pa=pa: v.scalar_tensor_tensor(
                        out=sct[pa][0:gl, 0:gl], in0=bk[0:gl, 0:gl], scalar=QK_SCALE,
                        in1=bias_dst(sl)[0:gl, 120 * h:120 * h + gl],
                        op0=ALU.mult, op1=ALU.add),
                      reads=[rb, r_bias], writes=[r_sct[pa]])
                    E("dve", lambda v, gl=gl, pa=pa: v.reduce_max(
                        out=nmax[pa][0:gl, 0:1], in_=sct[pa][0:gl, 0:gl], axis=AX.X, negate=True),
                      reads=[r_sct[pa]], writes=[r_nmax[pa]])
                    E("act", lambda a, gl=gl, pa=pa: a.activation(
                        out=et[pa][0:gl, 0:gl], in_=sct[pa][0:gl, 0:gl], func=AF.Exp,
                        bias=nmax[pa][0:gl, 0:1], scale=1.0),
                      reads=[r_sct[pa], r_nmax[pa]], writes=[r_et[pa]])
                    E("dve", lambda v, gl=gl, pa=pa: v.reduce_sum(
                        out=nmax[pa][0:gl, 1:2], in_=et[pa][0:gl, 0:gl], axis=AX.X),
                      reads=[r_et[pa]], writes=[r_nmax[pa]])
                    E("dve", lambda v, gl=gl, pa=pa: v.reciprocal(
                        out=nmax[pa][0:gl, 2:3], in_=nmax[pa][0:gl, 1:2]),
                      reads=[r_nmax[pa]], writes=[r_nmax[pa]])
                    E("dve", lambda v, gl=gl, pa=pa: v.tensor_scalar(
                        out=atb[pa][0:gl, 0:gl], in0=et[pa][0:gl, 0:gl],
                        scalar1=nmax[pa][0:gl, 2:3], scalar2=None, op0=ALU.mult),
                      reads=[r_et[pa], r_nmax[pa]], writes=[r_atb[pa]])
                    E("te", lambda t, gl=gl, pa=pa: t.transpose(
                        out=pT[pa][0:gl, 0:gl], in_=atb[pa][0:gl, 0:gl], identity=idb[0:gl, 0:gl]),
                      reads=[r_atb[pa], r_idb], writes=[r_pT[pa]])
                    E("te", lambda t, pi=pi, su=su, gl=gl, pa=pa: t.transpose(
                        out=pT[pa][0:gl, 512:544], in_=vp[pi][32 * su:32 * (su + 1), 0:gl],
                        identity=id2[32 * su:32 * (su + 1), 0:32]),
                      reads=[r_vp[pi], r_id2], writes=[r_pT[pa]])
                    E("dve", lambda v, gl=gl, pa=pa: v.tensor_copy(
                        out=atT[pa][0:gl, 0:gl], in_=pT[pa][0:gl, 0:gl]),
                      reads=[r_pT[pa]], writes=[r_atT[pa]])
                    E("dve", lambda v, gl=gl, pa=pa: v.tensor_copy(
                        out=vT[pa][0:gl, 0:32], in_=pT[pa][0:gl, 512:544]),
                      reads=[r_pT[pa]], writes=[r_vT[pa]])
                    bk2, rb2 = nbank()
                    E("te", lambda t, gl=gl, bk2=bk2, pa=pa: t.matmul(
                        out=bk2[0:32, 0:gl], lhsT=vT[pa][0:gl, 0:32], rhs=atT[pa][0:gl, 0:gl],
                        start=True, stop=True),
                      reads=[r_vT[pa], r_atT[pa]], writes=[rb2])
                    E("dve", lambda v, h=h, gl=gl, bk2=bk2: v.tensor_copy(
                        out=oh[h][:, 0:gl], in_=bk2[0:32, 0:gl]),
                      reads=[rb2], writes=[r_oh[h]])
                for mo in range(2):
                    bk, rb = nbank()
                    for h in range(HEADS):
                        E("te", lambda t, h=h, mo=mo, gl=gl, j=j, bk=bk: t.matmul(
                            out=bk[:, 0:gl],
                            lhsT=wgt[j][0:32, 1536 + h * 256 + mo * 128:1536 + h * 256 + (mo + 1) * 128],
                            rhs=oh[h][:, 0:gl], start=(h == 0), stop=(h == 7)),
                          reads=[r_proj, r_oh[h]], writes=[rb])
                    oc = 2 * j + mo
                    E("dve", lambda v, oc=oc, g0=g0, gl=gl, bk=bk: v.tensor_scalar(
                        out=XBA[:, O12_OFF + oc * EW + g0:O12_OFF + oc * EW + g0 + gl],
                        in0=bk[:, 0:gl], scalar1=vslice("proj_b", oc), scalar2=None, op0=ALU.add),
                      reads=[rb, r_vec], writes=[r_O12])

        marks["p2"] = p.gidx
        tap_XBA("o12", O12_OFF, r_O12, 4, EW)
        Prog.barrier(r_SCF_xf, [r_SCF_xipre])
        for (t0, tl) in TILES:
            for kh in range(2):
                E("sp", lambda s_, kh=kh, t0=t0, tl=tl: s_.dma_start(
                    out=SCF[:, 0:4 * tl].rearrange("p (k t) -> p k t", k=4),
                    in_=xT_v[:, 4 * kh:4 * kh + 4, t0:t0 + tl]),
                  reads=[r_xT], writes=[r_SCF_xf], dma=True)
                for mo4 in range(4):
                    mo = 4 * kh + mo4
                    bk, rb = nbank()
                    for k in range(4):
                        E("te", lambda t, k=k, mo=mo, t0=t0, tl=tl, bk=bk: t.matmul(
                            out=bk[:, 0:tl], lhsT=wdn[0][:, (k * 8 + mo) * 128:(k * 8 + mo + 1) * 128],
                            rhs=XBA[:, O12_OFF + k * EW + t0:O12_OFF + k * EW + t0 + tl],
                            start=(k == 0), stop=(k == 3)),
                          reads=[r_wdn[0], r_O12], writes=[rb])
                    E("dve", lambda v, mo=mo, mo4=mo4, t0=t0, tl=tl, bk=bk: v.scalar_tensor_tensor(
                        out=R2[:, mo * PW + t0:mo * PW + t0 + tl], in0=bk[:, 0:tl],
                        scalar=vslice("up_b", mo), in1=SCF[:, mo4 * tl:(mo4 + 1) * tl],
                        op0=ALU.add, op1=ALU.add),
                      reads=[rb, r_vec, r_SCF_xf], writes=[r_R2])

        marks["p3"] = p.gidx
        tap_R2("x2")
        Prog.barrier(r_HBF, [r_XIB, r_O12])
        for (t0, tl) in TILES:
            s2b, s2r = nbank()
            for mo in range(DK):
                E("act", lambda a, mo=mo, t0=t0, tl=tl: a.activation(
                    out=sbf[mo % 2][:, 0:tl], in_=R2[:, mo * PW + t0:mo * PW + t0 + tl],
                    func=AF.Square),
                  reads=[r_R2], writes=[r_sbf[mo % 2]])
                E("te", lambda t, mo=mo, tl=tl, s2b=s2b: t.matmul(
                    out=s2b[0:1, 0:tl], lhsT=onc[:], rhs=sbf[mo % 2][:, 0:tl],
                    start=(mo == 0), stop=(mo == DK - 1)),
                  reads=[r_sbf[mo % 2], r_onc], writes=[s2r])
            E("act", lambda a, s2b=s2b, tl=tl: a.activation(
                out=rowst[:, 0:tl], in_=s2b[0:1, 0:tl], func=AF.Sqrt,
                bias=vec[0:1, _vc["eps6"]:_vc["eps6"] + 1], scale=1.0 / 1024.0),
              reads=[s2r], writes=[r_rowst])
            E("dve", lambda v, tl=tl: v.reciprocal(out=rowst[:, 512:512 + tl], in_=rowst[:, 0:tl]),
              reads=[r_rowst], writes=[r_rowst])
            bb, rbb = nbank()
            E("te", lambda t, tl=tl, bb=bb: t.matmul(
                out=bb[:, 0:tl], lhsT=onr[:], rhs=rowst[:, 512:512 + tl], start=True, stop=True),
              reads=[r_rowst, r_onr], writes=[rbb])
            for mo in range(DK):
                E("dve", lambda v, mo=mo, t0=t0, tl=tl, bb=bb: v.tensor_tensor(
                    out=sc0[:, 0:tl], in0=R2[:, mo * PW + t0:mo * PW + t0 + tl],
                    in1=bb[:, 0:tl], op=ALU.mult),
                  reads=[r_R2, rbb], writes=[r_sc0])
                E("dve", lambda v, mo=mo, t0=t0, tl=tl: v.tensor_scalar(
                    out=XBA[:, HBF_OFF + mo * PW + t0:HBF_OFF + mo * PW + t0 + tl],
                    in0=sc0[:, 0:tl], scalar1=vslice("norm2_w", mo), scalar2=None, op0=ALU.mult),
                  reads=[r_sc0, r_vec], writes=[r_HBF])

        marks["p4"] = p.gidx
        Prog.barrier(r_wup[1], [r_bias])
        Prog.barrier(r_wdn[1], [r_bias])
        for blk in range(4):
            wb = blk % 2
            E("pool", lambda g, blk=blk, wb=wb: g.dma_start(
                out=wup[wb][:].rearrange("p (k h) -> p k h", k=8),
                in_=ap["w_ffn1"][:, blk * HB:(blk + 1) * HB].rearrange("(k p) h -> p k h", p=128)),
              writes=[r_wup[wb]], dma=True, sem="swup%d" % wb)
            E("pool", lambda g, blk=blk, wb=wb: g.dma_start(
                out=wdn[wb][:].rearrange("p (c d) -> p c d", c=4),
                in_=ap["w_ffn2"][blk * HB:(blk + 1) * HB, :].rearrange("(c p) d -> p c d", p=128)),
              writes=[r_wdn[wb]], dma=True, sem="swdn%d" % wb)
            for (t0, tl) in TILES:
                for m in range(4):
                    bk, rb = nbank()
                    for k in range(DK):
                        E("te", lambda t, k=k, m=m, t0=t0, tl=tl, wb=wb, bk=bk: t.matmul(
                            out=bk[:, 0:tl], lhsT=wup[wb][:, k * HB + m * 128:k * HB + (m + 1) * 128],
                            rhs=XBA[:, HBF_OFF + k * PW + t0:HBF_OFF + k * PW + t0 + tl],
                            start=(k == 0), stop=(k == DK - 1)),
                          reads=[r_wup[wb], r_HBF], writes=[rb])
                    bi = blk * 4 + m
                    E("act", lambda a, bi=bi, m=m, tl=tl, bk=bk: a.activation(
                        out=hid[:, m * 512:m * 512 + tl], in_=bk[:, 0:tl], func=AF.Gelu,
                        bias=vslice("ffn_b1", bi), scale=1.0),
                      reads=[rb, r_vec], writes=[r_hid[m]])
                for mo in range(DK):
                    bk, rb = nbank()
                    for c in range(4):
                        E("te", lambda t, c=c, mo=mo, tl=tl, wb=wb, bk=bk: t.matmul(
                            out=bk[:, 0:tl], lhsT=wdn[wb][:, c * D + mo * 128:c * D + (mo + 1) * 128],
                            rhs=hid[:, c * 512:c * 512 + tl], start=(c == 0), stop=(c == 3)),
                          reads=[r_wdn[wb], r_hid[c]], writes=[rb])
                    if blk == 0:
                        E("dve", lambda v, mo=mo, t0=t0, tl=tl, bk=bk: v.scalar_tensor_tensor(
                            out=R2[:, mo * PW + t0:mo * PW + t0 + tl], in0=bk[:, 0:tl],
                            scalar=vslice("ffn_b2", mo), in1=R2[:, mo * PW + t0:mo * PW + t0 + tl],
                            op0=ALU.add, op1=ALU.add),
                          reads=[rb, r_vec, r_R2], writes=[r_R2])
                    else:
                        E("dve", lambda v, mo=mo, t0=t0, tl=tl, bk=bk: v.tensor_tensor(
                            out=R2[:, mo * PW + t0:mo * PW + t0 + tl], in0=bk[:, 0:tl],
                            in1=R2[:, mo * PW + t0:mo * PW + t0 + tl], op=ALU.add),
                          reads=[rb, r_R2], writes=[r_R2])

        for (t0, tl) in TILES:
            for kh in range(2):
                E("sp", lambda s_, kh=kh, t0=t0, tl=tl: s_.dma_start(
                    out=SCF[:, 0:4 * tl].rearrange("p (k t) -> p k t", k=4),
                    in_=xT_v[:, 4 * kh:4 * kh + 4, t0:t0 + tl]),
                  reads=[r_xT], writes=[r_SCF_xf], dma=True)
                for mo4 in range(4):
                    mo = 4 * kh + mo4
                    E("dve", lambda v, mo=mo, mo4=mo4, t0=t0, tl=tl: v.tensor_tensor(
                        out=R2[:, mo * PW + t0:mo * PW + t0 + tl],
                        in0=R2[:, mo * PW + t0:mo * PW + t0 + tl],
                        in1=SCF[:, mo4 * tl:(mo4 + 1) * tl], op=ALU.add),
                      reads=[r_R2, r_SCF_xf], writes=[r_R2])

        marks["p5"] = p.gidx
        tap_R2("s1")
        for (t0, tl) in TILES:
            s1b, s1r = nbank()
            s2b, s2r = nbank()
            for mo in range(DK):
                E("act", lambda a, mo=mo, t0=t0, tl=tl: a.copy(
                    out=sbf[0][:, 0:tl], in_=R2[:, mo * PW + t0:mo * PW + t0 + tl]),
                  reads=[r_R2], writes=[r_sbf[0]])
                E("te", lambda t, mo=mo, tl=tl, s1b=s1b: t.matmul(
                    out=s1b[0:1, 0:tl], lhsT=onc[:], rhs=sbf[0][:, 0:tl],
                    start=(mo == 0), stop=(mo == DK - 1)),
                  reads=[r_sbf[0], r_onc], writes=[s1r])
                E("act", lambda a, mo=mo, t0=t0, tl=tl: a.activation(
                    out=sbf[1][:, 0:tl], in_=R2[:, mo * PW + t0:mo * PW + t0 + tl], func=AF.Square),
                  reads=[r_R2], writes=[r_sbf[1]])
                E("te", lambda t, mo=mo, tl=tl, s2b=s2b: t.matmul(
                    out=s2b[0:1, 0:tl], lhsT=onc[:], rhs=sbf[1][:, 0:tl],
                    start=(mo == 0), stop=(mo == DK - 1)),
                  reads=[r_sbf[1], r_onc], writes=[s2r])
            E("dve", lambda v, s1b=s1b, tl=tl: v.tensor_scalar(
                out=rowst[:, 0:tl], in0=s1b[0:1, 0:tl], scalar1=1.0 / 1024.0, scalar2=None,
                op0=ALU.mult),
              reads=[s1r], writes=[r_rowst])
            E("act", lambda a, tl=tl: a.activation(
                out=sc0[0:1, 0:tl], in_=rowst[:, 0:tl], func=AF.Square),
              reads=[r_rowst], writes=[r_sc0])
            E("dve", lambda v, s2b=s2b, tl=tl: v.scalar_tensor_tensor(
                out=rowst[:, 512:512 + tl], in0=s2b[0:1, 0:tl], scalar=1.0 / 1024.0,
                in1=sc0[0:1, 0:tl], op0=ALU.mult, op1=ALU.subtract),
              reads=[s2r, r_sc0], writes=[r_rowst])
            E("act", lambda a, tl=tl: a.activation(
                out=sc0[0:1, 0:tl], in_=rowst[:, 512:512 + tl], func=AF.Sqrt,
                bias=vec[0:1, _vc["eps5"]:_vc["eps5"] + 1], scale=1.0),
              reads=[r_rowst], writes=[r_sc0])
            E("dve", lambda v, tl=tl: v.reciprocal(
                out=rowst[:, 512:512 + tl], in_=sc0[0:1, 0:tl]),
              reads=[r_sc0], writes=[r_rowst])
            E("dve", lambda v, tl=tl: v.tensor_tensor(
                out=rowst[:, 1024:1024 + tl], in0=rowst[:, 0:tl], in1=rowst[:, 512:512 + tl],
                op=ALU.mult),
              reads=[r_rowst], writes=[r_rowst])
            rbb_b, rbb_r = nbank()
            mrb_b, mrb_r = nbank()
            E("te", lambda t, tl=tl, rbb_b=rbb_b: t.matmul(
                out=rbb_b[:, 0:tl], lhsT=onr[:], rhs=rowst[:, 512:512 + tl], start=True, stop=True),
              reads=[r_rowst, r_onr], writes=[rbb_r])
            E("te", lambda t, tl=tl, mrb_b=mrb_b: t.matmul(
                out=mrb_b[:, 0:tl], lhsT=onr[:], rhs=rowst[:, 1024:1024 + tl], start=True, stop=True),
              reads=[r_rowst, r_onr], writes=[mrb_r])
            for mo in range(DK):
                E("dve", lambda v, mo=mo, t0=t0, tl=tl, rbb_b=rbb_b: v.tensor_tensor(
                    out=sc0[:, 0:tl], in0=R2[:, mo * PW + t0:mo * PW + t0 + tl],
                    in1=rbb_b[:, 0:tl], op=ALU.mult),
                  reads=[r_R2, rbb_r], writes=[r_sc0])
                E("dve", lambda v, mo=mo, t0=t0, tl=tl, mrb_b=mrb_b: v.tensor_tensor(
                    out=R2[:, mo * PW + t0:mo * PW + t0 + tl], in0=sc0[:, 0:tl],
                    in1=mrb_b[:, 0:tl], op=ALU.subtract),
                  reads=[r_sc0, mrb_r], writes=[r_R2])
                E("act", lambda a, mo=mo, t0=t0, tl=tl: a.copy(
                    out=XBA[:, HBF_OFF + mo * PW + t0:HBF_OFF + mo * PW + t0 + tl],
                    in_=R2[:, mo * PW + t0:mo * PW + t0 + tl]),
                  reads=[r_R2], writes=[r_HBF])

        marks["p6"] = p.gidx
        tap_R2("y1")
        for (t0, tl) in TILES:
            lgb, lgr = nbank()
            for k in range(DK):
                E("te", lambda t, k=k, t0=t0, tl=tl, lgb=lgb: t.matmul(
                    out=lgb[0:8, 0:tl], lhsT=rW[:, k * 8:(k + 1) * 8],
                    rhs=XBA[:, HBF_OFF + k * PW + t0:HBF_OFF + k * PW + t0 + tl],
                    start=(k == 0), stop=(k == DK - 1)),
                  reads=[r_rW, r_HBF], writes=[lgr])
            E("dve", lambda v, tl=tl, lgb=lgb: v.tensor_copy(out=lgF[:, 0:tl], in_=lgb[0:8, 0:tl]),
              reads=[lgr], writes=[r_lgF])
            nsub = (tl + 127) // 128
            tpb, tpr = nbank()
            for ji in range(nsub):
                jl = min(128, tl - ji * 128)
                E("te", lambda t, ji=ji, jl=jl, tpb=tpb: t.transpose(
                    out=tpb[0:jl, 8 * ji:8 * ji + 8], in_=lgF[0:8, ji * 128:ji * 128 + jl],
                    identity=idf[0:8, 0:8]),
                  reads=[r_lgF, r_idf], writes=[tpr])
            E("dve", lambda v, nsub=nsub, tl=tl, tpb=tpb: v.tensor_copy(
                out=ltm[0:min(tl, 128), 0:8 * nsub] if tl < 128 else ltm[:, 0:8 * nsub],
                in_=tpb[0:min(tl, 128), 0:8 * nsub] if tl < 128 else tpb[0:128, 0:8 * nsub]),
              reads=[tpr], writes=[r_ltm])
            for ji in range(nsub):
                jl = min(128, tl - ji * 128)
                E("dve", lambda v, ji=ji, jl=jl: v.reduce_max(
                    out=mxp[0:jl, 0:1], in_=ltm[0:jl, 8 * ji:8 * ji + 8], axis=AX.X, negate=True),
                  reads=[r_ltm], writes=[r_mxp])
                E("act", lambda a, ji=ji, jl=jl: a.activation(
                    out=eqtm[0:jl, 8 * ji:8 * ji + 8], in_=ltm[0:jl, 8 * ji:8 * ji + 8],
                    func=AF.Exp, bias=mxp[0:jl, 0:1], scale=1.0),
                  reads=[r_ltm, r_mxp], writes=[r_eqtm])
                E("dve", lambda v, ji=ji, jl=jl: v.reduce_sum(
                    out=mxp[0:jl, 1:2], in_=eqtm[0:jl, 8 * ji:8 * ji + 8], axis=AX.X),
                  reads=[r_eqtm], writes=[r_mxp])
                E("dve", lambda v, jl=jl: v.reciprocal(out=mxp[0:jl, 2:3], in_=mxp[0:jl, 1:2]),
                  reads=[r_mxp], writes=[r_mxp])
                E("dve", lambda v, ji=ji, jl=jl: v.tensor_scalar(
                    out=eqtm[0:jl, 8 * ji:8 * ji + 8], in0=eqtm[0:jl, 8 * ji:8 * ji + 8],
                    scalar1=mxp[0:jl, 2:3], scalar2=None, op0=ALU.mult),
                  reads=[r_eqtm, r_mxp], writes=[r_eqtm])
                E("dve", lambda v, ji=ji, jl=jl: v.reduce_max(
                    out=mxp[0:jl, 3:4], in_=eqtm[0:jl, 8 * ji:8 * ji + 8], axis=AX.X),
                  reads=[r_eqtm], writes=[r_mxp])
                E("dve", lambda v, ji=ji, jl=jl: v.tensor_scalar(
                    out=ltm[0:jl, 8 * ji:8 * ji + 8], in0=eqtm[0:jl, 8 * ji:8 * ji + 8],
                    scalar1=mxp[0:jl, 3:4], scalar2=None, op0=ALU.is_equal),
                  reads=[r_eqtm, r_mxp], writes=[r_ltm])
                E("dve", lambda v, jl=jl: v.tensor_scalar(
                    out=mxp[0:jl, 4:5], in0=mxp[0:jl, 3:4], scalar1=2.0, scalar2=None,
                    op0=ALU.mult),
                  reads=[r_mxp], writes=[r_mxp])
                tb2, tr2 = nbank()
                E("te", lambda t, ji=ji, jl=jl, tb2=tb2: t.transpose(
                    out=tb2[0:8, 0:jl], in_=ltm[0:jl, 8 * ji:8 * ji + 8], identity=idf[0:jl, 0:jl]),
                  reads=[r_ltm, r_idf], writes=[tr2])
                E("dve", lambda v, ji=ji, jl=jl, tb2=tb2: v.tensor_copy(
                    out=eqT[:, ji * 128:ji * 128 + jl], in_=tb2[0:8, 0:jl]),
                  reads=[tr2], writes=[r_eqT])
                tb3, tr3 = nbank()
                E("te", lambda t, ji=ji, jl=jl, tb3=tb3: t.transpose(
                    out=tb3[0:1, 0:jl], in_=mxp[0:jl, 4:5], identity=idf[0:jl, 0:jl]),
                  reads=[r_mxp, r_idf], writes=[tr3])
                E("dve", lambda v, ji=ji, jl=jl, tb3=tb3: v.tensor_copy(
                    out=v2r[:, ji * 128:ji * 128 + jl], in_=tb3[0:1, 0:jl]),
                  reads=[tr3], writes=[r_v2r])
            pfb, pfr = nbank()
            E("te", lambda t, tl=tl, pfb=pfb: t.matmul(
                out=pfb[0:8, 0:tl], lhsT=triu[:], rhs=eqT[:, 0:tl], start=True, stop=True),
              reads=[r_triu, r_eqT], writes=[pfr])
            E("dve", lambda v, tl=tl, pfb=pfb: v.tensor_scalar(
                out=lgF[:, 0:tl], in0=pfb[0:8, 0:tl], scalar1=0.0, scalar2=None, op0=ALU.is_equal),
              reads=[pfr], writes=[r_lgF])
            E("dve", lambda v, tl=tl: v.tensor_tensor(
                out=eqT[:, 0:tl], in0=eqT[:, 0:tl], in1=lgF[:, 0:tl], op=ALU.mult),
              reads=[r_eqT, r_lgF], writes=[r_eqT])
            vbb, vbr = nbank()
            E("te", lambda t, tl=tl, vbb=vbb: t.matmul(
                out=vbb[0:8, 0:tl], lhsT=onr[0:1, 0:8], rhs=v2r[:, 0:tl], start=True, stop=True),
              reads=[r_onr, r_v2r], writes=[vbr])
            E("dve", lambda v, t0=t0, tl=tl, vbb=vbb: v.tensor_tensor(
                out=m2all[:, t0:t0 + tl], in0=eqT[:, 0:tl], in1=vbb[0:8, 0:tl], op=ALU.mult),
              reads=[r_eqT, vbr], writes=[r_m2all])

        marks["p7"] = p.gidx
        if "m2" in tap_aps:
            E("sp", lambda s_: s_.dma_start(out=tap_aps["m2"][:, :], in_=m2all[:]),
              reads=[r_m2all], writes=[r_taps["m2"]], dma=True)
        for ti in range(5):
            Prog.barrier(r_SCF_m2b[ti], [r_SCF_xf])
        Prog.barrier(r_wgt[0], [r_proj])
        Prog.barrier(r_wgt[1], [r_proj])
        for e in range(8):
            routed = e < 7
            wb = e % 2
            if routed:
                for ti, (t0, tl) in enumerate(TILES):
                    bk, rb = nbank()
                    E("te", lambda t, e=e, t0=t0, tl=tl, bk=bk: t.matmul(
                        out=bk[:, 0:tl], lhsT=sel8[:, e * 128:(e + 1) * 128],
                        rhs=m2all[:, t0:t0 + tl], start=True, stop=True),
                      reads=[r_sel8, r_m2all], writes=[rb])
                    E("act", lambda a, ti=ti, tl=tl, bk=bk: a.copy(
                        out=SCF[:, ti * 512:ti * 512 + tl], in_=bk[:, 0:tl]),
                      reads=[rb], writes=[r_SCF_m2b[ti]])
            for hb in range(8):
                if routed:
                    upv = ap["w_moe_up"][e, :, hb * HB:(hb + 1) * HB]
                    gtv = ap["w_moe_gate"][e, :, hb * HB:(hb + 1) * HB]
                    dnv = ap["w_moe_down"][e, hb * HB:(hb + 1) * HB, :]
                else:
                    upv = ap["w_sh_up"][:, hb * HB:(hb + 1) * HB]
                    gtv = ap["w_sh_gate"][:, hb * HB:(hb + 1) * HB]
                    dnv = ap["w_sh_down"][hb * HB:(hb + 1) * HB, :]
                E("pool", lambda g, upv=upv, wb=wb: g.dma_start(
                    out=wup[wb][:].rearrange("p (k h) -> p k h", k=8),
                    in_=upv.rearrange("(k p) h -> p k h", p=128)),
                  writes=[r_wup[wb]], dma=True, sem="swup%d" % wb)
                E("pool", lambda g, gtv=gtv, wb=wb: g.dma_start(
                    out=wgt[wb][:].rearrange("p (k h) -> p k h", k=8),
                    in_=gtv.rearrange("(k p) h -> p k h", p=128)),
                  writes=[r_wgt[wb]], dma=True, sem="swgt%d" % wb)
                E("pool", lambda g, dnv=dnv, wb=wb: g.dma_start(
                    out=wdn[wb][:].rearrange("p (c d) -> p c d", c=4),
                    in_=dnv.rearrange("(c p) d -> p c d", p=128)),
                  writes=[r_wdn[wb]], dma=True, sem="swdn%d" % wb)
                for ti, (t0, tl) in enumerate(TILES):
                    for m in range(4):
                        bU, rU = nbank()
                        bG, rG = nbank()
                        for k in range(DK):
                            E("te", lambda t, k=k, m=m, t0=t0, tl=tl, wb=wb, bU=bU: t.matmul(
                                out=bU[:, 0:tl], lhsT=wup[wb][:, k * HB + m * 128:k * HB + (m + 1) * 128],
                                rhs=XBA[:, HBF_OFF + k * PW + t0:HBF_OFF + k * PW + t0 + tl],
                                start=(k == 0), stop=(k == DK - 1)),
                              reads=[r_wup[wb], r_HBF], writes=[rU])
                        for k in range(DK):
                            E("te", lambda t, k=k, m=m, t0=t0, tl=tl, wb=wb, bG=bG: t.matmul(
                                out=bG[:, 0:tl], lhsT=wgt[wb][:, k * HB + m * 128:k * HB + (m + 1) * 128],
                                rhs=XBA[:, HBF_OFF + k * PW + t0:HBF_OFF + k * PW + t0 + tl],
                                start=(k == 0), stop=(k == DK - 1)),
                              reads=[r_wgt[wb], r_HBF], writes=[rG])
                        E("act", lambda a, tl=tl, bG=bG: a.activation(
                            out=sc1[:, 0:tl], in_=bG[:, 0:tl], func=AF.Sigmoid),
                          reads=[rG], writes=[r_sc1])
                        E("dve", lambda v, tl=tl, bG=bG: v.tensor_tensor(
                            out=sc0[:, 0:tl], in0=bG[:, 0:tl], in1=sc1[:, 0:tl], op=ALU.mult),
                          reads=[rG, r_sc1], writes=[r_sc0])
                        if routed:
                            E("dve", lambda v, tl=tl, bU=bU: v.tensor_tensor(
                                out=sc0[:, 0:tl], in0=bU[:, 0:tl], in1=sc0[:, 0:tl], op=ALU.mult),
                              reads=[rU, r_sc0], writes=[r_sc0])
                            E("dve", lambda v, m=m, ti=ti, tl=tl: v.tensor_tensor(
                                out=hid[:, m * 512:m * 512 + tl], in0=sc0[:, 0:tl],
                                in1=SCF[:, ti * 512:ti * 512 + tl], op=ALU.mult),
                              reads=[r_sc0, r_SCF_m2b[ti]], writes=[r_hid[m]])
                        else:
                            E("dve", lambda v, m=m, tl=tl, bU=bU: v.tensor_tensor(
                                out=hid[:, m * 512:m * 512 + tl], in0=bU[:, 0:tl],
                                in1=sc0[:, 0:tl], op=ALU.mult),
                              reads=[rU, r_sc0], writes=[r_hid[m]])
                    for mo in range(DK):
                        bk, rb = nbank()
                        for c in range(4):
                            E("te", lambda t, c=c, mo=mo, tl=tl, wb=wb, bk=bk: t.matmul(
                                out=bk[:, 0:tl], lhsT=wdn[wb][:, c * D + mo * 128:c * D + (mo + 1) * 128],
                                rhs=hid[:, c * 512:c * 512 + tl], start=(c == 0), stop=(c == 3)),
                              reads=[r_wdn[wb], r_hid[c]], writes=[rb])
                        E("dve", lambda v, mo=mo, t0=t0, tl=tl, bk=bk: v.tensor_tensor(
                            out=R2[:, mo * PW + t0:mo * PW + t0 + tl], in0=bk[:, 0:tl],
                            in1=R2[:, mo * PW + t0:mo * PW + t0 + tl], op=ALU.add),
                          reads=[rb, r_R2], writes=[r_R2])

        marks["p8"] = p.gidx
        tap_R2("racc")
        Prog.barrier(r_SCF_lnsq, r_SCF_m2b)
        subs = [(i * 128, 128) for i in range(16)] + [(2048, 32)]
        for (u0, ul) in subs:
            tb0, tr0 = nbank()
            tb1, tr1 = nbank()
            for mo in range(DK):
                tb, tr = (tb0, tr0) if mo < 4 else (tb1, tr1)
                E("te", lambda t, mo=mo, u0=u0, ul=ul, tb=tb: t.transpose(
                    out=tb[0:ul, (mo % 4) * 128:(mo % 4) * 128 + 128],
                    in_=R2[:, mo * PW + u0:mo * PW + u0 + ul], identity=idf[:]),
                  reads=[r_R2, r_idf], writes=[tr])
            E("dve", lambda v, ul=ul, tb0=tb0: v.tensor_copy(
                out=ttile[0:ul, 0:512], in_=tb0[0:ul, 0:512]),
              reads=[tr0], writes=[r_ttile])
            E("dve", lambda v, ul=ul, tb1=tb1: v.tensor_copy(
                out=ttile[0:ul, 512:1024], in_=tb1[0:ul, 0:512]),
              reads=[tr1], writes=[r_ttile])
            E("dve", lambda v, ul=ul: v.reduce_sum(
                out=mxp[0:ul, 0:1], in_=ttile[0:ul, :], axis=AX.X),
              reads=[r_ttile], writes=[r_mxp])
            E("dve", lambda v, ul=ul: v.tensor_scalar(
                out=mxp[0:ul, 1:2], in0=mxp[0:ul, 0:1], scalar1=1.0 / 1024.0, scalar2=None,
                op0=ALU.mult),
              reads=[r_mxp], writes=[r_mxp])
            E("act", lambda a, ul=ul: a.activation(
                out=SCF[0:ul, 0:1024], in_=ttile[0:ul, :], func=AF.Square),
              reads=[r_ttile], writes=[r_SCF_lnsq])
            E("dve", lambda v, ul=ul: v.reduce_sum(
                out=mxp[0:ul, 2:3], in_=SCF[0:ul, 0:1024], axis=AX.X),
              reads=[r_SCF_lnsq], writes=[r_mxp])
            E("act", lambda a, ul=ul: a.activation(
                out=mxp[0:ul, 3:4], in_=mxp[0:ul, 1:2], func=AF.Square),
              reads=[r_mxp], writes=[r_mxp])
            E("dve", lambda v, ul=ul: v.scalar_tensor_tensor(
                out=mxp[0:ul, 4:5], in0=mxp[0:ul, 2:3], scalar=1.0 / 1024.0,
                in1=mxp[0:ul, 3:4], op0=ALU.mult, op1=ALU.subtract),
              reads=[r_mxp], writes=[r_mxp])
            E("act", lambda a, ul=ul: a.activation(
                out=mxp[0:ul, 5:6], in_=mxp[0:ul, 4:5], func=AF.Sqrt,
                bias=vec[0:ul, _vc["eps5"]:_vc["eps5"] + 1], scale=1.0),
              reads=[r_mxp], writes=[r_mxp])
            E("dve", lambda v, ul=ul: v.reciprocal(out=mxp[0:ul, 6:7], in_=mxp[0:ul, 5:6]),
              reads=[r_mxp], writes=[r_mxp])
            E("dve", lambda v, ul=ul: v.tensor_scalar(
                out=ttile[0:ul, :], in0=ttile[0:ul, :], scalar1=mxp[0:ul, 1:2],
                scalar2=mxp[0:ul, 6:7], op0=ALU.subtract, op1=ALU.mult),
              reads=[r_ttile, r_mxp], writes=[r_ttile])
            E("sp", lambda s_, u0=u0, ul=ul: s_.dma_start(
                out=out_ap[u0:u0 + ul, :], in_=ttile[0:ul, :]),
              reads=[r_ttile], writes=[r_out], dma=True)

        marks["p9"] = p.gidx
        fin = [r_out] + list(r_taps.values())
        E("pool", lambda g: g.engine_nop(), reads=fin)

        semnames = set()
        for e in ENGINES:
            for op in p.ops[e]:
                semnames.add(op.sem)
        sems = {}
        for name in sorted(semnames):
            sems[name] = cm.enter_context(nc.semaphore("sem_" + name))
        block = cm.enter_context(nc.Block())
        p.finalize(block, sems, trunc=trunc)
    return nc, marks


def _toeplitz(rpb, w, m, h):
    idx = (np.arange(m)[:, None] - np.arange(m)[None, :]) + w - 1
    return rpb[idx, h]

def _core_blocks(hh, w, s):
    blocks = []
    if hh == 0:
        blocks.append((0, s, True))
        for o in range(s, 2048, w):
            blocks.append((o, w, True))
        pos = blocks[-1][0] + w
        while pos < EW:
            ln = min(w, EW - pos)
            blocks.append((pos, ln, False))
            pos += ln
    else:
        first = s + ((28 - s) // w) * w
        pos = 0
        while pos < first:
            ln = min(w, first - pos)
            blocks.append((pos, ln, False))
            pos += ln
        o = first
        while o + w <= EW - 4:
            blocks.append((o, w, True))
            o += w
        blocks.append((o, (EW - 4) - o, True))
        blocks.append((EW - 4, 4, False))
    return blocks

def _bias_for_core(hh, rpb1, rpb2):
    out = np.zeros((NSLOT, 128, BIAS_W), np.float32)
    for j, (w, s) in enumerate(BR):
        rpb = rpb1 if j == 0 else rpb2
        blocks = _core_blocks(hh, w, s)
        for g, (g0, gl) in enumerate(GROUPS[j]):
            sl = _slot(j, g)
            for h in range(HEADS):
                th = np.full((gl, gl), -100.0, np.float32)
                for (b0, bl, real) in blocks:
                    if b0 >= g0 + gl or b0 + bl <= g0:
                        continue
                    assert b0 >= g0 and b0 + bl <= g0 + gl, (j, g, b0, bl, g0, gl)
                    lo = b0 - g0
                    if real:
                        th[lo:lo + bl, lo:lo + bl] = _toeplitz(rpb, w, bl, h)
                    else:
                        th[lo:lo + bl, lo:lo + bl] = 0.0
                out[sl, 0:gl, 120 * h:120 * h + gl] = th
    return out

def _vecs_host(i):
    v = np.zeros((NV, 128), np.float32)
    def put(name, arr):
        a = np.asarray(arr, np.float32).reshape(-1, 128)
        v[_vc[name]:_vc[name] + a.shape[0], :] = a
    put("down_b", i["L0_attn_down_b"])
    put("up_b", i["L0_attn_up_b"])
    put("norm1_w", i["L0_attn_norm1_w"])
    put("norm2_w", i["L0_attn_norm2_w"])
    put("ffn_b1", i["L0_attn_ffn_b1"])
    put("ffn_b2", i["L0_attn_ffn_b2"])
    put("proj_b", np.concatenate([np.asarray(i["L0_attn_proj1_b"]),
                                  np.asarray(i["L0_attn_proj2_b"])]))
    qb = np.concatenate([np.asarray(i["L0_attn_qkv1_b"]), np.asarray(i["L0_attn_qkv2_b"])])
    v[_vc["qkv_b"]:_vc["qkv_b"] + 24, 0:64] = qb.reshape(24, 64)
    v[_vc["eps6"], :] = 1e-6
    v[_vc["eps5"], :] = 1e-5
    return v

_NC_CACHE = {}

def _in_maps(i, taps=()):
    x = i["L0_attn_x"]
    projp = np.stack([
        np.ascontiguousarray(np.asarray(i["L0_attn_proj1_w"], np.float32)),
        np.ascontiguousarray(np.asarray(i["L0_attn_proj2_w"], np.float32))])
    shared = dict(
        w_down=np.ascontiguousarray(np.asarray(i["L0_attn_down_w"], np.float32)),
        w_up=np.ascontiguousarray(np.asarray(i["L0_attn_up_w"], np.float32)),
        w_qkv1=np.ascontiguousarray(np.asarray(i["L0_attn_qkv1_w"], np.float32)),
        w_qkv2=np.ascontiguousarray(np.asarray(i["L0_attn_qkv2_w"], np.float32)),
        w_projp=projp,
        w_ffn1=np.ascontiguousarray(np.asarray(i["L0_attn_ffn_w1"], np.float32)),
        w_ffn2=np.ascontiguousarray(np.asarray(i["L0_attn_ffn_w2"], np.float32)),
        w_router=np.ascontiguousarray(np.asarray(i["L0_ffn_router_w"], np.float32)),
        w_moe_up=np.ascontiguousarray(np.asarray(i["L0_ffn_up_w"], np.float32)),
        w_moe_gate=np.ascontiguousarray(np.asarray(i["L0_ffn_gate_w"], np.float32)),
        w_moe_down=np.ascontiguousarray(np.asarray(i["L0_ffn_down_w"], np.float32)),
        w_sh_up=np.ascontiguousarray(np.asarray(i["L0_ffn_shared_up_w"], np.float32)),
        w_sh_gate=np.ascontiguousarray(np.asarray(i["L0_ffn_shared_gate_w"], np.float32)),
        w_sh_down=np.ascontiguousarray(np.asarray(i["L0_ffn_shared_down_w"], np.float32)),
        ident_f=np.eye(128, dtype=np.float32),
        ident_b=np.eye(128, dtype=ml_dtypes.bfloat16),
        ident2=np.ascontiguousarray(np.vstack([np.eye(32), np.eye(32)]).astype(ml_dtypes.bfloat16)),
        ones_r=np.ones((1, 128), np.float32),
        ones_c=np.ones((128, 1), ml_dtypes.bfloat16),
        tri_u=np.ascontiguousarray(np.triu(np.ones((8, 8), np.float32), 1)),
        sel8=np.ascontiguousarray(
            np.stack([np.tile((np.arange(8) == e).astype(np.float32)[:, None], (1, 128))
                      for e in range(N_EXP)], axis=1).reshape(8, N_EXP * 128)),
        vecs=_vecs_host(i),
    )
    bias_h = [_bias_for_core(hh, np.asarray(i["L0_attn_rpb1"], np.float32),
                             np.asarray(i["L0_attn_rpb2"], np.float32)) for hh in (0, 1)]
    maps = []
    for c in range(8):
        b, hh = divmod(c, 2)
        if hh == 0:
            xs = np.asarray(x[b, 0:2080], np.float32)
        else:
            xs = np.concatenate([np.asarray(x[b, 2020:4096], np.float32),
                                 np.zeros((4, D), np.float32)], axis=0)
        m = dict(shared)
        m["xT"] = np.ascontiguousarray(xs.T)
        m["bias6"] = bias_h[hh]
        maps.append(m)
    return maps

def kernel(**inputs):
    i = {k: np.asarray(v) for k, v in inputs.items()}
    if "main" not in _NC_CACHE:
        _NC_CACHE["main"] = build_program()[0]
    nc = _NC_CACHE["main"]
    res = run_bass_kernel_spmd(nc, _in_maps(i), core_ids=list(range(8)))
    out = np.empty((4, 4096, D), np.float32)
    for c in range(8):
        b, hh = divmod(c, 2)
        o = np.asarray(res.results[c]["out"])
        if hh == 0:
            out[b, 0:2048] = o[0:2048]
        else:
            out[b, 2048:4096] = o[28:28 + 2048]
    return out


# version 03
# speedup vs baseline: 1.1817x; 1.1817x over previous
import contextlib
import numpy as np
import ml_dtypes
import concourse.bass as bass
import concourse.mybir as mybir
from concourse.bass_utils import run_bass_kernel_spmd

F32 = mybir.dt.float32
BF16 = mybir.dt.bfloat16
AF = mybir.ActivationFunctionType
ALU = mybir.AluOpType
AX = mybir.AxisListType

D = 1024
DK = 8
DIN = 512
DBR = 256
HEADS = 8
HD = 32
PW = 2080
EW = 2084
QK_SCALE = float(HD) ** -0.5
BR = [(10, 5), (20, 10)]
H_MOE = 4096
HB = 512
N_EXP = 7
FFN_H = 2048

TILES = [(0, 512), (512, 512), (1024, 512), (1536, 512), (2048, 32)]

def _groups(w, s):
    lead = s + ((120 - s) // w) * w
    gs = [(0, lead)]
    pos = lead
    while pos + 120 <= EW - 30:
        gs.append((pos, 120))
        pos += 120
    gs.append((pos, EW - pos))
    return gs

GROUPS = {j: _groups(*BR[j]) for j in (0, 1)}

def _slot(j, g):
    return 3 * j + (0 if g == 0 else (2 if g == len(GROUPS[j]) - 1 else 1))

NSLOT = 6
BIAS_W = 8 * 120

_vc = {}
_vn = 0
def _vcol(name, n):
    global _vn
    _vc[name] = _vn
    _vn += n
for _nm, _n in [("down_b", 4), ("up_b", 8), ("norm1_w", 4), ("norm2_w", 8),
                ("ffn_b1", 16), ("ffn_b2", 8), ("proj_b", 4),
                ("qkv_b", 24), ("eps6", 1), ("eps5", 1)]:
    _vcol(_nm, _n)
NV = _vn

ENGINES = ("te", "act", "dve", "sp", "pool")

class Reg:
    __slots__ = ("w", "r")
    def __init__(self):
        self.w = None
        self.r = []

class _Op:
    __slots__ = ("fn", "inc", "waits", "needed", "sem", "gidx")
    def __init__(self, fn, inc, waits, sem, gidx):
        self.fn = fn; self.inc = inc; self.waits = waits
        self.needed = False; self.sem = sem; self.gidx = gidx

class Prog:
    def __init__(self):
        self.ops = {e: [] for e in ENGINES}
        self.wm = {e: {} for e in ENGINES}
        self.gidx = 0

    def emit(self, eng, fn, reads=(), writes=(), dma=False, boot=False, sem=None):
        deps = {}
        def add(dep):
            deng, didx = dep
            if deng == eng == "te":
                return
            if deng == eng and dma and self.ops[deng][didx].inc == 16:
                return
            key = (deng, self.ops[deng][didx].sem)
            cur = deps.get(key)
            if cur is None or cur[1] < didx:
                deps[key] = dep
        for r in reads:
            if r.w is not None:
                add(r.w)
        for wreg in writes:
            if wreg.w is not None:
                add(wreg.w)
            for rd in wreg.r:
                add(rd)
        waits = []
        wm = self.wm[eng]
        for (deng, didx) in deps.values():
            key = (deng, self.ops[deng][didx].sem)
            if wm.get(key, -1) < didx:
                waits.append((deng, didx))
                wm[key] = didx
        if sem is None:
            sem = "boot" if boot else ("poolc" if (eng == "pool" and not dma) else eng)
        op = _Op(fn, 16 if dma else 1, waits, sem, self.gidx)
        self.gidx += 1
        self.ops[eng].append(op)
        me = (eng, len(self.ops[eng]) - 1)
        for wreg in writes:
            wreg.w = me
            wreg.r = []
        for r in reads:
            r.r.append(me)
        return me

    @staticmethod
    def barrier(newreg, olds):
        for o in olds:
            if o.w is not None:
                newreg.r.append(o.w)
            newreg.r.extend(o.r)

    def finalize(self, block, sems, trunc=None):
        if trunc is not None:
            for e in ENGINES:
                self.ops[e] = [op for op in self.ops[e] if op.gidx < trunc]
        for e in ENGINES:
            for op in self.ops[e]:
                if op.sem == "boot" or op.inc == 16:
                    op.needed = True
                for (deng, didx) in op.waits:
                    self.ops[deng][didx].needed = True
        val = {}
        cum = {}
        for e in ENGINES:
            v = []
            for op in self.ops[e]:
                if op.needed:
                    cum[op.sem] = cum.get(op.sem, 0) + op.inc
                v.append(cum.get(op.sem, 0))
            val[e] = v
        boot_total = cum.get("boot", 0)
        def body(e):
            mysems = {}
            for op in self.ops[e]:
                if op.needed and op.inc == 16:
                    mysems[op.sem] = cum.get(op.sem, 0)
            def run(eng):
                for op in self.ops[e]:
                    for (deng, didx) in op.waits:
                        t = self.ops[deng][didx]
                        if t.sem == "boot":
                            eng.wait_ge(sems["boot"], boot_total)
                        else:
                            eng.wait_ge(sems[t.sem], val[deng][didx])
                    inst = op.fn(eng)
                    if op.needed:
                        inst.then_inc(sems[op.sem], op.inc)
                for s, tot in sorted(mysems.items()):
                    eng.wait_ge(sems[s], tot)
            return run
        if self.ops["te"]:
            block.tensor(body("te"))
        if self.ops["act"]:
            block.scalar(body("act"))
        if self.ops["dve"]:
            block.vector(body("dve"))
        if self.ops["sp"]:
            block.sync(body("sp"))
        if self.ops["pool"]:
            block.gpsimd(body("pool"))


def build_program(taps=(), trunc=None):
    nc = bass.Bass("TRN2", target_bir_lowering=False)
    ap = {}
    def din(name, shape, dt=F32):
        ap[name] = nc.dram_tensor(name, list(shape), dt, kind="ExternalInput").ap()
        return ap[name]
    din("xT", [D, PW])
    din("bias6", [NSLOT, 128, BIAS_W])
    din("w_down", [D, DIN]); din("w_up", [DIN, D])
    din("w_qkv1", [DBR, 3 * DBR]); din("w_qkv2", [DBR, 3 * DBR])
    din("w_projp", [2, HEADS * HD, DBR])
    din("w_ffn1", [D, FFN_H]); din("w_ffn2", [FFN_H, D])
    din("w_router", [D, 8])
    din("w_moe_up", [N_EXP, D, H_MOE]); din("w_moe_gate", [N_EXP, D, H_MOE])
    din("w_moe_down", [N_EXP, H_MOE, D])
    din("w_sh_up", [D, H_MOE]); din("w_sh_gate", [D, H_MOE]); din("w_sh_down", [H_MOE, D])
    din("vecs", [NV, 128])
    din("ident_f", [128, 128]); din("ident_b", [128, 128], BF16)
    din("ones_r", [1, 128]); din("ones_c", [128, 1], BF16)
    din("ident2", [64, 32], BF16)
    din("tri_u", [8, 8])
    din("sel8", [8, N_EXP * 128])
    out_ap = nc.dram_tensor("out", [PW, D], F32, kind="ExternalOutput").ap()
    tap_aps = {t: nc.dram_tensor("tap_" + t, list(s), F32, kind="ExternalOutput").ap()
               for (t, s) in taps}

    p = Prog()
    cm = contextlib.ExitStack()
    with cm:
        sb = lambda name, shape, dt=F32: cm.enter_context(nc.sbuf_tensor(name, shape, dt))
        R2 = sb("R2", [128, DK * PW])
        XBA = sb("XBA", [128, 2 * 4 * EW], BF16)
        XIB_OFF, O12_OFF, HBF_OFF = 0, 4 * EW, 0
        SCF = sb("SCF", [128, 2560])
        wup = [sb(f"wup{i}", [128, 8 * HB], BF16) for i in range(2)]
        wgt = [sb(f"wgt{i}", [128, 8 * HB], BF16) for i in range(2)]
        wdn = [sb(f"wdn{i}", [128, 4 * D], BF16) for i in range(2)]
        xbf = sb("xbf", [128, 4 * 512], BF16)
        hid = sb("hid", [128, 4 * 512], BF16)
        rW = sb("rW", [128, 8 * 8], BF16)
        vec = sb("vec", [128, NV])
        idf = sb("idf", [128, 128]); idb = sb("idb", [128, 128], BF16)
        onr = sb("onr", [1, 128]); onc = sb("onc", [128, 1], BF16)
        id2 = sb("id2_sb", [64, 32], BF16)
        triu = sb("triu", [8, 8])
        sel8 = sb("sel8_sb", [8, N_EXP * 128])
        qp = [sb(f"qp{i}", [64, 128], BF16) for i in range(4)]
        kp = [sb(f"kp{i}", [64, 128], BF16) for i in range(4)]
        vp = [sb(f"vp{i}", [64, 128], BF16) for i in range(4)]
        oh = [sb(f"oh{i}", [32, 128], BF16) for i in range(8)]
        sct = [sb("sct0", [128, 120]), sb("sct1", [128, 120])]
        et = [sb("et0", [128, 120]), sb("et1", [128, 120])]
        atb = [sb("atb0", [128, 120], BF16), sb("atb1", [128, 120], BF16)]
        atT = [sb("atT0", [128, 120], BF16), sb("atT1", [128, 120], BF16)]
        vT = [sb("vT0", [128, 32], BF16), sb("vT1", [128, 32], BF16)]
        nmax = [sb("nmax0", [128, 4]), sb("nmax1", [128, 4])]
        sc0 = sb("sc0", [128, 512]); sc1 = sb("sc1", [128, 512])
        sbf = [sb("sbf0", [128, 512], BF16), sb("sbf1", [128, 512], BF16)]
        rowst = sb("rowst", [1, 1536])
        m2all = sb("m2all", [8, PW])
        lgF = sb("lgF", [8, 512])
        ltm = sb("ltm", [128, 32])
        eqtm = sb("eqtm", [128, 32])
        mxp = sb("mxp", [128, 8])
        eqT = sb("eqT", [8, 512])
        v2r = sb("v2r", [1, 512])
        ttile = sb("ttile", [128, 1024])
        pb = [cm.enter_context(nc.psum_tensor(f"pb{i}", [128, 512], F32)) for i in range(6)]
        pT = [cm.enter_context(nc.psum_tensor(f"pT{i}", [128, 1024], BF16)) for i in range(2)]

        RG = Reg
        r_xT = RG()
        r_R2 = RG(); r_XIB = RG(); r_O12 = RG(); r_HBF = RG()
        r_SCF_xipre = RG(); r_SCF_xf = RG(); r_SCF_m2b = [RG() for _ in range(5)]
        r_SCF_lnsq = RG()
        r_wup = [RG(), RG()]; r_wgt = [RG(), RG()]; r_wdn = [RG(), RG()]
        r_xbf = RG(); r_hid = [RG() for _ in range(4)]; r_rW = RG(); r_vec = RG()
        r_idf = RG(); r_idb = RG(); r_id2 = RG(); r_onr = RG(); r_onc = RG(); r_triu = RG(); r_sel8 = RG()
        r_qp = [RG() for _ in range(4)]; r_kp = [RG() for _ in range(4)]; r_vp = [RG() for _ in range(4)]
        r_oh = [RG() for _ in range(8)]
        r_sct = [RG(), RG()]; r_et = [RG(), RG()]; r_atb = [RG(), RG()]
        r_atT = [RG(), RG()]; r_vT = [RG(), RG()]; r_nmax = [RG(), RG()]
        r_sc0 = RG(); r_sc1 = RG(); r_sbf = [RG(), RG()]
        r_rowst = RG(); r_m2all = RG(); r_lgF = RG(); r_ltm = RG(); r_eqtm = RG()
        r_mxp = RG(); r_eqT = RG(); r_v2r = RG(); r_ttile = RG()
        r_pb = [RG() for _ in range(6)]; r_pT = [RG(), RG()]
        r_bias = RG()
        r_proj = RG()
        r_out = RG()
        r_taps = {t: RG() for t in tap_aps}

        bank_i = [0]
        def nbank():
            b = bank_i[0] % 6
            bank_i[0] += 1
            return pb[b], r_pb[b]

        E = p.emit

        for vi in range(NV):
            E("pool", lambda g, vi=vi: g.dma_start(
                out=vec[:, vi:vi + 1],
                in_=ap["vecs"][vi].rearrange("(p o) -> p o", o=1)),
              writes=[r_vec], dma=True, boot=True)
        E("pool", lambda g: g.dma_start(out=idf[:], in_=ap["ident_f"]), writes=[r_idf], dma=True, boot=True)
        E("pool", lambda g: g.dma_start(out=idb[:], in_=ap["ident_b"]), writes=[r_idb], dma=True, boot=True)
        E("pool", lambda g: g.dma_start(out=onr[:], in_=ap["ones_r"]), writes=[r_onr], dma=True, boot=True)
        E("pool", lambda g: g.dma_start(out=onc[:], in_=ap["ones_c"]), writes=[r_onc], dma=True, boot=True)
        E("pool", lambda g: g.dma_start(out=id2[:], in_=ap["ident2"]), writes=[r_id2], dma=True, boot=True)
        E("pool", lambda g: g.dma_start(out=triu[:], in_=ap["tri_u"]), writes=[r_triu], dma=True, boot=True)
        E("pool", lambda g: g.dma_start(out=sel8[:], in_=ap["sel8"]), writes=[r_sel8], dma=True, boot=True)

        def bias_dst(sl):
            if sl < 4:
                return wup[1][:, 960 * sl:960 * (sl + 1)]
            return wdn[1][:, 960 * (sl - 4):960 * (sl - 3)]
        for sl in range(NSLOT):
            E("pool", lambda g, sl=sl: g.dma_start(out=bias_dst(sl), in_=ap["bias6"][sl]),
              writes=[r_bias], dma=True, boot=True)
        E("pool", lambda g: g.dma_start(out=wup[0][:].rearrange("p (k m) -> p k m", k=8),
                                        in_=ap["w_down"].rearrange("(k p) m -> p k m", p=128)),
          writes=[r_wup[0]], dma=True, boot=True)
        E("pool", lambda g: g.dma_start(out=wdn[0][:].rearrange("p (k m) -> p k m", k=4),
                                        in_=ap["w_up"].rearrange("(k p) m -> p k m", p=128)),
          writes=[r_wdn[0]], dma=True, boot=True)
        for j in range(2):
            E("pool", lambda g, j=j: g.dma_start(
                out=wgt[j][:, 0:1536].rearrange("p (k m) -> p k m", k=2),
                in_=ap["w_qkv" + str(j + 1)].rearrange("(k p) m -> p k m", p=128)),
              writes=[r_wgt[j]], dma=True, boot=True)
            E("pool", lambda g, j=j: g.dma_start(
                out=wgt[j][0:32, 1536:3584].rearrange("p (h m) -> p h m", h=8),
                in_=ap["w_projp"][j].rearrange("(h p) m -> p h m", p=32)),
              writes=[r_proj], dma=True, boot=True)
        E("pool", lambda g: g.dma_start(out=rW[:].rearrange("p (k e) -> p k e", k=8),
                                        in_=ap["w_router"].rearrange("(k p) e -> p k e", p=128)),
          writes=[r_rW], dma=True, boot=True)

        vslice = lambda name, i: vec[:, _vc[name] + i:_vc[name] + i + 1]
        v64 = lambda i: vec[0:64, _vc["qkv_b"] + i:_vc["qkv_b"] + i + 1]

        xT_v = ap["xT"].rearrange("(k p) t -> p k t", p=128)
        marks = {"boot": p.gidx}

        def tap_R2(t):
            if t in tap_aps:
                for mo in range(DK):
                    E("sp", lambda s_, mo=mo, t=t: s_.dma_start(
                        out=tap_aps[t][mo * 128:(mo + 1) * 128, :], in_=R2[:, mo * PW:(mo + 1) * PW]),
                      reads=[r_R2], writes=[r_taps[t]], dma=True)

        def tap_XBA(t, off, rr, nch, width):
            if t in tap_aps:
                for mo in range(nch):
                    for seg in range(0, width, 512):
                        w = min(512, width - seg)
                        E("act", lambda a, mo=mo, off=off, seg=seg, w=w: a.copy(
                            out=sc0[:, 0:w], in_=XBA[:, off + mo * width + seg:off + mo * width + seg + w]),
                          reads=[rr], writes=[r_sc0])
                        E("sp", lambda s_, mo=mo, seg=seg, w=w, t=t: s_.dma_start(
                            out=tap_aps[t][mo * 128:(mo + 1) * 128, seg:seg + w], in_=sc0[:, 0:w]),
                          reads=[r_sc0], writes=[r_taps[t]], dma=True)

        for (t0, tl) in TILES:
            dbanks = [nbank() for _ in range(4)]
            for kh in range(2):
                E("pool", lambda g, kh=kh, t0=t0, tl=tl: g.dma_start(
                    out=xbf[:, 0:4 * tl].rearrange("p (k t) -> p k t", k=4),
                    in_=xT_v[:, 4 * kh:4 * kh + 4, t0:t0 + tl]),
                  reads=[r_xT], writes=[r_xbf], dma=True, sem="sxbf")
                for mo in range(4):
                    bk, rb = dbanks[mo]
                    for k4 in range(4):
                        k = 4 * kh + k4
                        E("te", lambda t, k=k, k4=k4, mo=mo, tl=tl, bk=bk: t.matmul(
                            out=bk[:, 0:tl],
                            lhsT=wup[0][:, (k * 4 + mo) * 128:(k * 4 + mo + 1) * 128],
                            rhs=xbf[:, k4 * tl:(k4 + 1) * tl],
                            start=(k == 0), stop=(k == 7)),
                          reads=[r_wup[0], r_xbf], writes=[rb])
            sqb, sqr = nbank()
            for mo in range(4):
                bk, rb = dbanks[mo]
                E("act", lambda a, mo=mo, tl=tl, bk=bk: a.add(
                    out=SCF[:, mo * 512:mo * 512 + tl], in_=bk[:, 0:tl],
                    add=vslice("down_b", mo)),
                  reads=[rb, r_vec], writes=[r_SCF_xipre])
                E("act", lambda a, mo=mo, tl=tl, bk=bk: a.activation(
                    out=sbf[mo % 2][:, 0:tl], in_=bk[:, 0:tl], func=AF.Square,
                    bias=vslice("down_b", mo), scale=1.0),
                  reads=[rb, r_vec], writes=[r_sbf[mo % 2]])
                E("te", lambda t, mo=mo, tl=tl, sqb=sqb: t.matmul(
                    out=sqb[0:1, 0:tl], lhsT=onc[:], rhs=sbf[mo % 2][:, 0:tl],
                    start=(mo == 0), stop=(mo == 3)),
                  reads=[r_sbf[mo % 2], r_onc], writes=[sqr])
            E("act", lambda a, sqb=sqb, tl=tl: a.activation(
                out=rowst[:, 0:tl], in_=sqb[0:1, 0:tl], func=AF.Sqrt,
                bias=vec[0:1, _vc["eps6"]:_vc["eps6"] + 1], scale=1.0 / 512.0),
              reads=[sqr], writes=[r_rowst])
            E("dve", lambda v, tl=tl: v.reciprocal(out=rowst[:, 512:512 + tl], in_=rowst[:, 0:tl]),
              reads=[r_rowst], writes=[r_rowst])
            bb, rbb = nbank()
            E("te", lambda t, tl=tl, bb=bb: t.matmul(
                out=bb[:, 0:tl], lhsT=onr[:], rhs=rowst[:, 512:512 + tl], start=True, stop=True),
              reads=[r_rowst, r_onr], writes=[rbb])
            for mo in range(4):
                E("dve", lambda v, mo=mo, tl=tl, bb=bb: v.tensor_tensor(
                    out=sc0[:, 0:tl], in0=SCF[:, mo * 512:mo * 512 + tl],
                    in1=bb[:, 0:tl], op=ALU.mult),
                  reads=[r_SCF_xipre, rbb], writes=[r_sc0])
                E("dve", lambda v, mo=mo, t0=t0, tl=tl: v.tensor_scalar(
                    out=XBA[:, XIB_OFF + mo * EW + t0:XIB_OFF + mo * EW + t0 + tl],
                    in0=sc0[:, 0:tl], scalar1=vslice("norm1_w", mo), scalar2=None, op0=ALU.mult),
                  reads=[r_sc0, r_vec], writes=[r_XIB])
        for mo in range(4):
            E("pool", lambda g, mo=mo: g.memset(
                XBA[:, XIB_OFF + mo * EW + 2076:XIB_OFF + mo * EW + 2084], 0.0),
              writes=[r_XIB])

        marks["p1"] = p.gidx
        tap_XBA("xi", XIB_OFF, r_XIB, 4, EW)
        for j in (0, 1):
            for g, (g0, gl) in enumerate(GROUPS[j]):
                sl = _slot(j, g)
                for mc in range(6):
                    bk, rb = nbank()
                    for k in range(2):
                        kc = 2 * j + k
                        E("te", lambda t, mc=mc, k=k, kc=kc, g0=g0, gl=gl, j=j, bk=bk: t.matmul(
                            out=bk[:, 0:gl],
                            lhsT=wgt[j][:, (k * 6 + mc) * 128:(k * 6 + mc + 1) * 128],
                            rhs=XBA[:, XIB_OFF + kc * EW + g0:XIB_OFF + kc * EW + g0 + gl],
                            start=(k == 0), stop=(k == 1)),
                          reads=[r_wgt[j], r_XIB], writes=[rb])
                    kind, sub = divmod(mc, 2)
                    tgt = (qp, kp, vp)[kind]
                    rtg = (r_qp, r_kp, r_vp)[kind]
                    for half in range(2):
                        pi = sub * 2 + half
                        E("dve", lambda v, half=half, pi=pi, gl=gl, j=j, mc=mc, tgt=tgt, bk=bk: v.tensor_scalar(
                            out=tgt[pi][:, 0:gl], in0=bk[64 * half:64 * (half + 1), 0:gl],
                            scalar1=v64(j * 12 + mc * 2 + half), scalar2=None, op0=ALU.add),
                          reads=[rb, r_vec], writes=[rtg[pi]])
                sbanks = []
                def emit_scores(h):
                    pi, su = divmod(h, 2)
                    bk, rb = nbank()
                    E("te", lambda t, pi=pi, su=su, gl=gl, bk=bk: t.matmul(
                        out=bk[0:gl, 0:gl], lhsT=qp[pi][32 * su:32 * (su + 1), 0:gl],
                        rhs=kp[pi][32 * su:32 * (su + 1), 0:gl], start=True, stop=True),
                      reads=[r_qp[pi], r_kp[pi]], writes=[rb])
                    sbanks.append((bk, rb))
                for h in range(4):
                    emit_scores(h)
                for h in range(HEADS):
                    pi, su = divmod(h, 2)
                    pa = h % 2
                    bk, rb = sbanks[h]
                    E("dve", lambda v, gl=gl, sl=sl, h=h, bk=bk, pa=pa: v.scalar_tensor_tensor(
                        out=sct[pa][0:gl, 0:gl], in0=bk[0:gl, 0:gl], scalar=QK_SCALE,
                        in1=bias_dst(sl)[0:gl, 120 * h:120 * h + gl],
                        op0=ALU.mult, op1=ALU.add),
                      reads=[rb, r_bias], writes=[r_sct[pa]])
                    E("dve", lambda v, gl=gl, pa=pa: v.reduce_max(
                        out=nmax[pa][0:gl, 0:1], in_=sct[pa][0:gl, 0:gl], axis=AX.X, negate=True),
                      reads=[r_sct[pa]], writes=[r_nmax[pa]])
                    E("act", lambda a, gl=gl, pa=pa: a.activation(
                        out=et[pa][0:gl, 0:gl], in_=sct[pa][0:gl, 0:gl], func=AF.Exp,
                        bias=nmax[pa][0:gl, 0:1], scale=1.0),
                      reads=[r_sct[pa], r_nmax[pa]], writes=[r_et[pa]])
                    E("dve", lambda v, gl=gl, pa=pa: v.reduce_sum(
                        out=nmax[pa][0:gl, 1:2], in_=et[pa][0:gl, 0:gl], axis=AX.X),
                      reads=[r_et[pa]], writes=[r_nmax[pa]])
                    E("dve", lambda v, gl=gl, pa=pa: v.reciprocal(
                        out=nmax[pa][0:gl, 2:3], in_=nmax[pa][0:gl, 1:2]),
                      reads=[r_nmax[pa]], writes=[r_nmax[pa]])
                    E("dve", lambda v, gl=gl, pa=pa: v.tensor_scalar(
                        out=atb[pa][0:gl, 0:gl], in0=et[pa][0:gl, 0:gl],
                        scalar1=nmax[pa][0:gl, 2:3], scalar2=None, op0=ALU.mult),
                      reads=[r_et[pa], r_nmax[pa]], writes=[r_atb[pa]])
                    E("te", lambda t, gl=gl, pa=pa: t.transpose(
                        out=pT[pa][0:gl, 0:gl], in_=atb[pa][0:gl, 0:gl], identity=idb[0:gl, 0:gl]),
                      reads=[r_atb[pa], r_idb], writes=[r_pT[pa]])
                    E("te", lambda t, pi=pi, su=su, gl=gl, pa=pa: t.transpose(
                        out=pT[pa][0:gl, 512:544], in_=vp[pi][32 * su:32 * (su + 1), 0:gl],
                        identity=id2[32 * su:32 * (su + 1), 0:32]),
                      reads=[r_vp[pi], r_id2], writes=[r_pT[pa]])
                    E("dve", lambda v, gl=gl, pa=pa: v.tensor_copy(
                        out=atT[pa][0:gl, 0:gl], in_=pT[pa][0:gl, 0:gl]),
                      reads=[r_pT[pa]], writes=[r_atT[pa]])
                    E("dve", lambda v, gl=gl, pa=pa: v.tensor_copy(
                        out=vT[pa][0:gl, 0:32], in_=pT[pa][0:gl, 512:544]),
                      reads=[r_pT[pa]], writes=[r_vT[pa]])
                    bk2, rb2 = nbank()
                    E("te", lambda t, gl=gl, bk2=bk2, pa=pa: t.matmul(
                        out=bk2[0:32, 0:gl], lhsT=vT[pa][0:gl, 0:32], rhs=atT[pa][0:gl, 0:gl],
                        start=True, stop=True),
                      reads=[r_vT[pa], r_atT[pa]], writes=[rb2])
                    E("dve", lambda v, h=h, gl=gl, bk2=bk2: v.tensor_copy(
                        out=oh[h][:, 0:gl], in_=bk2[0:32, 0:gl]),
                      reads=[rb2], writes=[r_oh[h]])
                    if h + 4 < HEADS:
                        emit_scores(h + 4)
                for mo in range(2):
                    bk, rb = nbank()
                    for h in range(HEADS):
                        E("te", lambda t, h=h, mo=mo, gl=gl, j=j, bk=bk: t.matmul(
                            out=bk[:, 0:gl],
                            lhsT=wgt[j][0:32, 1536 + h * 256 + mo * 128:1536 + h * 256 + (mo + 1) * 128],
                            rhs=oh[h][:, 0:gl], start=(h == 0), stop=(h == 7)),
                          reads=[r_proj, r_oh[h]], writes=[rb])
                    oc = 2 * j + mo
                    E("dve", lambda v, oc=oc, g0=g0, gl=gl, bk=bk: v.tensor_scalar(
                        out=XBA[:, O12_OFF + oc * EW + g0:O12_OFF + oc * EW + g0 + gl],
                        in0=bk[:, 0:gl], scalar1=vslice("proj_b", oc), scalar2=None, op0=ALU.add),
                      reads=[rb, r_vec], writes=[r_O12])

        marks["p2"] = p.gidx
        tap_XBA("o12", O12_OFF, r_O12, 4, EW)
        Prog.barrier(r_SCF_xf, [r_SCF_xipre])
        for (t0, tl) in TILES:
            for kh in range(2):
                E("sp", lambda s_, kh=kh, t0=t0, tl=tl: s_.dma_start(
                    out=SCF[:, 0:4 * tl].rearrange("p (k t) -> p k t", k=4),
                    in_=xT_v[:, 4 * kh:4 * kh + 4, t0:t0 + tl]),
                  reads=[r_xT], writes=[r_SCF_xf], dma=True)
                for mo4 in range(4):
                    mo = 4 * kh + mo4
                    bk, rb = nbank()
                    for k in range(4):
                        E("te", lambda t, k=k, mo=mo, t0=t0, tl=tl, bk=bk: t.matmul(
                            out=bk[:, 0:tl], lhsT=wdn[0][:, (k * 8 + mo) * 128:(k * 8 + mo + 1) * 128],
                            rhs=XBA[:, O12_OFF + k * EW + t0:O12_OFF + k * EW + t0 + tl],
                            start=(k == 0), stop=(k == 3)),
                          reads=[r_wdn[0], r_O12], writes=[rb])
                    E("dve", lambda v, mo=mo, mo4=mo4, t0=t0, tl=tl, bk=bk: v.scalar_tensor_tensor(
                        out=R2[:, mo * PW + t0:mo * PW + t0 + tl], in0=bk[:, 0:tl],
                        scalar=vslice("up_b", mo), in1=SCF[:, mo4 * tl:(mo4 + 1) * tl],
                        op0=ALU.add, op1=ALU.add),
                      reads=[rb, r_vec, r_SCF_xf], writes=[r_R2])

        marks["p3"] = p.gidx
        tap_R2("x2")
        Prog.barrier(r_HBF, [r_XIB, r_O12])
        for (t0, tl) in TILES:
            s2b, s2r = nbank()
            for mo in range(DK):
                E("act", lambda a, mo=mo, t0=t0, tl=tl: a.activation(
                    out=sbf[mo % 2][:, 0:tl], in_=R2[:, mo * PW + t0:mo * PW + t0 + tl],
                    func=AF.Square),
                  reads=[r_R2], writes=[r_sbf[mo % 2]])
                E("te", lambda t, mo=mo, tl=tl, s2b=s2b: t.matmul(
                    out=s2b[0:1, 0:tl], lhsT=onc[:], rhs=sbf[mo % 2][:, 0:tl],
                    start=(mo == 0), stop=(mo == DK - 1)),
                  reads=[r_sbf[mo % 2], r_onc], writes=[s2r])
            E("act", lambda a, s2b=s2b, tl=tl: a.activation(
                out=rowst[:, 0:tl], in_=s2b[0:1, 0:tl], func=AF.Sqrt,
                bias=vec[0:1, _vc["eps6"]:_vc["eps6"] + 1], scale=1.0 / 1024.0),
              reads=[s2r], writes=[r_rowst])
            E("dve", lambda v, tl=tl: v.reciprocal(out=rowst[:, 512:512 + tl], in_=rowst[:, 0:tl]),
              reads=[r_rowst], writes=[r_rowst])
            bb, rbb = nbank()
            E("te", lambda t, tl=tl, bb=bb: t.matmul(
                out=bb[:, 0:tl], lhsT=onr[:], rhs=rowst[:, 512:512 + tl], start=True, stop=True),
              reads=[r_rowst, r_onr], writes=[rbb])
            for mo in range(DK):
                E("dve", lambda v, mo=mo, t0=t0, tl=tl, bb=bb: v.tensor_tensor(
                    out=sc0[:, 0:tl], in0=R2[:, mo * PW + t0:mo * PW + t0 + tl],
                    in1=bb[:, 0:tl], op=ALU.mult),
                  reads=[r_R2, rbb], writes=[r_sc0])
                E("dve", lambda v, mo=mo, t0=t0, tl=tl: v.tensor_scalar(
                    out=XBA[:, HBF_OFF + mo * PW + t0:HBF_OFF + mo * PW + t0 + tl],
                    in0=sc0[:, 0:tl], scalar1=vslice("norm2_w", mo), scalar2=None, op0=ALU.mult),
                  reads=[r_sc0, r_vec], writes=[r_HBF])

        marks["p4"] = p.gidx
        Prog.barrier(r_wup[1], [r_bias])
        Prog.barrier(r_wdn[1], [r_bias])
        for blk in range(4):
            wb = blk % 2
            E("pool", lambda g, blk=blk, wb=wb: g.dma_start(
                out=wup[wb][:].rearrange("p (k h) -> p k h", k=8),
                in_=ap["w_ffn1"][:, blk * HB:(blk + 1) * HB].rearrange("(k p) h -> p k h", p=128)),
              writes=[r_wup[wb]], dma=True, sem="swup%d" % wb)
            E("pool", lambda g, blk=blk, wb=wb: g.dma_start(
                out=wdn[wb][:].rearrange("p (c d) -> p c d", c=4),
                in_=ap["w_ffn2"][blk * HB:(blk + 1) * HB, :].rearrange("(c p) d -> p c d", p=128)),
              writes=[r_wdn[wb]], dma=True, sem="swdn%d" % wb)
            for (t0, tl) in TILES:
                for m in range(4):
                    bk, rb = nbank()
                    for k in range(DK):
                        E("te", lambda t, k=k, m=m, t0=t0, tl=tl, wb=wb, bk=bk: t.matmul(
                            out=bk[:, 0:tl], lhsT=wup[wb][:, k * HB + m * 128:k * HB + (m + 1) * 128],
                            rhs=XBA[:, HBF_OFF + k * PW + t0:HBF_OFF + k * PW + t0 + tl],
                            start=(k == 0), stop=(k == DK - 1)),
                          reads=[r_wup[wb], r_HBF], writes=[rb])
                    bi = blk * 4 + m
                    E("act", lambda a, bi=bi, m=m, tl=tl, bk=bk: a.activation(
                        out=hid[:, m * 512:m * 512 + tl], in_=bk[:, 0:tl], func=AF.Gelu,
                        bias=vslice("ffn_b1", bi), scale=1.0),
                      reads=[rb, r_vec], writes=[r_hid[m]])
                for mo in range(DK):
                    bk, rb = nbank()
                    for c in range(4):
                        E("te", lambda t, c=c, mo=mo, tl=tl, wb=wb, bk=bk: t.matmul(
                            out=bk[:, 0:tl], lhsT=wdn[wb][:, c * D + mo * 128:c * D + (mo + 1) * 128],
                            rhs=hid[:, c * 512:c * 512 + tl], start=(c == 0), stop=(c == 3)),
                          reads=[r_wdn[wb], r_hid[c]], writes=[rb])
                    if blk == 0:
                        E("dve", lambda v, mo=mo, t0=t0, tl=tl, bk=bk: v.scalar_tensor_tensor(
                            out=R2[:, mo * PW + t0:mo * PW + t0 + tl], in0=bk[:, 0:tl],
                            scalar=vslice("ffn_b2", mo), in1=R2[:, mo * PW + t0:mo * PW + t0 + tl],
                            op0=ALU.add, op1=ALU.add),
                          reads=[rb, r_vec, r_R2], writes=[r_R2])
                    else:
                        E("dve", lambda v, mo=mo, t0=t0, tl=tl, bk=bk: v.tensor_tensor(
                            out=R2[:, mo * PW + t0:mo * PW + t0 + tl], in0=bk[:, 0:tl],
                            in1=R2[:, mo * PW + t0:mo * PW + t0 + tl], op=ALU.add),
                          reads=[rb, r_R2], writes=[r_R2])

        for (t0, tl) in TILES:
            for kh in range(2):
                E("sp", lambda s_, kh=kh, t0=t0, tl=tl: s_.dma_start(
                    out=SCF[:, 0:4 * tl].rearrange("p (k t) -> p k t", k=4),
                    in_=xT_v[:, 4 * kh:4 * kh + 4, t0:t0 + tl]),
                  reads=[r_xT], writes=[r_SCF_xf], dma=True)
                for mo4 in range(4):
                    mo = 4 * kh + mo4
                    E("dve", lambda v, mo=mo, mo4=mo4, t0=t0, tl=tl: v.tensor_tensor(
                        out=R2[:, mo * PW + t0:mo * PW + t0 + tl],
                        in0=R2[:, mo * PW + t0:mo * PW + t0 + tl],
                        in1=SCF[:, mo4 * tl:(mo4 + 1) * tl], op=ALU.add),
                      reads=[r_R2, r_SCF_xf], writes=[r_R2])

        marks["p5"] = p.gidx
        tap_R2("s1")
        for (t0, tl) in TILES:
            s1b, s1r = nbank()
            s2b, s2r = nbank()
            for mo in range(DK):
                E("act", lambda a, mo=mo, t0=t0, tl=tl: a.copy(
                    out=sbf[0][:, 0:tl], in_=R2[:, mo * PW + t0:mo * PW + t0 + tl]),
                  reads=[r_R2], writes=[r_sbf[0]])
                E("te", lambda t, mo=mo, tl=tl, s1b=s1b: t.matmul(
                    out=s1b[0:1, 0:tl], lhsT=onc[:], rhs=sbf[0][:, 0:tl],
                    start=(mo == 0), stop=(mo == DK - 1)),
                  reads=[r_sbf[0], r_onc], writes=[s1r])
                E("act", lambda a, mo=mo, t0=t0, tl=tl: a.activation(
                    out=sbf[1][:, 0:tl], in_=R2[:, mo * PW + t0:mo * PW + t0 + tl], func=AF.Square),
                  reads=[r_R2], writes=[r_sbf[1]])
                E("te", lambda t, mo=mo, tl=tl, s2b=s2b: t.matmul(
                    out=s2b[0:1, 0:tl], lhsT=onc[:], rhs=sbf[1][:, 0:tl],
                    start=(mo == 0), stop=(mo == DK - 1)),
                  reads=[r_sbf[1], r_onc], writes=[s2r])
            E("dve", lambda v, s1b=s1b, tl=tl: v.tensor_scalar(
                out=rowst[:, 0:tl], in0=s1b[0:1, 0:tl], scalar1=1.0 / 1024.0, scalar2=None,
                op0=ALU.mult),
              reads=[s1r], writes=[r_rowst])
            E("act", lambda a, tl=tl: a.activation(
                out=sc0[0:1, 0:tl], in_=rowst[:, 0:tl], func=AF.Square),
              reads=[r_rowst], writes=[r_sc0])
            E("dve", lambda v, s2b=s2b, tl=tl: v.scalar_tensor_tensor(
                out=rowst[:, 512:512 + tl], in0=s2b[0:1, 0:tl], scalar=1.0 / 1024.0,
                in1=sc0[0:1, 0:tl], op0=ALU.mult, op1=ALU.subtract),
              reads=[s2r, r_sc0], writes=[r_rowst])
            E("act", lambda a, tl=tl: a.activation(
                out=sc0[0:1, 0:tl], in_=rowst[:, 512:512 + tl], func=AF.Sqrt,
                bias=vec[0:1, _vc["eps5"]:_vc["eps5"] + 1], scale=1.0),
              reads=[r_rowst], writes=[r_sc0])
            E("dve", lambda v, tl=tl: v.reciprocal(
                out=rowst[:, 512:512 + tl], in_=sc0[0:1, 0:tl]),
              reads=[r_sc0], writes=[r_rowst])
            E("dve", lambda v, tl=tl: v.tensor_tensor(
                out=rowst[:, 1024:1024 + tl], in0=rowst[:, 0:tl], in1=rowst[:, 512:512 + tl],
                op=ALU.mult),
              reads=[r_rowst], writes=[r_rowst])
            rbb_b, rbb_r = nbank()
            mrb_b, mrb_r = nbank()
            E("te", lambda t, tl=tl, rbb_b=rbb_b: t.matmul(
                out=rbb_b[:, 0:tl], lhsT=onr[:], rhs=rowst[:, 512:512 + tl], start=True, stop=True),
              reads=[r_rowst, r_onr], writes=[rbb_r])
            E("te", lambda t, tl=tl, mrb_b=mrb_b: t.matmul(
                out=mrb_b[:, 0:tl], lhsT=onr[:], rhs=rowst[:, 1024:1024 + tl], start=True, stop=True),
              reads=[r_rowst, r_onr], writes=[mrb_r])
            for mo in range(DK):
                E("dve", lambda v, mo=mo, t0=t0, tl=tl, rbb_b=rbb_b: v.tensor_tensor(
                    out=sc0[:, 0:tl], in0=R2[:, mo * PW + t0:mo * PW + t0 + tl],
                    in1=rbb_b[:, 0:tl], op=ALU.mult),
                  reads=[r_R2, rbb_r], writes=[r_sc0])
                E("dve", lambda v, mo=mo, t0=t0, tl=tl, mrb_b=mrb_b: v.tensor_tensor(
                    out=R2[:, mo * PW + t0:mo * PW + t0 + tl], in0=sc0[:, 0:tl],
                    in1=mrb_b[:, 0:tl], op=ALU.subtract),
                  reads=[r_sc0, mrb_r], writes=[r_R2])
                E("act", lambda a, mo=mo, t0=t0, tl=tl: a.copy(
                    out=XBA[:, HBF_OFF + mo * PW + t0:HBF_OFF + mo * PW + t0 + tl],
                    in_=R2[:, mo * PW + t0:mo * PW + t0 + tl]),
                  reads=[r_R2], writes=[r_HBF])

        marks["p6"] = p.gidx
        tap_R2("y1")
        for (t0, tl) in TILES:
            lgb, lgr = nbank()
            for k in range(DK):
                E("te", lambda t, k=k, t0=t0, tl=tl, lgb=lgb: t.matmul(
                    out=lgb[0:8, 0:tl], lhsT=rW[:, k * 8:(k + 1) * 8],
                    rhs=XBA[:, HBF_OFF + k * PW + t0:HBF_OFF + k * PW + t0 + tl],
                    start=(k == 0), stop=(k == DK - 1)),
                  reads=[r_rW, r_HBF], writes=[lgr])
            E("dve", lambda v, tl=tl, lgb=lgb: v.tensor_copy(out=lgF[:, 0:tl], in_=lgb[0:8, 0:tl]),
              reads=[lgr], writes=[r_lgF])
            nsub = (tl + 127) // 128
            tpb, tpr = nbank()
            for ji in range(nsub):
                jl = min(128, tl - ji * 128)
                E("te", lambda t, ji=ji, jl=jl, tpb=tpb: t.transpose(
                    out=tpb[0:jl, 8 * ji:8 * ji + 8], in_=lgF[0:8, ji * 128:ji * 128 + jl],
                    identity=idf[0:8, 0:8]),
                  reads=[r_lgF, r_idf], writes=[tpr])
            E("dve", lambda v, nsub=nsub, tl=tl, tpb=tpb: v.tensor_copy(
                out=ltm[0:min(tl, 128), 0:8 * nsub] if tl < 128 else ltm[:, 0:8 * nsub],
                in_=tpb[0:min(tl, 128), 0:8 * nsub] if tl < 128 else tpb[0:128, 0:8 * nsub]),
              reads=[tpr], writes=[r_ltm])
            for ji in range(nsub):
                jl = min(128, tl - ji * 128)
                E("dve", lambda v, ji=ji, jl=jl: v.reduce_max(
                    out=mxp[0:jl, 0:1], in_=ltm[0:jl, 8 * ji:8 * ji + 8], axis=AX.X, negate=True),
                  reads=[r_ltm], writes=[r_mxp])
                E("act", lambda a, ji=ji, jl=jl: a.activation(
                    out=eqtm[0:jl, 8 * ji:8 * ji + 8], in_=ltm[0:jl, 8 * ji:8 * ji + 8],
                    func=AF.Exp, bias=mxp[0:jl, 0:1], scale=1.0),
                  reads=[r_ltm, r_mxp], writes=[r_eqtm])
                E("dve", lambda v, ji=ji, jl=jl: v.reduce_sum(
                    out=mxp[0:jl, 1:2], in_=eqtm[0:jl, 8 * ji:8 * ji + 8], axis=AX.X),
                  reads=[r_eqtm], writes=[r_mxp])
                E("dve", lambda v, jl=jl: v.reciprocal(out=mxp[0:jl, 2:3], in_=mxp[0:jl, 1:2]),
                  reads=[r_mxp], writes=[r_mxp])
                E("dve", lambda v, ji=ji, jl=jl: v.tensor_scalar(
                    out=eqtm[0:jl, 8 * ji:8 * ji + 8], in0=eqtm[0:jl, 8 * ji:8 * ji + 8],
                    scalar1=mxp[0:jl, 2:3], scalar2=None, op0=ALU.mult),
                  reads=[r_eqtm, r_mxp], writes=[r_eqtm])
                E("dve", lambda v, ji=ji, jl=jl: v.reduce_max(
                    out=mxp[0:jl, 3:4], in_=eqtm[0:jl, 8 * ji:8 * ji + 8], axis=AX.X),
                  reads=[r_eqtm], writes=[r_mxp])
                E("dve", lambda v, ji=ji, jl=jl: v.tensor_scalar(
                    out=ltm[0:jl, 8 * ji:8 * ji + 8], in0=eqtm[0:jl, 8 * ji:8 * ji + 8],
                    scalar1=mxp[0:jl, 3:4], scalar2=None, op0=ALU.is_equal),
                  reads=[r_eqtm, r_mxp], writes=[r_ltm])
                E("dve", lambda v, jl=jl: v.tensor_scalar(
                    out=mxp[0:jl, 4:5], in0=mxp[0:jl, 3:4], scalar1=2.0, scalar2=None,
                    op0=ALU.mult),
                  reads=[r_mxp], writes=[r_mxp])
                tb2, tr2 = nbank()
                E("te", lambda t, ji=ji, jl=jl, tb2=tb2: t.transpose(
                    out=tb2[0:8, 0:jl], in_=ltm[0:jl, 8 * ji:8 * ji + 8], identity=idf[0:jl, 0:jl]),
                  reads=[r_ltm, r_idf], writes=[tr2])
                E("dve", lambda v, ji=ji, jl=jl, tb2=tb2: v.tensor_copy(
                    out=eqT[:, ji * 128:ji * 128 + jl], in_=tb2[0:8, 0:jl]),
                  reads=[tr2], writes=[r_eqT])
                tb3, tr3 = nbank()
                E("te", lambda t, ji=ji, jl=jl, tb3=tb3: t.transpose(
                    out=tb3[0:1, 0:jl], in_=mxp[0:jl, 4:5], identity=idf[0:jl, 0:jl]),
                  reads=[r_mxp, r_idf], writes=[tr3])
                E("dve", lambda v, ji=ji, jl=jl, tb3=tb3: v.tensor_copy(
                    out=v2r[:, ji * 128:ji * 128 + jl], in_=tb3[0:1, 0:jl]),
                  reads=[tr3], writes=[r_v2r])
            pfb, pfr = nbank()
            E("te", lambda t, tl=tl, pfb=pfb: t.matmul(
                out=pfb[0:8, 0:tl], lhsT=triu[:], rhs=eqT[:, 0:tl], start=True, stop=True),
              reads=[r_triu, r_eqT], writes=[pfr])
            E("dve", lambda v, tl=tl, pfb=pfb: v.tensor_scalar(
                out=lgF[:, 0:tl], in0=pfb[0:8, 0:tl], scalar1=0.0, scalar2=None, op0=ALU.is_equal),
              reads=[pfr], writes=[r_lgF])
            E("dve", lambda v, tl=tl: v.tensor_tensor(
                out=eqT[:, 0:tl], in0=eqT[:, 0:tl], in1=lgF[:, 0:tl], op=ALU.mult),
              reads=[r_eqT, r_lgF], writes=[r_eqT])
            vbb, vbr = nbank()
            E("te", lambda t, tl=tl, vbb=vbb: t.matmul(
                out=vbb[0:8, 0:tl], lhsT=onr[0:1, 0:8], rhs=v2r[:, 0:tl], start=True, stop=True),
              reads=[r_onr, r_v2r], writes=[vbr])
            E("dve", lambda v, t0=t0, tl=tl, vbb=vbb: v.tensor_tensor(
                out=m2all[:, t0:t0 + tl], in0=eqT[:, 0:tl], in1=vbb[0:8, 0:tl], op=ALU.mult),
              reads=[r_eqT, vbr], writes=[r_m2all])

        marks["p7"] = p.gidx
        if "m2" in tap_aps:
            E("sp", lambda s_: s_.dma_start(out=tap_aps["m2"][:, :], in_=m2all[:]),
              reads=[r_m2all], writes=[r_taps["m2"]], dma=True)
        for ti in range(5):
            Prog.barrier(r_SCF_m2b[ti], [r_SCF_xf])
        Prog.barrier(r_wgt[0], [r_proj])
        Prog.barrier(r_wgt[1], [r_proj])
        for e in range(8):
            routed = e < 7
            wb = e % 2
            if routed:
                for ti, (t0, tl) in enumerate(TILES):
                    bk, rb = nbank()
                    E("te", lambda t, e=e, t0=t0, tl=tl, bk=bk: t.matmul(
                        out=bk[:, 0:tl], lhsT=sel8[:, e * 128:(e + 1) * 128],
                        rhs=m2all[:, t0:t0 + tl], start=True, stop=True),
                      reads=[r_sel8, r_m2all], writes=[rb])
                    E("act", lambda a, ti=ti, tl=tl, bk=bk: a.copy(
                        out=SCF[:, ti * 512:ti * 512 + tl], in_=bk[:, 0:tl]),
                      reads=[rb], writes=[r_SCF_m2b[ti]])
            for hb in range(8):
                if routed:
                    upv = ap["w_moe_up"][e, :, hb * HB:(hb + 1) * HB]
                    gtv = ap["w_moe_gate"][e, :, hb * HB:(hb + 1) * HB]
                    dnv = ap["w_moe_down"][e, hb * HB:(hb + 1) * HB, :]
                else:
                    upv = ap["w_sh_up"][:, hb * HB:(hb + 1) * HB]
                    gtv = ap["w_sh_gate"][:, hb * HB:(hb + 1) * HB]
                    dnv = ap["w_sh_down"][hb * HB:(hb + 1) * HB, :]
                E("pool", lambda g, upv=upv, wb=wb: g.dma_start(
                    out=wup[wb][:].rearrange("p (k h) -> p k h", k=8),
                    in_=upv.rearrange("(k p) h -> p k h", p=128)),
                  writes=[r_wup[wb]], dma=True, sem="swup%d" % wb)
                E("pool", lambda g, gtv=gtv, wb=wb: g.dma_start(
                    out=wgt[wb][:].rearrange("p (k h) -> p k h", k=8),
                    in_=gtv.rearrange("(k p) h -> p k h", p=128)),
                  writes=[r_wgt[wb]], dma=True, sem="swgt%d" % wb)
                E("pool", lambda g, dnv=dnv, wb=wb: g.dma_start(
                    out=wdn[wb][:].rearrange("p (c d) -> p c d", c=4),
                    in_=dnv.rearrange("(c p) d -> p c d", p=128)),
                  writes=[r_wdn[wb]], dma=True, sem="swdn%d" % wb)
                for ti, (t0, tl) in enumerate(TILES):
                    for m in range(4):
                        bU, rU = nbank()
                        bG, rG = nbank()
                        for k in range(DK):
                            E("te", lambda t, k=k, m=m, t0=t0, tl=tl, wb=wb, bU=bU: t.matmul(
                                out=bU[:, 0:tl], lhsT=wup[wb][:, k * HB + m * 128:k * HB + (m + 1) * 128],
                                rhs=XBA[:, HBF_OFF + k * PW + t0:HBF_OFF + k * PW + t0 + tl],
                                start=(k == 0), stop=(k == DK - 1)),
                              reads=[r_wup[wb], r_HBF], writes=[rU])
                        for k in range(DK):
                            E("te", lambda t, k=k, m=m, t0=t0, tl=tl, wb=wb, bG=bG: t.matmul(
                                out=bG[:, 0:tl], lhsT=wgt[wb][:, k * HB + m * 128:k * HB + (m + 1) * 128],
                                rhs=XBA[:, HBF_OFF + k * PW + t0:HBF_OFF + k * PW + t0 + tl],
                                start=(k == 0), stop=(k == DK - 1)),
                              reads=[r_wgt[wb], r_HBF], writes=[rG])
                        E("act", lambda a, tl=tl, bG=bG: a.activation(
                            out=sc1[:, 0:tl], in_=bG[:, 0:tl], func=AF.Sigmoid),
                          reads=[rG], writes=[r_sc1])
                        E("dve", lambda v, tl=tl, bG=bG: v.tensor_tensor(
                            out=sc0[:, 0:tl], in0=bG[:, 0:tl], in1=sc1[:, 0:tl], op=ALU.mult),
                          reads=[rG, r_sc1], writes=[r_sc0])
                        if routed:
                            E("dve", lambda v, tl=tl, bU=bU: v.tensor_tensor(
                                out=sc0[:, 0:tl], in0=bU[:, 0:tl], in1=sc0[:, 0:tl], op=ALU.mult),
                              reads=[rU, r_sc0], writes=[r_sc0])
                            E("dve", lambda v, m=m, ti=ti, tl=tl: v.tensor_tensor(
                                out=hid[:, m * 512:m * 512 + tl], in0=sc0[:, 0:tl],
                                in1=SCF[:, ti * 512:ti * 512 + tl], op=ALU.mult),
                              reads=[r_sc0, r_SCF_m2b[ti]], writes=[r_hid[m]])
                        else:
                            E("dve", lambda v, m=m, tl=tl, bU=bU: v.tensor_tensor(
                                out=hid[:, m * 512:m * 512 + tl], in0=bU[:, 0:tl],
                                in1=sc0[:, 0:tl], op=ALU.mult),
                              reads=[rU, r_sc0], writes=[r_hid[m]])
                    for mo in range(DK):
                        bk, rb = nbank()
                        for c in range(4):
                            E("te", lambda t, c=c, mo=mo, tl=tl, wb=wb, bk=bk: t.matmul(
                                out=bk[:, 0:tl], lhsT=wdn[wb][:, c * D + mo * 128:c * D + (mo + 1) * 128],
                                rhs=hid[:, c * 512:c * 512 + tl], start=(c == 0), stop=(c == 3)),
                              reads=[r_wdn[wb], r_hid[c]], writes=[rb])
                        E("dve", lambda v, mo=mo, t0=t0, tl=tl, bk=bk: v.tensor_tensor(
                            out=R2[:, mo * PW + t0:mo * PW + t0 + tl], in0=bk[:, 0:tl],
                            in1=R2[:, mo * PW + t0:mo * PW + t0 + tl], op=ALU.add),
                          reads=[rb, r_R2], writes=[r_R2])

        marks["p8"] = p.gidx
        tap_R2("racc")
        Prog.barrier(r_SCF_lnsq, r_SCF_m2b)
        subs = [(i * 128, 128) for i in range(16)] + [(2048, 32)]
        for (u0, ul) in subs:
            tb0, tr0 = nbank()
            tb1, tr1 = nbank()
            for mo in range(DK):
                tb, tr = (tb0, tr0) if mo < 4 else (tb1, tr1)
                E("te", lambda t, mo=mo, u0=u0, ul=ul, tb=tb: t.transpose(
                    out=tb[0:ul, (mo % 4) * 128:(mo % 4) * 128 + 128],
                    in_=R2[:, mo * PW + u0:mo * PW + u0 + ul], identity=idf[:]),
                  reads=[r_R2, r_idf], writes=[tr])
            E("dve", lambda v, ul=ul, tb0=tb0: v.tensor_copy(
                out=ttile[0:ul, 0:512], in_=tb0[0:ul, 0:512]),
              reads=[tr0], writes=[r_ttile])
            E("dve", lambda v, ul=ul, tb1=tb1: v.tensor_copy(
                out=ttile[0:ul, 512:1024], in_=tb1[0:ul, 0:512]),
              reads=[tr1], writes=[r_ttile])
            E("dve", lambda v, ul=ul: v.reduce_sum(
                out=mxp[0:ul, 0:1], in_=ttile[0:ul, :], axis=AX.X),
              reads=[r_ttile], writes=[r_mxp])
            E("dve", lambda v, ul=ul: v.tensor_scalar(
                out=mxp[0:ul, 1:2], in0=mxp[0:ul, 0:1], scalar1=1.0 / 1024.0, scalar2=None,
                op0=ALU.mult),
              reads=[r_mxp], writes=[r_mxp])
            E("act", lambda a, ul=ul: a.activation(
                out=SCF[0:ul, 0:1024], in_=ttile[0:ul, :], func=AF.Square),
              reads=[r_ttile], writes=[r_SCF_lnsq])
            E("dve", lambda v, ul=ul: v.reduce_sum(
                out=mxp[0:ul, 2:3], in_=SCF[0:ul, 0:1024], axis=AX.X),
              reads=[r_SCF_lnsq], writes=[r_mxp])
            E("act", lambda a, ul=ul: a.activation(
                out=mxp[0:ul, 3:4], in_=mxp[0:ul, 1:2], func=AF.Square),
              reads=[r_mxp], writes=[r_mxp])
            E("dve", lambda v, ul=ul: v.scalar_tensor_tensor(
                out=mxp[0:ul, 4:5], in0=mxp[0:ul, 2:3], scalar=1.0 / 1024.0,
                in1=mxp[0:ul, 3:4], op0=ALU.mult, op1=ALU.subtract),
              reads=[r_mxp], writes=[r_mxp])
            E("act", lambda a, ul=ul: a.activation(
                out=mxp[0:ul, 5:6], in_=mxp[0:ul, 4:5], func=AF.Sqrt,
                bias=vec[0:ul, _vc["eps5"]:_vc["eps5"] + 1], scale=1.0),
              reads=[r_mxp], writes=[r_mxp])
            E("dve", lambda v, ul=ul: v.reciprocal(out=mxp[0:ul, 6:7], in_=mxp[0:ul, 5:6]),
              reads=[r_mxp], writes=[r_mxp])
            E("dve", lambda v, ul=ul: v.tensor_scalar(
                out=ttile[0:ul, :], in0=ttile[0:ul, :], scalar1=mxp[0:ul, 1:2],
                scalar2=mxp[0:ul, 6:7], op0=ALU.subtract, op1=ALU.mult),
              reads=[r_ttile, r_mxp], writes=[r_ttile])
            E("sp", lambda s_, u0=u0, ul=ul: s_.dma_start(
                out=out_ap[u0:u0 + ul, :], in_=ttile[0:ul, :]),
              reads=[r_ttile], writes=[r_out], dma=True)

        marks["p9"] = p.gidx
        fin = [r_out] + list(r_taps.values())
        E("pool", lambda g: g.engine_nop(), reads=fin)

        semnames = set()
        for e in ENGINES:
            for op in p.ops[e]:
                semnames.add(op.sem)
        sems = {}
        for name in sorted(semnames):
            sems[name] = cm.enter_context(nc.semaphore("sem_" + name))
        block = cm.enter_context(nc.Block())
        p.finalize(block, sems, trunc=trunc)
    return nc, marks


def _toeplitz(rpb, w, m, h):
    idx = (np.arange(m)[:, None] - np.arange(m)[None, :]) + w - 1
    return rpb[idx, h]

def _core_blocks(hh, w, s):
    blocks = []
    if hh == 0:
        blocks.append((0, s, True))
        for o in range(s, 2048, w):
            blocks.append((o, w, True))
        pos = blocks[-1][0] + w
        while pos < EW:
            ln = min(w, EW - pos)
            blocks.append((pos, ln, False))
            pos += ln
    else:
        first = s + ((28 - s) // w) * w
        pos = 0
        while pos < first:
            ln = min(w, first - pos)
            blocks.append((pos, ln, False))
            pos += ln
        o = first
        while o + w <= EW - 4:
            blocks.append((o, w, True))
            o += w
        blocks.append((o, (EW - 4) - o, True))
        blocks.append((EW - 4, 4, False))
    return blocks

def _bias_for_core(hh, rpb1, rpb2):
    out = np.zeros((NSLOT, 128, BIAS_W), np.float32)
    for j, (w, s) in enumerate(BR):
        rpb = rpb1 if j == 0 else rpb2
        blocks = _core_blocks(hh, w, s)
        for g, (g0, gl) in enumerate(GROUPS[j]):
            sl = _slot(j, g)
            for h in range(HEADS):
                th = np.full((gl, gl), -100.0, np.float32)
                for (b0, bl, real) in blocks:
                    if b0 >= g0 + gl or b0 + bl <= g0:
                        continue
                    assert b0 >= g0 and b0 + bl <= g0 + gl, (j, g, b0, bl, g0, gl)
                    lo = b0 - g0
                    if real:
                        th[lo:lo + bl, lo:lo + bl] = _toeplitz(rpb, w, bl, h)
                    else:
                        th[lo:lo + bl, lo:lo + bl] = 0.0
                out[sl, 0:gl, 120 * h:120 * h + gl] = th
    return out

def _vecs_host(i):
    v = np.zeros((NV, 128), np.float32)
    def put(name, arr):
        a = np.asarray(arr, np.float32).reshape(-1, 128)
        v[_vc[name]:_vc[name] + a.shape[0], :] = a
    put("down_b", i["L0_attn_down_b"])
    put("up_b", i["L0_attn_up_b"])
    put("norm1_w", i["L0_attn_norm1_w"])
    put("norm2_w", i["L0_attn_norm2_w"])
    put("ffn_b1", i["L0_attn_ffn_b1"])
    put("ffn_b2", i["L0_attn_ffn_b2"])
    put("proj_b", np.concatenate([np.asarray(i["L0_attn_proj1_b"]),
                                  np.asarray(i["L0_attn_proj2_b"])]))
    qb = np.concatenate([np.asarray(i["L0_attn_qkv1_b"]), np.asarray(i["L0_attn_qkv2_b"])])
    v[_vc["qkv_b"]:_vc["qkv_b"] + 24, 0:64] = qb.reshape(24, 64)
    v[_vc["eps6"], :] = 1e-6
    v[_vc["eps5"], :] = 1e-5
    return v

_NC_CACHE = {}

def _in_maps(i, taps=()):
    x = i["L0_attn_x"]
    projp = np.stack([
        np.ascontiguousarray(np.asarray(i["L0_attn_proj1_w"], np.float32)),
        np.ascontiguousarray(np.asarray(i["L0_attn_proj2_w"], np.float32))])
    shared = dict(
        w_down=np.ascontiguousarray(np.asarray(i["L0_attn_down_w"], np.float32)),
        w_up=np.ascontiguousarray(np.asarray(i["L0_attn_up_w"], np.float32)),
        w_qkv1=np.ascontiguousarray(np.asarray(i["L0_attn_qkv1_w"], np.float32)),
        w_qkv2=np.ascontiguousarray(np.asarray(i["L0_attn_qkv2_w"], np.float32)),
        w_projp=projp,
        w_ffn1=np.ascontiguousarray(np.asarray(i["L0_attn_ffn_w1"], np.float32)),
        w_ffn2=np.ascontiguousarray(np.asarray(i["L0_attn_ffn_w2"], np.float32)),
        w_router=np.ascontiguousarray(np.asarray(i["L0_ffn_router_w"], np.float32)),
        w_moe_up=np.ascontiguousarray(np.asarray(i["L0_ffn_up_w"], np.float32)),
        w_moe_gate=np.ascontiguousarray(np.asarray(i["L0_ffn_gate_w"], np.float32)),
        w_moe_down=np.ascontiguousarray(np.asarray(i["L0_ffn_down_w"], np.float32)),
        w_sh_up=np.ascontiguousarray(np.asarray(i["L0_ffn_shared_up_w"], np.float32)),
        w_sh_gate=np.ascontiguousarray(np.asarray(i["L0_ffn_shared_gate_w"], np.float32)),
        w_sh_down=np.ascontiguousarray(np.asarray(i["L0_ffn_shared_down_w"], np.float32)),
        ident_f=np.eye(128, dtype=np.float32),
        ident_b=np.eye(128, dtype=ml_dtypes.bfloat16),
        ident2=np.ascontiguousarray(np.vstack([np.eye(32), np.eye(32)]).astype(ml_dtypes.bfloat16)),
        ones_r=np.ones((1, 128), np.float32),
        ones_c=np.ones((128, 1), ml_dtypes.bfloat16),
        tri_u=np.ascontiguousarray(np.triu(np.ones((8, 8), np.float32), 1)),
        sel8=np.ascontiguousarray(
            np.stack([np.tile((np.arange(8) == e).astype(np.float32)[:, None], (1, 128))
                      for e in range(N_EXP)], axis=1).reshape(8, N_EXP * 128)),
        vecs=_vecs_host(i),
    )
    bias_h = [_bias_for_core(hh, np.asarray(i["L0_attn_rpb1"], np.float32),
                             np.asarray(i["L0_attn_rpb2"], np.float32)) for hh in (0, 1)]
    maps = []
    for c in range(8):
        b, hh = divmod(c, 2)
        if hh == 0:
            xs = np.asarray(x[b, 0:2080], np.float32)
        else:
            xs = np.concatenate([np.asarray(x[b, 2020:4096], np.float32),
                                 np.zeros((4, D), np.float32)], axis=0)
        m = dict(shared)
        m["xT"] = np.ascontiguousarray(xs.T)
        m["bias6"] = bias_h[hh]
        maps.append(m)
    return maps

def kernel(**inputs):
    i = {k: np.asarray(v) for k, v in inputs.items()}
    if "main" not in _NC_CACHE:
        _NC_CACHE["main"] = build_program()[0]
    nc = _NC_CACHE["main"]
    res = run_bass_kernel_spmd(nc, _in_maps(i), core_ids=list(range(8)))
    out = np.empty((4, 4096, D), np.float32)
    for c in range(8):
        b, hh = divmod(c, 2)
        o = np.asarray(res.results[c]["out"])
        if hh == 0:
            out[b, 0:2048] = o[0:2048]
        else:
            out[b, 2048:4096] = o[28:28 + 2048]
    return out


# version 04
# speedup vs baseline: 1.2722x; 1.0766x over previous
import contextlib
import numpy as np
import ml_dtypes
import concourse.bass as bass
import concourse.mybir as mybir
from concourse.bass_utils import run_bass_kernel_spmd

F32 = mybir.dt.float32
BF16 = mybir.dt.bfloat16
AF = mybir.ActivationFunctionType
ALU = mybir.AluOpType
AX = mybir.AxisListType

D = 1024
DK = 8
DIN = 512
DBR = 256
HEADS = 8
HD = 32
PW = 2080
EW = 2084
QK_SCALE = float(HD) ** -0.5
BR = [(10, 5), (20, 10)]
H_MOE = 4096
HB = 512
N_EXP = 7
FFN_H = 2048

TILES = [(0, 512), (512, 512), (1024, 512), (1536, 512), (2048, 32)]

def _groups(w, s):
    lead = s + ((120 - s) // w) * w
    gs = [(0, lead)]
    pos = lead
    while pos + 120 <= EW - 30:
        gs.append((pos, 120))
        pos += 120
    gs.append((pos, EW - pos))
    return gs

GROUPS = {j: _groups(*BR[j]) for j in (0, 1)}

def _slot(j, g):
    return 3 * j + (0 if g == 0 else (2 if g == len(GROUPS[j]) - 1 else 1))

NSLOT = 6
BIAS_W = 8 * 120

_vc = {}
_vn = 0
def _vcol(name, n):
    global _vn
    _vc[name] = _vn
    _vn += n
for _nm, _n in [("down_b", 4), ("up_b", 8), ("norm1_w", 4), ("norm2_w", 8),
                ("ffn_b1", 16), ("ffn_b2", 8), ("proj_b", 4),
                ("qkv_b", 24), ("eps6", 1), ("eps5", 1)]:
    _vcol(_nm, _n)
NV = _vn

ENGINES = ("te", "act", "dve", "sp", "pool")

class Reg:
    __slots__ = ("w", "r")
    def __init__(self):
        self.w = None
        self.r = []

class _Op:
    __slots__ = ("fn", "inc", "waits", "needed", "sem", "gidx")
    def __init__(self, fn, inc, waits, sem, gidx):
        self.fn = fn; self.inc = inc; self.waits = waits
        self.needed = False; self.sem = sem; self.gidx = gidx

class Prog:
    def __init__(self):
        self.ops = {e: [] for e in ENGINES}
        self.wm = {e: {} for e in ENGINES}
        self.gidx = 0

    def emit(self, eng, fn, reads=(), writes=(), dma=False, boot=False, sem=None):
        deps = {}
        def add(dep):
            deng, didx = dep
            if deng == eng == "te":
                return
            if deng == eng and dma and self.ops[deng][didx].inc == 16:
                return
            key = (deng, self.ops[deng][didx].sem)
            cur = deps.get(key)
            if cur is None or cur[1] < didx:
                deps[key] = dep
        for r in reads:
            if r.w is not None:
                add(r.w)
        for wreg in writes:
            if wreg.w is not None:
                add(wreg.w)
            for rd in wreg.r:
                add(rd)
        waits = []
        wm = self.wm[eng]
        for (deng, didx) in deps.values():
            key = (deng, self.ops[deng][didx].sem)
            if wm.get(key, -1) < didx:
                waits.append((deng, didx))
                wm[key] = didx
        if sem is None:
            sem = "boot" if boot else ("poolc" if (eng == "pool" and not dma) else eng)
        op = _Op(fn, 16 if dma else 1, waits, sem, self.gidx)
        self.gidx += 1
        self.ops[eng].append(op)
        me = (eng, len(self.ops[eng]) - 1)
        for wreg in writes:
            wreg.w = me
            wreg.r = []
        for r in reads:
            r.r.append(me)
        return me

    @staticmethod
    def barrier(newreg, olds):
        for o in olds:
            if o.w is not None:
                newreg.r.append(o.w)
            newreg.r.extend(o.r)

    def finalize(self, block, sems, trunc=None):
        if trunc is not None:
            for e in ENGINES:
                self.ops[e] = [op for op in self.ops[e] if op.gidx < trunc]
        for e in ENGINES:
            for op in self.ops[e]:
                if op.sem == "boot" or op.inc == 16:
                    op.needed = True
                for (deng, didx) in op.waits:
                    self.ops[deng][didx].needed = True
        val = {}
        cum = {}
        for e in ENGINES:
            v = []
            for op in self.ops[e]:
                if op.needed:
                    cum[op.sem] = cum.get(op.sem, 0) + op.inc
                v.append(cum.get(op.sem, 0))
            val[e] = v
        boot_total = cum.get("boot", 0)
        def body(e):
            mysems = {}
            for op in self.ops[e]:
                if op.needed and op.inc == 16:
                    mysems[op.sem] = cum.get(op.sem, 0)
            def run(eng):
                for op in self.ops[e]:
                    for (deng, didx) in op.waits:
                        t = self.ops[deng][didx]
                        if t.sem == "boot":
                            eng.wait_ge(sems["boot"], boot_total)
                        else:
                            eng.wait_ge(sems[t.sem], val[deng][didx])
                    inst = op.fn(eng)
                    if op.needed:
                        inst.then_inc(sems[op.sem], op.inc)
                for s, tot in sorted(mysems.items()):
                    eng.wait_ge(sems[s], tot)
            return run
        if self.ops["te"]:
            block.tensor(body("te"))
        if self.ops["act"]:
            block.scalar(body("act"))
        if self.ops["dve"]:
            block.vector(body("dve"))
        if self.ops["sp"]:
            block.sync(body("sp"))
        if self.ops["pool"]:
            block.gpsimd(body("pool"))


def build_program(taps=(), trunc=None):
    nc = bass.Bass("TRN2", target_bir_lowering=False)
    ap = {}
    def din(name, shape, dt=F32):
        ap[name] = nc.dram_tensor(name, list(shape), dt, kind="ExternalInput").ap()
        return ap[name]
    din("xT", [D, PW])
    din("bias6", [NSLOT, 128, BIAS_W])
    din("w_down", [D, DIN]); din("w_up", [DIN, D])
    din("w_qkv1", [DBR, 3 * DBR]); din("w_qkv2", [DBR, 3 * DBR])
    din("w_projp", [2, HEADS * HD, DBR])
    din("w_ffn1", [D, FFN_H]); din("w_ffn2", [FFN_H, D])
    din("w_router", [D, 8])
    din("w_moe_up", [N_EXP, D, H_MOE]); din("w_moe_gate", [N_EXP, D, H_MOE])
    din("w_moe_down", [N_EXP, H_MOE, D])
    din("w_sh_up", [D, H_MOE]); din("w_sh_gate", [D, H_MOE]); din("w_sh_down", [H_MOE, D])
    din("vecs", [NV, 128])
    din("ident_f", [128, 128]); din("ident_b", [128, 128], BF16)
    din("ones_r", [1, 128]); din("ones_c", [128, 1], BF16)
    din("ident2", [64, 32], BF16)
    din("tri_u", [8, 8])
    din("sel8", [8, N_EXP * 128])
    out_ap = nc.dram_tensor("out", [PW, D], F32, kind="ExternalOutput").ap()
    tap_aps = {t: nc.dram_tensor("tap_" + t, list(s), F32, kind="ExternalOutput").ap()
               for (t, s) in taps}

    p = Prog()
    cm = contextlib.ExitStack()
    with cm:
        sb = lambda name, shape, dt=F32: cm.enter_context(nc.sbuf_tensor(name, shape, dt))
        R2 = sb("R2", [128, DK * PW])
        XBA = sb("XBA", [128, 2 * 4 * EW], BF16)
        XIB_OFF, O12_OFF, HBF_OFF = 0, 4 * EW, 0
        SCF = sb("SCF", [128, 2560])
        wup = [sb(f"wup{i}", [128, 8 * HB], BF16) for i in range(2)]
        wgt = [sb(f"wgt{i}", [128, 8 * HB], BF16) for i in range(2)]
        wdn = [sb(f"wdn{i}", [128, 4 * D], BF16) for i in range(2)]
        xbf = sb("xbf", [128, 4 * 512], BF16)
        hid = sb("hid", [128, 4 * 512], BF16)
        rW = sb("rW", [128, 8 * 8], BF16)
        vec = sb("vec", [128, NV])
        idf = sb("idf", [128, 128]); idb = sb("idb", [128, 128], BF16)
        onr = sb("onr", [1, 128]); onc = sb("onc", [128, 1], BF16)
        id2 = sb("id2_sb", [64, 32], BF16)
        triu = sb("triu", [8, 8])
        sel8 = sb("sel8_sb", [8, N_EXP * 128])
        qp = [sb(f"qp{i}", [64, 128], BF16) for i in range(4)]
        kp = [sb(f"kp{i}", [64, 128], BF16) for i in range(4)]
        vp = [sb(f"vp{i}", [64, 128], BF16) for i in range(4)]
        oh = [sb(f"oh{i}", [32, 128], BF16) for i in range(8)]
        sct = [sb("sct0", [128, 120]), sb("sct1", [128, 120])]
        et = [sb("et0", [128, 120]), sb("et1", [128, 120])]
        atb = [sb("atb0", [128, 120], BF16), sb("atb1", [128, 120], BF16)]
        atT = [sb("atT0", [128, 120], BF16), sb("atT1", [128, 120], BF16)]
        vT = [sb("vT0", [128, 32], BF16), sb("vT1", [128, 32], BF16)]
        nmax = [sb("nmax0", [128, 4]), sb("nmax1", [128, 4])]
        sc0 = sb("sc0", [128, 512]); sc1 = sb("sc1", [128, 512])
        sbf = [sb("sbf0", [128, 512], BF16), sb("sbf1", [128, 512], BF16)]
        rowst = sb("rowst", [1, 1536])
        m2all = sb("m2all", [8, PW])
        lgF = sb("lgF", [8, 512])
        ltm = sb("ltm", [128, 32])
        eqtm = sb("eqtm", [128, 32])
        mxp = sb("mxp", [128, 8])
        eqT = sb("eqT", [8, 512])
        v2r = sb("v2r", [1, 512])
        ttile = sb("ttile", [128, 1024])
        pb = [cm.enter_context(nc.psum_tensor(f"pb{i}", [128, 512], F32)) for i in range(6)]
        pT = [cm.enter_context(nc.psum_tensor(f"pT{i}", [128, 1024], BF16)) for i in range(2)]

        RG = Reg
        r_xT = RG()
        r_R2 = RG(); r_XIB = RG(); r_O12 = RG(); r_HBF = RG()
        r_SCF_xipre = RG(); r_SCF_xf = RG(); r_SCF_m2b = [RG() for _ in range(5)]
        r_SCF_lnsq = RG()
        r_wup = [RG(), RG()]; r_wgt = [RG(), RG()]; r_wdn = [RG(), RG()]
        r_xbf = RG(); r_hid = [RG() for _ in range(4)]; r_rW = RG(); r_vec = RG()
        r_idf = RG(); r_idb = RG(); r_id2 = RG(); r_onr = RG(); r_onc = RG(); r_triu = RG(); r_sel8 = RG()
        r_qp = [RG() for _ in range(4)]; r_kp = [RG() for _ in range(4)]; r_vp = [RG() for _ in range(4)]
        r_oh = [RG() for _ in range(8)]
        r_sct = [RG(), RG()]; r_et = [RG(), RG()]; r_atb = [RG(), RG()]
        r_atT = [RG(), RG()]; r_vT = [RG(), RG()]; r_nmax = [RG(), RG()]
        r_sc0 = RG(); r_sc1 = RG(); r_sbf = [RG(), RG()]
        r_rowst = RG(); r_m2all = RG(); r_lgF = RG(); r_ltm = RG(); r_eqtm = RG()
        r_mxp = RG(); r_eqT = RG(); r_v2r = RG(); r_ttile = RG()
        r_pb = [RG() for _ in range(6)]; r_pT = [RG(), RG()]
        r_bias = RG()
        r_proj = RG()
        r_out = RG()
        r_taps = {t: RG() for t in tap_aps}

        bank_i = [0]
        def nbank():
            b = bank_i[0] % 6
            bank_i[0] += 1
            return pb[b], r_pb[b]

        E = p.emit

        for vi in range(NV):
            E("pool", lambda g, vi=vi: g.dma_start(
                out=vec[:, vi:vi + 1],
                in_=ap["vecs"][vi].rearrange("(p o) -> p o", o=1)),
              writes=[r_vec], dma=True, boot=True)
        E("pool", lambda g: g.dma_start(out=idf[:], in_=ap["ident_f"]), writes=[r_idf], dma=True, boot=True)
        E("pool", lambda g: g.dma_start(out=idb[:], in_=ap["ident_b"]), writes=[r_idb], dma=True, boot=True)
        E("pool", lambda g: g.dma_start(out=onr[:], in_=ap["ones_r"]), writes=[r_onr], dma=True, boot=True)
        E("pool", lambda g: g.dma_start(out=onc[:], in_=ap["ones_c"]), writes=[r_onc], dma=True, boot=True)
        E("pool", lambda g: g.dma_start(out=id2[:], in_=ap["ident2"]), writes=[r_id2], dma=True, boot=True)
        E("pool", lambda g: g.dma_start(out=triu[:], in_=ap["tri_u"]), writes=[r_triu], dma=True, boot=True)
        E("pool", lambda g: g.dma_start(out=sel8[:], in_=ap["sel8"]), writes=[r_sel8], dma=True, boot=True)

        def bias_dst(sl):
            if sl < 4:
                return wup[1][:, 960 * sl:960 * (sl + 1)]
            return wdn[1][:, 960 * (sl - 4):960 * (sl - 3)]
        for sl in range(NSLOT):
            E("pool", lambda g, sl=sl: g.dma_start(out=bias_dst(sl), in_=ap["bias6"][sl]),
              writes=[r_bias], dma=True, boot=True)
        E("pool", lambda g: g.dma_start(out=wup[0][:].rearrange("p (k m) -> p k m", k=8),
                                        in_=ap["w_down"].rearrange("(k p) m -> p k m", p=128)),
          writes=[r_wup[0]], dma=True, boot=True)
        E("pool", lambda g: g.dma_start(out=wdn[0][:].rearrange("p (k m) -> p k m", k=4),
                                        in_=ap["w_up"].rearrange("(k p) m -> p k m", p=128)),
          writes=[r_wdn[0]], dma=True, boot=True)
        for j in range(2):
            E("pool", lambda g, j=j: g.dma_start(
                out=wgt[j][:, 0:1536].rearrange("p (k m) -> p k m", k=2),
                in_=ap["w_qkv" + str(j + 1)].rearrange("(k p) m -> p k m", p=128)),
              writes=[r_wgt[j]], dma=True, boot=True)
            E("pool", lambda g, j=j: g.dma_start(
                out=wgt[j][0:32, 1536:3584].rearrange("p (h m) -> p h m", h=8),
                in_=ap["w_projp"][j].rearrange("(h p) m -> p h m", p=32)),
              writes=[r_proj], dma=True, boot=True)
        E("pool", lambda g: g.dma_start(out=rW[:].rearrange("p (k e) -> p k e", k=8),
                                        in_=ap["w_router"].rearrange("(k p) e -> p k e", p=128)),
          writes=[r_rW], dma=True, boot=True)

        vslice = lambda name, i: vec[:, _vc[name] + i:_vc[name] + i + 1]
        v64 = lambda i: vec[0:64, _vc["qkv_b"] + i:_vc["qkv_b"] + i + 1]

        xT_v = ap["xT"].rearrange("(k p) t -> p k t", p=128)
        marks = {"boot": p.gidx}

        def tap_R2(t):
            if t in tap_aps:
                for mo in range(DK):
                    E("sp", lambda s_, mo=mo, t=t: s_.dma_start(
                        out=tap_aps[t][mo * 128:(mo + 1) * 128, :], in_=R2[:, mo * PW:(mo + 1) * PW]),
                      reads=[r_R2], writes=[r_taps[t]], dma=True)

        def tap_XBA(t, off, rr, nch, width):
            if t in tap_aps:
                for mo in range(nch):
                    for seg in range(0, width, 512):
                        w = min(512, width - seg)
                        E("act", lambda a, mo=mo, off=off, seg=seg, w=w: a.copy(
                            out=sc0[:, 0:w], in_=XBA[:, off + mo * width + seg:off + mo * width + seg + w]),
                          reads=[rr], writes=[r_sc0])
                        E("sp", lambda s_, mo=mo, seg=seg, w=w, t=t: s_.dma_start(
                            out=tap_aps[t][mo * 128:(mo + 1) * 128, seg:seg + w], in_=sc0[:, 0:w]),
                          reads=[r_sc0], writes=[r_taps[t]], dma=True)

        for (t0, tl) in TILES:
            dbanks = [nbank() for _ in range(4)]
            for kh in range(2):
                E("pool", lambda g, kh=kh, t0=t0, tl=tl: g.dma_start(
                    out=xbf[:, 0:4 * tl].rearrange("p (k t) -> p k t", k=4),
                    in_=xT_v[:, 4 * kh:4 * kh + 4, t0:t0 + tl]),
                  reads=[r_xT], writes=[r_xbf], dma=True, sem="sxbf")
                for mo in range(4):
                    bk, rb = dbanks[mo]
                    for k4 in range(4):
                        k = 4 * kh + k4
                        E("te", lambda t, k=k, k4=k4, mo=mo, tl=tl, bk=bk: t.matmul(
                            out=bk[:, 0:tl],
                            lhsT=wup[0][:, (k * 4 + mo) * 128:(k * 4 + mo + 1) * 128],
                            rhs=xbf[:, k4 * tl:(k4 + 1) * tl],
                            start=(k == 0), stop=(k == 7)),
                          reads=[r_wup[0], r_xbf], writes=[rb])
            sqb, sqr = nbank()
            for mo in range(4):
                bk, rb = dbanks[mo]
                E("act", lambda a, mo=mo, tl=tl, bk=bk: a.add(
                    out=SCF[:, mo * 512:mo * 512 + tl], in_=bk[:, 0:tl],
                    add=vslice("down_b", mo)),
                  reads=[rb, r_vec], writes=[r_SCF_xipre])
                E("act", lambda a, mo=mo, tl=tl, bk=bk: a.activation(
                    out=sbf[mo % 2][:, 0:tl], in_=bk[:, 0:tl], func=AF.Square,
                    bias=vslice("down_b", mo), scale=1.0),
                  reads=[rb, r_vec], writes=[r_sbf[mo % 2]])
                E("te", lambda t, mo=mo, tl=tl, sqb=sqb: t.matmul(
                    out=sqb[0:1, 0:tl], lhsT=onc[:], rhs=sbf[mo % 2][:, 0:tl],
                    start=(mo == 0), stop=(mo == 3)),
                  reads=[r_sbf[mo % 2], r_onc], writes=[sqr])
            E("act", lambda a, sqb=sqb, tl=tl: a.activation(
                out=rowst[:, 0:tl], in_=sqb[0:1, 0:tl], func=AF.Sqrt,
                bias=vec[0:1, _vc["eps6"]:_vc["eps6"] + 1], scale=1.0 / 512.0),
              reads=[sqr], writes=[r_rowst])
            E("dve", lambda v, tl=tl: v.reciprocal(out=rowst[:, 512:512 + tl], in_=rowst[:, 0:tl]),
              reads=[r_rowst], writes=[r_rowst])
            bb, rbb = nbank()
            E("te", lambda t, tl=tl, bb=bb: t.matmul(
                out=bb[:, 0:tl], lhsT=onr[:], rhs=rowst[:, 512:512 + tl], start=True, stop=True),
              reads=[r_rowst, r_onr], writes=[rbb])
            for mo in range(4):
                E("dve", lambda v, mo=mo, tl=tl, bb=bb: v.tensor_tensor(
                    out=sc0[:, 0:tl], in0=SCF[:, mo * 512:mo * 512 + tl],
                    in1=bb[:, 0:tl], op=ALU.mult),
                  reads=[r_SCF_xipre, rbb], writes=[r_sc0])
                E("dve", lambda v, mo=mo, t0=t0, tl=tl: v.tensor_scalar(
                    out=XBA[:, XIB_OFF + mo * EW + t0:XIB_OFF + mo * EW + t0 + tl],
                    in0=sc0[:, 0:tl], scalar1=vslice("norm1_w", mo), scalar2=None, op0=ALU.mult),
                  reads=[r_sc0, r_vec], writes=[r_XIB])
        for mo in range(4):
            E("pool", lambda g, mo=mo: g.memset(
                XBA[:, XIB_OFF + mo * EW + 2076:XIB_OFF + mo * EW + 2084], 0.0),
              writes=[r_XIB])

        marks["p1"] = p.gidx
        tap_XBA("xi", XIB_OFF, r_XIB, 4, EW)
        for j in (0, 1):
            for g, (g0, gl) in enumerate(GROUPS[j]):
                sl = _slot(j, g)
                for mc in range(6):
                    bk, rb = nbank()
                    for k in range(2):
                        kc = 2 * j + k
                        E("te", lambda t, mc=mc, k=k, kc=kc, g0=g0, gl=gl, j=j, bk=bk: t.matmul(
                            out=bk[:, 0:gl],
                            lhsT=wgt[j][:, (k * 6 + mc) * 128:(k * 6 + mc + 1) * 128],
                            rhs=XBA[:, XIB_OFF + kc * EW + g0:XIB_OFF + kc * EW + g0 + gl],
                            start=(k == 0), stop=(k == 1)),
                          reads=[r_wgt[j], r_XIB], writes=[rb])
                    kind, sub = divmod(mc, 2)
                    tgt = (qp, kp, vp)[kind]
                    rtg = (r_qp, r_kp, r_vp)[kind]
                    for half in range(2):
                        pi = sub * 2 + half
                        E("dve", lambda v, half=half, pi=pi, gl=gl, j=j, mc=mc, tgt=tgt, bk=bk: v.tensor_scalar(
                            out=tgt[pi][:, 0:gl], in0=bk[64 * half:64 * (half + 1), 0:gl],
                            scalar1=v64(j * 12 + mc * 2 + half), scalar2=None, op0=ALU.add),
                          reads=[rb, r_vec], writes=[rtg[pi]])
                sbanks = []
                def emit_scores(h):
                    pi, su = divmod(h, 2)
                    bk, rb = nbank()
                    E("te", lambda t, pi=pi, su=su, gl=gl, bk=bk: t.matmul(
                        out=bk[0:gl, 0:gl], lhsT=qp[pi][32 * su:32 * (su + 1), 0:gl],
                        rhs=kp[pi][32 * su:32 * (su + 1), 0:gl], start=True, stop=True),
                      reads=[r_qp[pi], r_kp[pi]], writes=[rb])
                    sbanks.append((bk, rb))
                for h in range(4):
                    emit_scores(h)
                for h in range(HEADS):
                    pi, su = divmod(h, 2)
                    pa = h % 2
                    bk, rb = sbanks[h]
                    E("dve", lambda v, gl=gl, sl=sl, h=h, bk=bk, pa=pa: v.scalar_tensor_tensor(
                        out=sct[pa][0:gl, 0:gl], in0=bk[0:gl, 0:gl], scalar=QK_SCALE,
                        in1=bias_dst(sl)[0:gl, 120 * h:120 * h + gl],
                        op0=ALU.mult, op1=ALU.add),
                      reads=[rb, r_bias], writes=[r_sct[pa]])
                    E("dve", lambda v, gl=gl, pa=pa: v.reduce_max(
                        out=nmax[pa][0:gl, 0:1], in_=sct[pa][0:gl, 0:gl], axis=AX.X, negate=True),
                      reads=[r_sct[pa]], writes=[r_nmax[pa]])
                    E("act", lambda a, gl=gl, pa=pa: a.activation(
                        out=et[pa][0:gl, 0:gl], in_=sct[pa][0:gl, 0:gl], func=AF.Exp,
                        bias=nmax[pa][0:gl, 0:1], scale=1.0),
                      reads=[r_sct[pa], r_nmax[pa]], writes=[r_et[pa]])
                    E("dve", lambda v, gl=gl, pa=pa: v.reduce_sum(
                        out=nmax[pa][0:gl, 1:2], in_=et[pa][0:gl, 0:gl], axis=AX.X),
                      reads=[r_et[pa]], writes=[r_nmax[pa]])
                    E("dve", lambda v, gl=gl, pa=pa: v.reciprocal(
                        out=nmax[pa][0:gl, 2:3], in_=nmax[pa][0:gl, 1:2]),
                      reads=[r_nmax[pa]], writes=[r_nmax[pa]])
                    E("dve", lambda v, gl=gl, pa=pa: v.tensor_scalar(
                        out=atb[pa][0:gl, 0:gl], in0=et[pa][0:gl, 0:gl],
                        scalar1=nmax[pa][0:gl, 2:3], scalar2=None, op0=ALU.mult),
                      reads=[r_et[pa], r_nmax[pa]], writes=[r_atb[pa]])
                    E("te", lambda t, gl=gl, pa=pa: t.transpose(
                        out=pT[pa][0:gl, 0:gl], in_=atb[pa][0:gl, 0:gl], identity=idb[0:gl, 0:gl]),
                      reads=[r_atb[pa], r_idb], writes=[r_pT[pa]])
                    E("te", lambda t, pi=pi, su=su, gl=gl, pa=pa: t.transpose(
                        out=pT[pa][0:gl, 512:544], in_=vp[pi][32 * su:32 * (su + 1), 0:gl],
                        identity=id2[32 * su:32 * (su + 1), 0:32]),
                      reads=[r_vp[pi], r_id2], writes=[r_pT[pa]])
                    E("dve", lambda v, gl=gl, pa=pa: v.tensor_copy(
                        out=atT[pa][0:gl, 0:gl], in_=pT[pa][0:gl, 0:gl]),
                      reads=[r_pT[pa]], writes=[r_atT[pa]])
                    E("dve", lambda v, gl=gl, pa=pa: v.tensor_copy(
                        out=vT[pa][0:gl, 0:32], in_=pT[pa][0:gl, 512:544]),
                      reads=[r_pT[pa]], writes=[r_vT[pa]])
                    bk2, rb2 = nbank()
                    E("te", lambda t, gl=gl, bk2=bk2, pa=pa: t.matmul(
                        out=bk2[0:32, 0:gl], lhsT=vT[pa][0:gl, 0:32], rhs=atT[pa][0:gl, 0:gl],
                        start=True, stop=True),
                      reads=[r_vT[pa], r_atT[pa]], writes=[rb2])
                    E("dve", lambda v, h=h, gl=gl, bk2=bk2: v.tensor_copy(
                        out=oh[h][:, 0:gl], in_=bk2[0:32, 0:gl]),
                      reads=[rb2], writes=[r_oh[h]])
                    if h + 4 < HEADS:
                        emit_scores(h + 4)
                for mo in range(2):
                    bk, rb = nbank()
                    for h in range(HEADS):
                        E("te", lambda t, h=h, mo=mo, gl=gl, j=j, bk=bk: t.matmul(
                            out=bk[:, 0:gl],
                            lhsT=wgt[j][0:32, 1536 + h * 256 + mo * 128:1536 + h * 256 + (mo + 1) * 128],
                            rhs=oh[h][:, 0:gl], start=(h == 0), stop=(h == 7)),
                          reads=[r_proj, r_oh[h]], writes=[rb])
                    oc = 2 * j + mo
                    E("dve", lambda v, oc=oc, g0=g0, gl=gl, bk=bk: v.tensor_scalar(
                        out=XBA[:, O12_OFF + oc * EW + g0:O12_OFF + oc * EW + g0 + gl],
                        in0=bk[:, 0:gl], scalar1=vslice("proj_b", oc), scalar2=None, op0=ALU.add),
                      reads=[rb, r_vec], writes=[r_O12])

        marks["p2"] = p.gidx
        tap_XBA("o12", O12_OFF, r_O12, 4, EW)
        Prog.barrier(r_SCF_xf, [r_SCF_xipre])
        for (t0, tl) in TILES:
            for kh in range(2):
                E("sp", lambda s_, kh=kh, t0=t0, tl=tl: s_.dma_start(
                    out=SCF[:, 0:4 * tl].rearrange("p (k t) -> p k t", k=4),
                    in_=xT_v[:, 4 * kh:4 * kh + 4, t0:t0 + tl]),
                  reads=[r_xT], writes=[r_SCF_xf], dma=True)
                for mo4 in range(4):
                    mo = 4 * kh + mo4
                    bk, rb = nbank()
                    for k in range(4):
                        E("te", lambda t, k=k, mo=mo, t0=t0, tl=tl, bk=bk: t.matmul(
                            out=bk[:, 0:tl], lhsT=wdn[0][:, (k * 8 + mo) * 128:(k * 8 + mo + 1) * 128],
                            rhs=XBA[:, O12_OFF + k * EW + t0:O12_OFF + k * EW + t0 + tl],
                            start=(k == 0), stop=(k == 3)),
                          reads=[r_wdn[0], r_O12], writes=[rb])
                    E("dve", lambda v, mo=mo, mo4=mo4, t0=t0, tl=tl, bk=bk: v.scalar_tensor_tensor(
                        out=R2[:, mo * PW + t0:mo * PW + t0 + tl], in0=bk[:, 0:tl],
                        scalar=vslice("up_b", mo), in1=SCF[:, mo4 * tl:(mo4 + 1) * tl],
                        op0=ALU.add, op1=ALU.add),
                      reads=[rb, r_vec, r_SCF_xf], writes=[r_R2])

        marks["p3"] = p.gidx
        tap_R2("x2")
        Prog.barrier(r_HBF, [r_XIB, r_O12])
        for (t0, tl) in TILES:
            s2b, s2r = nbank()
            for mo in range(DK):
                E("act", lambda a, mo=mo, t0=t0, tl=tl: a.activation(
                    out=sbf[mo % 2][:, 0:tl], in_=R2[:, mo * PW + t0:mo * PW + t0 + tl],
                    func=AF.Square),
                  reads=[r_R2], writes=[r_sbf[mo % 2]])
                E("te", lambda t, mo=mo, tl=tl, s2b=s2b: t.matmul(
                    out=s2b[0:1, 0:tl], lhsT=onc[:], rhs=sbf[mo % 2][:, 0:tl],
                    start=(mo == 0), stop=(mo == DK - 1)),
                  reads=[r_sbf[mo % 2], r_onc], writes=[s2r])
            E("act", lambda a, s2b=s2b, tl=tl: a.activation(
                out=rowst[:, 0:tl], in_=s2b[0:1, 0:tl], func=AF.Sqrt,
                bias=vec[0:1, _vc["eps6"]:_vc["eps6"] + 1], scale=1.0 / 1024.0),
              reads=[s2r], writes=[r_rowst])
            E("dve", lambda v, tl=tl: v.reciprocal(out=rowst[:, 512:512 + tl], in_=rowst[:, 0:tl]),
              reads=[r_rowst], writes=[r_rowst])
            bb, rbb = nbank()
            E("te", lambda t, tl=tl, bb=bb: t.matmul(
                out=bb[:, 0:tl], lhsT=onr[:], rhs=rowst[:, 512:512 + tl], start=True, stop=True),
              reads=[r_rowst, r_onr], writes=[rbb])
            for mo in range(DK):
                E("dve", lambda v, mo=mo, t0=t0, tl=tl, bb=bb: v.tensor_tensor(
                    out=sc0[:, 0:tl], in0=R2[:, mo * PW + t0:mo * PW + t0 + tl],
                    in1=bb[:, 0:tl], op=ALU.mult),
                  reads=[r_R2, rbb], writes=[r_sc0])
                E("dve", lambda v, mo=mo, t0=t0, tl=tl: v.tensor_scalar(
                    out=XBA[:, HBF_OFF + mo * PW + t0:HBF_OFF + mo * PW + t0 + tl],
                    in0=sc0[:, 0:tl], scalar1=vslice("norm2_w", mo), scalar2=None, op0=ALU.mult),
                  reads=[r_sc0, r_vec], writes=[r_HBF])

        marks["p4"] = p.gidx
        Prog.barrier(r_wup[1], [r_bias])
        Prog.barrier(r_wdn[1], [r_bias])
        for blk in range(4):
            wb = blk % 2
            E("pool", lambda g, blk=blk, wb=wb: g.dma_start(
                out=wup[wb][:].rearrange("p (k h) -> p k h", k=8),
                in_=ap["w_ffn1"][:, blk * HB:(blk + 1) * HB].rearrange("(k p) h -> p k h", p=128)),
              writes=[r_wup[wb]], dma=True, sem="swup%d" % wb)
            E("pool", lambda g, blk=blk, wb=wb: g.dma_start(
                out=wdn[wb][:].rearrange("p (c d) -> p c d", c=4),
                in_=ap["w_ffn2"][blk * HB:(blk + 1) * HB, :].rearrange("(c p) d -> p c d", p=128)),
              writes=[r_wdn[wb]], dma=True, sem="swdn%d" % wb)
            for (t0, tl) in TILES:
                for m in range(4):
                    bk, rb = nbank()
                    for k in range(DK):
                        E("te", lambda t, k=k, m=m, t0=t0, tl=tl, wb=wb, bk=bk: t.matmul(
                            out=bk[:, 0:tl], lhsT=wup[wb][:, k * HB + m * 128:k * HB + (m + 1) * 128],
                            rhs=XBA[:, HBF_OFF + k * PW + t0:HBF_OFF + k * PW + t0 + tl],
                            start=(k == 0), stop=(k == DK - 1)),
                          reads=[r_wup[wb], r_HBF], writes=[rb])
                    bi = blk * 4 + m
                    E("act", lambda a, bi=bi, m=m, tl=tl, bk=bk: a.activation(
                        out=hid[:, m * 512:m * 512 + tl], in_=bk[:, 0:tl], func=AF.Gelu,
                        bias=vslice("ffn_b1", bi), scale=1.0),
                      reads=[rb, r_vec], writes=[r_hid[m]])
                for mo in range(DK):
                    bk, rb = nbank()
                    for c in range(4):
                        E("te", lambda t, c=c, mo=mo, tl=tl, wb=wb, bk=bk: t.matmul(
                            out=bk[:, 0:tl], lhsT=wdn[wb][:, c * D + mo * 128:c * D + (mo + 1) * 128],
                            rhs=hid[:, c * 512:c * 512 + tl], start=(c == 0), stop=(c == 3)),
                          reads=[r_wdn[wb], r_hid[c]], writes=[rb])
                    if blk == 0:
                        E("dve", lambda v, mo=mo, t0=t0, tl=tl, bk=bk: v.scalar_tensor_tensor(
                            out=R2[:, mo * PW + t0:mo * PW + t0 + tl], in0=bk[:, 0:tl],
                            scalar=vslice("ffn_b2", mo), in1=R2[:, mo * PW + t0:mo * PW + t0 + tl],
                            op0=ALU.add, op1=ALU.add),
                          reads=[rb, r_vec, r_R2], writes=[r_R2])
                    else:
                        E("dve", lambda v, mo=mo, t0=t0, tl=tl, bk=bk: v.tensor_tensor(
                            out=R2[:, mo * PW + t0:mo * PW + t0 + tl], in0=bk[:, 0:tl],
                            in1=R2[:, mo * PW + t0:mo * PW + t0 + tl], op=ALU.add),
                          reads=[rb, r_R2], writes=[r_R2])

        for (t0, tl) in TILES:
            for kh in range(2):
                E("sp", lambda s_, kh=kh, t0=t0, tl=tl: s_.dma_start(
                    out=SCF[:, 0:4 * tl].rearrange("p (k t) -> p k t", k=4),
                    in_=xT_v[:, 4 * kh:4 * kh + 4, t0:t0 + tl]),
                  reads=[r_xT], writes=[r_SCF_xf], dma=True)
                for mo4 in range(4):
                    mo = 4 * kh + mo4
                    E("dve", lambda v, mo=mo, mo4=mo4, t0=t0, tl=tl: v.tensor_tensor(
                        out=R2[:, mo * PW + t0:mo * PW + t0 + tl],
                        in0=R2[:, mo * PW + t0:mo * PW + t0 + tl],
                        in1=SCF[:, mo4 * tl:(mo4 + 1) * tl], op=ALU.add),
                      reads=[r_R2, r_SCF_xf], writes=[r_R2])

        marks["p5"] = p.gidx
        tap_R2("s1")
        for (t0, tl) in TILES:
            s1b, s1r = nbank()
            s2b, s2r = nbank()
            for mo in range(DK):
                E("act", lambda a, mo=mo, t0=t0, tl=tl: a.copy(
                    out=sbf[0][:, 0:tl], in_=R2[:, mo * PW + t0:mo * PW + t0 + tl]),
                  reads=[r_R2], writes=[r_sbf[0]])
                E("te", lambda t, mo=mo, tl=tl, s1b=s1b: t.matmul(
                    out=s1b[0:1, 0:tl], lhsT=onc[:], rhs=sbf[0][:, 0:tl],
                    start=(mo == 0), stop=(mo == DK - 1)),
                  reads=[r_sbf[0], r_onc], writes=[s1r])
                E("act", lambda a, mo=mo, t0=t0, tl=tl: a.activation(
                    out=sbf[1][:, 0:tl], in_=R2[:, mo * PW + t0:mo * PW + t0 + tl], func=AF.Square),
                  reads=[r_R2], writes=[r_sbf[1]])
                E("te", lambda t, mo=mo, tl=tl, s2b=s2b: t.matmul(
                    out=s2b[0:1, 0:tl], lhsT=onc[:], rhs=sbf[1][:, 0:tl],
                    start=(mo == 0), stop=(mo == DK - 1)),
                  reads=[r_sbf[1], r_onc], writes=[s2r])
            E("dve", lambda v, s1b=s1b, tl=tl: v.tensor_scalar(
                out=rowst[:, 0:tl], in0=s1b[0:1, 0:tl], scalar1=1.0 / 1024.0, scalar2=None,
                op0=ALU.mult),
              reads=[s1r], writes=[r_rowst])
            E("act", lambda a, tl=tl: a.activation(
                out=sc0[0:1, 0:tl], in_=rowst[:, 0:tl], func=AF.Square),
              reads=[r_rowst], writes=[r_sc0])
            E("dve", lambda v, s2b=s2b, tl=tl: v.scalar_tensor_tensor(
                out=rowst[:, 512:512 + tl], in0=s2b[0:1, 0:tl], scalar=1.0 / 1024.0,
                in1=sc0[0:1, 0:tl], op0=ALU.mult, op1=ALU.subtract),
              reads=[s2r, r_sc0], writes=[r_rowst])
            E("act", lambda a, tl=tl: a.activation(
                out=sc0[0:1, 0:tl], in_=rowst[:, 512:512 + tl], func=AF.Sqrt,
                bias=vec[0:1, _vc["eps5"]:_vc["eps5"] + 1], scale=1.0),
              reads=[r_rowst], writes=[r_sc0])
            E("dve", lambda v, tl=tl: v.reciprocal(
                out=rowst[:, 512:512 + tl], in_=sc0[0:1, 0:tl]),
              reads=[r_sc0], writes=[r_rowst])
            E("dve", lambda v, tl=tl: v.tensor_tensor(
                out=rowst[:, 1024:1024 + tl], in0=rowst[:, 0:tl], in1=rowst[:, 512:512 + tl],
                op=ALU.mult),
              reads=[r_rowst], writes=[r_rowst])
            rbb_b, rbb_r = nbank()
            mrb_b, mrb_r = nbank()
            E("te", lambda t, tl=tl, rbb_b=rbb_b: t.matmul(
                out=rbb_b[:, 0:tl], lhsT=onr[:], rhs=rowst[:, 512:512 + tl], start=True, stop=True),
              reads=[r_rowst, r_onr], writes=[rbb_r])
            E("te", lambda t, tl=tl, mrb_b=mrb_b: t.matmul(
                out=mrb_b[:, 0:tl], lhsT=onr[:], rhs=rowst[:, 1024:1024 + tl], start=True, stop=True),
              reads=[r_rowst, r_onr], writes=[mrb_r])
            for mo in range(DK):
                E("dve", lambda v, mo=mo, t0=t0, tl=tl, rbb_b=rbb_b: v.tensor_tensor(
                    out=sc0[:, 0:tl], in0=R2[:, mo * PW + t0:mo * PW + t0 + tl],
                    in1=rbb_b[:, 0:tl], op=ALU.mult),
                  reads=[r_R2, rbb_r], writes=[r_sc0])
                E("dve", lambda v, mo=mo, t0=t0, tl=tl, mrb_b=mrb_b: v.tensor_tensor(
                    out=R2[:, mo * PW + t0:mo * PW + t0 + tl], in0=sc0[:, 0:tl],
                    in1=mrb_b[:, 0:tl], op=ALU.subtract),
                  reads=[r_sc0, mrb_r], writes=[r_R2])
                E("act", lambda a, mo=mo, t0=t0, tl=tl: a.copy(
                    out=XBA[:, HBF_OFF + mo * PW + t0:HBF_OFF + mo * PW + t0 + tl],
                    in_=R2[:, mo * PW + t0:mo * PW + t0 + tl]),
                  reads=[r_R2], writes=[r_HBF])

        marks["p6"] = p.gidx
        tap_R2("y1")
        for (t0, tl) in TILES:
            lgb, lgr = nbank()
            for k in range(DK):
                E("te", lambda t, k=k, t0=t0, tl=tl, lgb=lgb: t.matmul(
                    out=lgb[0:8, 0:tl], lhsT=rW[:, k * 8:(k + 1) * 8],
                    rhs=XBA[:, HBF_OFF + k * PW + t0:HBF_OFF + k * PW + t0 + tl],
                    start=(k == 0), stop=(k == DK - 1)),
                  reads=[r_rW, r_HBF], writes=[lgr])
            E("dve", lambda v, tl=tl, lgb=lgb: v.tensor_copy(out=lgF[:, 0:tl], in_=lgb[0:8, 0:tl]),
              reads=[lgr], writes=[r_lgF])
            nsub = (tl + 127) // 128
            tpb, tpr = nbank()
            for ji in range(nsub):
                jl = min(128, tl - ji * 128)
                E("te", lambda t, ji=ji, jl=jl, tpb=tpb: t.transpose(
                    out=tpb[0:jl, 8 * ji:8 * ji + 8], in_=lgF[0:8, ji * 128:ji * 128 + jl],
                    identity=idf[0:8, 0:8]),
                  reads=[r_lgF, r_idf], writes=[tpr])
            E("dve", lambda v, nsub=nsub, tl=tl, tpb=tpb: v.tensor_copy(
                out=ltm[0:min(tl, 128), 0:8 * nsub] if tl < 128 else ltm[:, 0:8 * nsub],
                in_=tpb[0:min(tl, 128), 0:8 * nsub] if tl < 128 else tpb[0:128, 0:8 * nsub]),
              reads=[tpr], writes=[r_ltm])
            for ji in range(nsub):
                jl = min(128, tl - ji * 128)
                E("dve", lambda v, ji=ji, jl=jl: v.reduce_max(
                    out=mxp[0:jl, 0:1], in_=ltm[0:jl, 8 * ji:8 * ji + 8], axis=AX.X, negate=True),
                  reads=[r_ltm], writes=[r_mxp])
                E("act", lambda a, ji=ji, jl=jl: a.activation(
                    out=eqtm[0:jl, 8 * ji:8 * ji + 8], in_=ltm[0:jl, 8 * ji:8 * ji + 8],
                    func=AF.Exp, bias=mxp[0:jl, 0:1], scale=1.0),
                  reads=[r_ltm, r_mxp], writes=[r_eqtm])
                E("dve", lambda v, ji=ji, jl=jl: v.reduce_sum(
                    out=mxp[0:jl, 1:2], in_=eqtm[0:jl, 8 * ji:8 * ji + 8], axis=AX.X),
                  reads=[r_eqtm], writes=[r_mxp])
                E("dve", lambda v, jl=jl: v.reciprocal(out=mxp[0:jl, 2:3], in_=mxp[0:jl, 1:2]),
                  reads=[r_mxp], writes=[r_mxp])
                E("dve", lambda v, ji=ji, jl=jl: v.tensor_scalar(
                    out=eqtm[0:jl, 8 * ji:8 * ji + 8], in0=eqtm[0:jl, 8 * ji:8 * ji + 8],
                    scalar1=mxp[0:jl, 2:3], scalar2=None, op0=ALU.mult),
                  reads=[r_eqtm, r_mxp], writes=[r_eqtm])
                E("dve", lambda v, ji=ji, jl=jl: v.reduce_max(
                    out=mxp[0:jl, 3:4], in_=eqtm[0:jl, 8 * ji:8 * ji + 8], axis=AX.X),
                  reads=[r_eqtm], writes=[r_mxp])
                E("dve", lambda v, ji=ji, jl=jl: v.tensor_scalar(
                    out=ltm[0:jl, 8 * ji:8 * ji + 8], in0=eqtm[0:jl, 8 * ji:8 * ji + 8],
                    scalar1=mxp[0:jl, 3:4], scalar2=None, op0=ALU.is_equal),
                  reads=[r_eqtm, r_mxp], writes=[r_ltm])
                E("dve", lambda v, jl=jl: v.tensor_scalar(
                    out=mxp[0:jl, 4:5], in0=mxp[0:jl, 3:4], scalar1=2.0, scalar2=None,
                    op0=ALU.mult),
                  reads=[r_mxp], writes=[r_mxp])
                tb2, tr2 = nbank()
                E("te", lambda t, ji=ji, jl=jl, tb2=tb2: t.transpose(
                    out=tb2[0:8, 0:jl], in_=ltm[0:jl, 8 * ji:8 * ji + 8], identity=idf[0:jl, 0:jl]),
                  reads=[r_ltm, r_idf], writes=[tr2])
                E("dve", lambda v, ji=ji, jl=jl, tb2=tb2: v.tensor_copy(
                    out=eqT[:, ji * 128:ji * 128 + jl], in_=tb2[0:8, 0:jl]),
                  reads=[tr2], writes=[r_eqT])
                tb3, tr3 = nbank()
                E("te", lambda t, ji=ji, jl=jl, tb3=tb3: t.transpose(
                    out=tb3[0:1, 0:jl], in_=mxp[0:jl, 4:5], identity=idf[0:jl, 0:jl]),
                  reads=[r_mxp, r_idf], writes=[tr3])
                E("dve", lambda v, ji=ji, jl=jl, tb3=tb3: v.tensor_copy(
                    out=v2r[:, ji * 128:ji * 128 + jl], in_=tb3[0:1, 0:jl]),
                  reads=[tr3], writes=[r_v2r])
            pfb, pfr = nbank()
            E("te", lambda t, tl=tl, pfb=pfb: t.matmul(
                out=pfb[0:8, 0:tl], lhsT=triu[:], rhs=eqT[:, 0:tl], start=True, stop=True),
              reads=[r_triu, r_eqT], writes=[pfr])
            E("dve", lambda v, tl=tl, pfb=pfb: v.tensor_scalar(
                out=lgF[:, 0:tl], in0=pfb[0:8, 0:tl], scalar1=0.0, scalar2=None, op0=ALU.is_equal),
              reads=[pfr], writes=[r_lgF])
            E("dve", lambda v, tl=tl: v.tensor_tensor(
                out=eqT[:, 0:tl], in0=eqT[:, 0:tl], in1=lgF[:, 0:tl], op=ALU.mult),
              reads=[r_eqT, r_lgF], writes=[r_eqT])
            vbb, vbr = nbank()
            E("te", lambda t, tl=tl, vbb=vbb: t.matmul(
                out=vbb[0:8, 0:tl], lhsT=onr[0:1, 0:8], rhs=v2r[:, 0:tl], start=True, stop=True),
              reads=[r_onr, r_v2r], writes=[vbr])
            E("dve", lambda v, t0=t0, tl=tl, vbb=vbb: v.tensor_tensor(
                out=m2all[:, t0:t0 + tl], in0=eqT[:, 0:tl], in1=vbb[0:8, 0:tl], op=ALU.mult),
              reads=[r_eqT, vbr], writes=[r_m2all])

        marks["p7"] = p.gidx
        if "m2" in tap_aps:
            E("sp", lambda s_: s_.dma_start(out=tap_aps["m2"][:, :], in_=m2all[:]),
              reads=[r_m2all], writes=[r_taps["m2"]], dma=True)
        for ti in range(5):
            Prog.barrier(r_SCF_m2b[ti], [r_SCF_xf])
        Prog.barrier(r_wgt[0], [r_proj])
        Prog.barrier(r_wgt[1], [r_proj])
        for e in range(8):
            routed = e < 7
            if routed:
                for ti, (t0, tl) in enumerate(TILES):
                    bk, rb = nbank()
                    E("te", lambda t, e=e, t0=t0, tl=tl, bk=bk: t.matmul(
                        out=bk[:, 0:tl], lhsT=sel8[:, e * 128:(e + 1) * 128],
                        rhs=m2all[:, t0:t0 + tl], start=True, stop=True),
                      reads=[r_sel8, r_m2all], writes=[rb])
                    E("act", lambda a, ti=ti, tl=tl, bk=bk: a.copy(
                        out=SCF[:, ti * 512:ti * 512 + tl], in_=bk[:, 0:tl]),
                      reads=[rb], writes=[r_SCF_m2b[ti]])
            for hb in range(8):
                wb = hb % 2
                if routed:
                    upv = ap["w_moe_up"][e, :, hb * HB:(hb + 1) * HB]
                    gtv = ap["w_moe_gate"][e, :, hb * HB:(hb + 1) * HB]
                    dnv = ap["w_moe_down"][e, hb * HB:(hb + 1) * HB, :]
                else:
                    upv = ap["w_sh_up"][:, hb * HB:(hb + 1) * HB]
                    gtv = ap["w_sh_gate"][:, hb * HB:(hb + 1) * HB]
                    dnv = ap["w_sh_down"][hb * HB:(hb + 1) * HB, :]
                E("pool", lambda g, upv=upv, wb=wb: g.dma_start(
                    out=wup[wb][:].rearrange("p (k h) -> p k h", k=8),
                    in_=upv.rearrange("(k p) h -> p k h", p=128)),
                  writes=[r_wup[wb]], dma=True, sem="swup%d" % wb)
                E("pool", lambda g, gtv=gtv, wb=wb: g.dma_start(
                    out=wgt[wb][:].rearrange("p (k h) -> p k h", k=8),
                    in_=gtv.rearrange("(k p) h -> p k h", p=128)),
                  writes=[r_wgt[wb]], dma=True, sem="swgt%d" % wb)
                E("pool", lambda g, dnv=dnv, wb=wb: g.dma_start(
                    out=wdn[wb][:].rearrange("p (c d) -> p c d", c=4),
                    in_=dnv.rearrange("(c p) d -> p c d", p=128)),
                  writes=[r_wdn[wb]], dma=True, sem="swdn%d" % wb)
                for ti, (t0, tl) in enumerate(TILES):
                    for m in range(4):
                        bU, rU = nbank()
                        bG, rG = nbank()
                        for k in range(DK):
                            E("te", lambda t, k=k, m=m, t0=t0, tl=tl, wb=wb, bU=bU: t.matmul(
                                out=bU[:, 0:tl], lhsT=wup[wb][:, k * HB + m * 128:k * HB + (m + 1) * 128],
                                rhs=XBA[:, HBF_OFF + k * PW + t0:HBF_OFF + k * PW + t0 + tl],
                                start=(k == 0), stop=(k == DK - 1)),
                              reads=[r_wup[wb], r_HBF], writes=[rU])
                        for k in range(DK):
                            E("te", lambda t, k=k, m=m, t0=t0, tl=tl, wb=wb, bG=bG: t.matmul(
                                out=bG[:, 0:tl], lhsT=wgt[wb][:, k * HB + m * 128:k * HB + (m + 1) * 128],
                                rhs=XBA[:, HBF_OFF + k * PW + t0:HBF_OFF + k * PW + t0 + tl],
                                start=(k == 0), stop=(k == DK - 1)),
                              reads=[r_wgt[wb], r_HBF], writes=[rG])
                        E("act", lambda a, tl=tl, bG=bG: a.activation(
                            out=sc1[:, 0:tl], in_=bG[:, 0:tl], func=AF.Sigmoid),
                          reads=[rG], writes=[r_sc1])
                        E("dve", lambda v, tl=tl, bG=bG: v.tensor_tensor(
                            out=sc0[:, 0:tl], in0=bG[:, 0:tl], in1=sc1[:, 0:tl], op=ALU.mult),
                          reads=[rG, r_sc1], writes=[r_sc0])
                        if routed:
                            E("dve", lambda v, tl=tl, bU=bU: v.tensor_tensor(
                                out=sc0[:, 0:tl], in0=bU[:, 0:tl], in1=sc0[:, 0:tl], op=ALU.mult),
                              reads=[rU, r_sc0], writes=[r_sc0])
                            E("dve", lambda v, m=m, ti=ti, tl=tl: v.tensor_tensor(
                                out=hid[:, m * 512:m * 512 + tl], in0=sc0[:, 0:tl],
                                in1=SCF[:, ti * 512:ti * 512 + tl], op=ALU.mult),
                              reads=[r_sc0, r_SCF_m2b[ti]], writes=[r_hid[m]])
                        else:
                            E("dve", lambda v, m=m, tl=tl, bU=bU: v.tensor_tensor(
                                out=hid[:, m * 512:m * 512 + tl], in0=bU[:, 0:tl],
                                in1=sc0[:, 0:tl], op=ALU.mult),
                              reads=[rU, r_sc0], writes=[r_hid[m]])
                    for mo in range(DK):
                        bk, rb = nbank()
                        for c in range(4):
                            E("te", lambda t, c=c, mo=mo, tl=tl, wb=wb, bk=bk: t.matmul(
                                out=bk[:, 0:tl], lhsT=wdn[wb][:, c * D + mo * 128:c * D + (mo + 1) * 128],
                                rhs=hid[:, c * 512:c * 512 + tl], start=(c == 0), stop=(c == 3)),
                              reads=[r_wdn[wb], r_hid[c]], writes=[rb])
                        E("dve", lambda v, mo=mo, t0=t0, tl=tl, bk=bk: v.tensor_tensor(
                            out=R2[:, mo * PW + t0:mo * PW + t0 + tl], in0=bk[:, 0:tl],
                            in1=R2[:, mo * PW + t0:mo * PW + t0 + tl], op=ALU.add),
                          reads=[rb, r_R2], writes=[r_R2])

        marks["p8"] = p.gidx
        tap_R2("racc")
        Prog.barrier(r_SCF_lnsq, r_SCF_m2b)
        subs = [(i * 128, 128) for i in range(16)] + [(2048, 32)]
        for (u0, ul) in subs:
            tb0, tr0 = nbank()
            tb1, tr1 = nbank()
            for mo in range(DK):
                tb, tr = (tb0, tr0) if mo < 4 else (tb1, tr1)
                E("te", lambda t, mo=mo, u0=u0, ul=ul, tb=tb: t.transpose(
                    out=tb[0:ul, (mo % 4) * 128:(mo % 4) * 128 + 128],
                    in_=R2[:, mo * PW + u0:mo * PW + u0 + ul], identity=idf[:]),
                  reads=[r_R2, r_idf], writes=[tr])
            E("dve", lambda v, ul=ul, tb0=tb0: v.tensor_copy(
                out=ttile[0:ul, 0:512], in_=tb0[0:ul, 0:512]),
              reads=[tr0], writes=[r_ttile])
            E("dve", lambda v, ul=ul, tb1=tb1: v.tensor_copy(
                out=ttile[0:ul, 512:1024], in_=tb1[0:ul, 0:512]),
              reads=[tr1], writes=[r_ttile])
            E("dve", lambda v, ul=ul: v.reduce_sum(
                out=mxp[0:ul, 0:1], in_=ttile[0:ul, :], axis=AX.X),
              reads=[r_ttile], writes=[r_mxp])
            E("dve", lambda v, ul=ul: v.tensor_scalar(
                out=mxp[0:ul, 1:2], in0=mxp[0:ul, 0:1], scalar1=1.0 / 1024.0, scalar2=None,
                op0=ALU.mult),
              reads=[r_mxp], writes=[r_mxp])
            E("act", lambda a, ul=ul: a.activation(
                out=SCF[0:ul, 0:1024], in_=ttile[0:ul, :], func=AF.Square),
              reads=[r_ttile], writes=[r_SCF_lnsq])
            E("dve", lambda v, ul=ul: v.reduce_sum(
                out=mxp[0:ul, 2:3], in_=SCF[0:ul, 0:1024], axis=AX.X),
              reads=[r_SCF_lnsq], writes=[r_mxp])
            E("act", lambda a, ul=ul: a.activation(
                out=mxp[0:ul, 3:4], in_=mxp[0:ul, 1:2], func=AF.Square),
              reads=[r_mxp], writes=[r_mxp])
            E("dve", lambda v, ul=ul: v.scalar_tensor_tensor(
                out=mxp[0:ul, 4:5], in0=mxp[0:ul, 2:3], scalar=1.0 / 1024.0,
                in1=mxp[0:ul, 3:4], op0=ALU.mult, op1=ALU.subtract),
              reads=[r_mxp], writes=[r_mxp])
            E("act", lambda a, ul=ul: a.activation(
                out=mxp[0:ul, 5:6], in_=mxp[0:ul, 4:5], func=AF.Sqrt,
                bias=vec[0:ul, _vc["eps5"]:_vc["eps5"] + 1], scale=1.0),
              reads=[r_mxp], writes=[r_mxp])
            E("dve", lambda v, ul=ul: v.reciprocal(out=mxp[0:ul, 6:7], in_=mxp[0:ul, 5:6]),
              reads=[r_mxp], writes=[r_mxp])
            E("dve", lambda v, ul=ul: v.tensor_scalar(
                out=ttile[0:ul, :], in0=ttile[0:ul, :], scalar1=mxp[0:ul, 1:2],
                scalar2=mxp[0:ul, 6:7], op0=ALU.subtract, op1=ALU.mult),
              reads=[r_ttile, r_mxp], writes=[r_ttile])
            E("sp", lambda s_, u0=u0, ul=ul: s_.dma_start(
                out=out_ap[u0:u0 + ul, :], in_=ttile[0:ul, :]),
              reads=[r_ttile], writes=[r_out], dma=True)

        marks["p9"] = p.gidx
        fin = [r_out] + list(r_taps.values())
        E("pool", lambda g: g.engine_nop(), reads=fin)

        semnames = set()
        for e in ENGINES:
            for op in p.ops[e]:
                semnames.add(op.sem)
        sems = {}
        for name in sorted(semnames):
            sems[name] = cm.enter_context(nc.semaphore("sem_" + name))
        block = cm.enter_context(nc.Block())
        p.finalize(block, sems, trunc=trunc)
    return nc, marks


def _toeplitz(rpb, w, m, h):
    idx = (np.arange(m)[:, None] - np.arange(m)[None, :]) + w - 1
    return rpb[idx, h]

def _core_blocks(hh, w, s):
    blocks = []
    if hh == 0:
        blocks.append((0, s, True))
        for o in range(s, 2048, w):
            blocks.append((o, w, True))
        pos = blocks[-1][0] + w
        while pos < EW:
            ln = min(w, EW - pos)
            blocks.append((pos, ln, False))
            pos += ln
    else:
        first = s + ((28 - s) // w) * w
        pos = 0
        while pos < first:
            ln = min(w, first - pos)
            blocks.append((pos, ln, False))
            pos += ln
        o = first
        while o + w <= EW - 4:
            blocks.append((o, w, True))
            o += w
        blocks.append((o, (EW - 4) - o, True))
        blocks.append((EW - 4, 4, False))
    return blocks

def _bias_for_core(hh, rpb1, rpb2):
    out = np.zeros((NSLOT, 128, BIAS_W), np.float32)
    for j, (w, s) in enumerate(BR):
        rpb = rpb1 if j == 0 else rpb2
        blocks = _core_blocks(hh, w, s)
        for g, (g0, gl) in enumerate(GROUPS[j]):
            sl = _slot(j, g)
            for h in range(HEADS):
                th = np.full((gl, gl), -100.0, np.float32)
                for (b0, bl, real) in blocks:
                    if b0 >= g0 + gl or b0 + bl <= g0:
                        continue
                    assert b0 >= g0 and b0 + bl <= g0 + gl, (j, g, b0, bl, g0, gl)
                    lo = b0 - g0
                    if real:
                        th[lo:lo + bl, lo:lo + bl] = _toeplitz(rpb, w, bl, h)
                    else:
                        th[lo:lo + bl, lo:lo + bl] = 0.0
                out[sl, 0:gl, 120 * h:120 * h + gl] = th
    return out

def _vecs_host(i):
    v = np.zeros((NV, 128), np.float32)
    def put(name, arr):
        a = np.asarray(arr, np.float32).reshape(-1, 128)
        v[_vc[name]:_vc[name] + a.shape[0], :] = a
    put("down_b", i["L0_attn_down_b"])
    put("up_b", i["L0_attn_up_b"])
    put("norm1_w", i["L0_attn_norm1_w"])
    put("norm2_w", i["L0_attn_norm2_w"])
    put("ffn_b1", i["L0_attn_ffn_b1"])
    put("ffn_b2", i["L0_attn_ffn_b2"])
    put("proj_b", np.concatenate([np.asarray(i["L0_attn_proj1_b"]),
                                  np.asarray(i["L0_attn_proj2_b"])]))
    qb = np.concatenate([np.asarray(i["L0_attn_qkv1_b"]), np.asarray(i["L0_attn_qkv2_b"])])
    v[_vc["qkv_b"]:_vc["qkv_b"] + 24, 0:64] = qb.reshape(24, 64)
    v[_vc["eps6"], :] = 1e-6
    v[_vc["eps5"], :] = 1e-5
    return v

_NC_CACHE = {}

def _in_maps(i, taps=()):
    x = i["L0_attn_x"]
    projp = np.stack([
        np.ascontiguousarray(np.asarray(i["L0_attn_proj1_w"], np.float32)),
        np.ascontiguousarray(np.asarray(i["L0_attn_proj2_w"], np.float32))])
    shared = dict(
        w_down=np.ascontiguousarray(np.asarray(i["L0_attn_down_w"], np.float32)),
        w_up=np.ascontiguousarray(np.asarray(i["L0_attn_up_w"], np.float32)),
        w_qkv1=np.ascontiguousarray(np.asarray(i["L0_attn_qkv1_w"], np.float32)),
        w_qkv2=np.ascontiguousarray(np.asarray(i["L0_attn_qkv2_w"], np.float32)),
        w_projp=projp,
        w_ffn1=np.ascontiguousarray(np.asarray(i["L0_attn_ffn_w1"], np.float32)),
        w_ffn2=np.ascontiguousarray(np.asarray(i["L0_attn_ffn_w2"], np.float32)),
        w_router=np.ascontiguousarray(np.asarray(i["L0_ffn_router_w"], np.float32)),
        w_moe_up=np.ascontiguousarray(np.asarray(i["L0_ffn_up_w"], np.float32)),
        w_moe_gate=np.ascontiguousarray(np.asarray(i["L0_ffn_gate_w"], np.float32)),
        w_moe_down=np.ascontiguousarray(np.asarray(i["L0_ffn_down_w"], np.float32)),
        w_sh_up=np.ascontiguousarray(np.asarray(i["L0_ffn_shared_up_w"], np.float32)),
        w_sh_gate=np.ascontiguousarray(np.asarray(i["L0_ffn_shared_gate_w"], np.float32)),
        w_sh_down=np.ascontiguousarray(np.asarray(i["L0_ffn_shared_down_w"], np.float32)),
        ident_f=np.eye(128, dtype=np.float32),
        ident_b=np.eye(128, dtype=ml_dtypes.bfloat16),
        ident2=np.ascontiguousarray(np.vstack([np.eye(32), np.eye(32)]).astype(ml_dtypes.bfloat16)),
        ones_r=np.ones((1, 128), np.float32),
        ones_c=np.ones((128, 1), ml_dtypes.bfloat16),
        tri_u=np.ascontiguousarray(np.triu(np.ones((8, 8), np.float32), 1)),
        sel8=np.ascontiguousarray(
            np.stack([np.tile((np.arange(8) == e).astype(np.float32)[:, None], (1, 128))
                      for e in range(N_EXP)], axis=1).reshape(8, N_EXP * 128)),
        vecs=_vecs_host(i),
    )
    bias_h = [_bias_for_core(hh, np.asarray(i["L0_attn_rpb1"], np.float32),
                             np.asarray(i["L0_attn_rpb2"], np.float32)) for hh in (0, 1)]
    maps = []
    for c in range(8):
        b, hh = divmod(c, 2)
        if hh == 0:
            xs = np.asarray(x[b, 0:2080], np.float32)
        else:
            xs = np.concatenate([np.asarray(x[b, 2020:4096], np.float32),
                                 np.zeros((4, D), np.float32)], axis=0)
        m = dict(shared)
        m["xT"] = np.ascontiguousarray(xs.T)
        m["bias6"] = bias_h[hh]
        maps.append(m)
    return maps

def kernel(**inputs):
    i = {k: np.asarray(v) for k, v in inputs.items()}
    if "main" not in _NC_CACHE:
        _NC_CACHE["main"] = build_program()[0]
    nc = _NC_CACHE["main"]
    res = run_bass_kernel_spmd(nc, _in_maps(i), core_ids=list(range(8)))
    out = np.empty((4, 4096, D), np.float32)
    for c in range(8):
        b, hh = divmod(c, 2)
        o = np.asarray(res.results[c]["out"])
        if hh == 0:
            out[b, 0:2048] = o[0:2048]
        else:
            out[b, 2048:4096] = o[28:28 + 2048]
    return out


# version 05
# speedup vs baseline: 2.3112x; 1.8166x over previous
import contextlib
import numpy as np
import ml_dtypes
import concourse.bass as bass
import concourse.mybir as mybir
from concourse.bass_utils import run_bass_kernel_spmd

F32 = mybir.dt.float32
BF16 = mybir.dt.bfloat16
AF = mybir.ActivationFunctionType
ALU = mybir.AluOpType
AX = mybir.AxisListType

D = 1024
DK = 8
DIN = 512
DBR = 256
HEADS = 8
HD = 32
PW = 2080
EW = 2084
QK_SCALE = float(HD) ** -0.5
BR = [(10, 5), (20, 10)]
H_MOE = 4096
HB = 512
N_EXP = 7
FFN_H = 2048
CAP = 512
BIGIDX = 1.0e7

TILES = [(0, 512), (512, 512), (1024, 512), (1536, 512), (2048, 32)]

def _groups(w, s):
    lead = s + ((120 - s) // w) * w
    gs = [(0, lead)]
    pos = lead
    while pos + 120 <= EW - 30:
        gs.append((pos, 120))
        pos += 120
    gs.append((pos, EW - pos))
    return gs

GROUPS = {j: _groups(*BR[j]) for j in (0, 1)}

def _slot(j, g):
    return 3 * j + (0 if g == 0 else (2 if g == len(GROUPS[j]) - 1 else 1))

NSLOT = 6
BIAS_W = 8 * 120

_vc = {}
_vn = 0
def _vcol(name, n):
    global _vn
    _vc[name] = _vn
    _vn += n
for _nm, _n in [("down_b", 4), ("up_b", 8), ("norm1_w", 4), ("norm2_w", 8),
                ("ffn_b1", 16), ("ffn_b2", 8), ("proj_b", 4),
                ("qkv_b", 24), ("eps6", 1), ("eps5", 1), ("ecap", 1), ("one8", 1)]:
    _vcol(_nm, _n)
NV = _vn

ENGINES = ("te", "act", "dve", "sp", "pool")

class Reg:
    __slots__ = ("w", "r")
    def __init__(self):
        self.w = None
        self.r = []

class _Op:
    __slots__ = ("fn", "inc", "waits", "needed", "sem", "gidx")
    def __init__(self, fn, inc, waits, sem, gidx):
        self.fn = fn; self.inc = inc; self.waits = waits
        self.needed = False; self.sem = sem; self.gidx = gidx

class Prog:
    def __init__(self):
        self.ops = {e: [] for e in ENGINES}
        self.wm = {e: {} for e in ENGINES}
        self.gidx = 0

    def emit(self, eng, fn, reads=(), writes=(), dma=False, boot=False, sem=None):
        deps = {}
        def add(dep):
            deng, didx = dep
            if deng == eng == "te":
                return
            if deng == eng and dma and self.ops[deng][didx].inc == 16:
                return
            key = (deng, self.ops[deng][didx].sem)
            cur = deps.get(key)
            if cur is None or cur[1] < didx:
                deps[key] = dep
        for r in reads:
            if r.w is not None:
                add(r.w)
        for wreg in writes:
            if wreg.w is not None:
                add(wreg.w)
            for rd in wreg.r:
                add(rd)
        waits = []
        wm = self.wm[eng]
        for (deng, didx) in deps.values():
            key = (deng, self.ops[deng][didx].sem)
            if wm.get(key, -1) < didx:
                waits.append((deng, didx))
                wm[key] = didx
        if sem is None:
            sem = "boot" if boot else ("poolc" if (eng == "pool" and not dma) else eng)
        op = _Op(fn, 16 if dma else 1, waits, sem, self.gidx)
        self.gidx += 1
        self.ops[eng].append(op)
        me = (eng, len(self.ops[eng]) - 1)
        for wreg in writes:
            wreg.w = me
            wreg.r = []
        for r in reads:
            r.r.append(me)
        return me

    @staticmethod
    def barrier(newreg, olds):
        for o in olds:
            if o.w is not None:
                newreg.r.append(o.w)
            newreg.r.extend(o.r)

    def finalize(self, block, sems, trunc=None):
        if trunc is not None:
            for e in ENGINES:
                self.ops[e] = [op for op in self.ops[e] if op.gidx < trunc]
        for e in ENGINES:
            for op in self.ops[e]:
                if op.sem == "boot" or op.inc == 16:
                    op.needed = True
                for (deng, didx) in op.waits:
                    self.ops[deng][didx].needed = True
        val = {}
        cum = {}
        for e in ENGINES:
            v = []
            for op in self.ops[e]:
                if op.needed:
                    cum[op.sem] = cum.get(op.sem, 0) + op.inc
                v.append(cum.get(op.sem, 0))
            val[e] = v
        boot_total = cum.get("boot", 0)
        def body(e):
            mysems = {}
            for op in self.ops[e]:
                if op.needed and op.inc == 16:
                    mysems[op.sem] = cum.get(op.sem, 0)
            def run(eng):
                for op in self.ops[e]:
                    for (deng, didx) in op.waits:
                        t = self.ops[deng][didx]
                        if t.sem == "boot":
                            eng.wait_ge(sems["boot"], boot_total)
                        else:
                            eng.wait_ge(sems[t.sem], val[deng][didx])
                    inst = op.fn(eng)
                    if op.needed:
                        inst.then_inc(sems[op.sem], op.inc)
                for s, tot in sorted(mysems.items()):
                    eng.wait_ge(sems[s], tot)
            return run
        if self.ops["te"]:
            block.tensor(body("te"))
        if self.ops["act"]:
            block.scalar(body("act"))
        if self.ops["dve"]:
            block.vector(body("dve"))
        if self.ops["sp"]:
            block.sync(body("sp"))
        if self.ops["pool"]:
            block.gpsimd(body("pool"))


def build_program(taps=(), trunc=None):
    nc = bass.Bass("TRN2", target_bir_lowering=False)
    ap = {}
    def din(name, shape, dt=F32):
        ap[name] = nc.dram_tensor(name, list(shape), dt, kind="ExternalInput").ap()
        return ap[name]
    din("xT", [D, PW])
    din("bias6", [NSLOT, 128, BIAS_W])
    din("w_down", [D, DIN]); din("w_up", [DIN, D])
    din("w_qkv1", [DBR, 3 * DBR]); din("w_qkv2", [DBR, 3 * DBR])
    din("w_projp", [2, HEADS * HD, DBR])
    din("w_ffn1", [D, FFN_H]); din("w_ffn2", [FFN_H, D])
    din("w_router", [D, 8])
    din("w_moe_up", [N_EXP, D, H_MOE]); din("w_moe_gate", [N_EXP, D, H_MOE])
    din("w_moe_down", [N_EXP, H_MOE, D])
    din("w_sh_up", [D, H_MOE]); din("w_sh_gate", [D, H_MOE]); din("w_sh_down", [H_MOE, D])
    din("vecs", [NV, 128])
    y1c_ap = nc.dram_tensor("y1c_scr", [8 * CAP, D], BF16, kind="Internal").ap()
    rout_ap = nc.dram_tensor("rout_scr", [8 * CAP, D], BF16, kind="Internal").ap()
    din("ident_f", [128, 128]); din("ident_b", [128, 128], BF16)
    din("ones_r", [1, 128]); din("ones_c", [128, 1], BF16)
    din("ident2", [64, 32], BF16)
    din("tri_u", [8, 8])
    din("sel8", [8, N_EXP * 128])
    out_ap = nc.dram_tensor("out", [PW, D], F32, kind="ExternalOutput").ap()
    tap_aps = {t: nc.dram_tensor("tap_" + t, list(s), F32, kind="ExternalOutput").ap()
               for (t, s) in taps}

    p = Prog()
    cm = contextlib.ExitStack()
    with cm:
        sb = lambda name, shape, dt=F32: cm.enter_context(nc.sbuf_tensor(name, shape, dt))
        R2 = sb("R2", [128, DK * PW])
        XBA = sb("XBA", [128, 2 * 4 * EW], BF16)
        XIB_OFF, O12_OFF, HBF_OFF = 0, 4 * EW, 0
        SCF = sb("SCF", [128, 2080])
        wup = [sb(f"wup{i}", [128, 8 * HB], BF16) for i in range(2)]
        wgt = [sb(f"wgt{i}", [128, 8 * HB], BF16) for i in range(2)]
        wdn = [sb(f"wdn{i}", [128, 4 * D], BF16) for i in range(2)]

        hid = sb("hid", [128, 4 * 512], BF16)
        rW = sb("rW", [128, 8 * 8], BF16)
        vec = sb("vec", [128, NV])
        idf = sb("idf", [128, 128]); idb = sb("idb", [128, 128], BF16)
        onr = sb("onr", [1, 128]); onc = sb("onc", [128, 1], BF16)
        id2 = sb("id2_sb", [64, 32], BF16)
        triu = sb("triu", [8, 8])
        sel8 = sb("sel8_sb", [8, N_EXP * 128])
        qp = [sb(f"qp{i}", [64, 128], BF16) for i in range(4)]
        kp = [sb(f"kp{i}", [64, 128], BF16) for i in range(4)]
        vp = [sb(f"vp{i}", [64, 128], BF16) for i in range(4)]
        oh = [sb(f"oh{i}", [32, 128], BF16) for i in range(8)]
        sct = [sb("sct0", [128, 120]), sb("sct1", [128, 120])]
        et = [sb("et0", [128, 120]), sb("et1", [128, 120])]
        atb = [sb("atb0", [128, 120], BF16), sb("atb1", [128, 120], BF16)]
        atT = [sb("atT0", [128, 120], BF16), sb("atT1", [128, 120], BF16)]
        vT = [sb("vT0", [128, 32], BF16), sb("vT1", [128, 32], BF16)]
        nmax = [sb("nmax0", [128, 4]), sb("nmax1", [128, 4])]
        sc0 = sb("sc0", [128, 512]); sc1 = sb("sc1", [128, 512])
        sbf = [sb("sbf0", [128, 512], BF16), sb("sbf1", [128, 512], BF16)]
        rowst = sb("rowst", [1, 1024])
        m2all = sb("m2all", [8, PW])
        lgF = sb("lgF", [8, 512])
        ltm = sb("ltm", [128, 32])
        eqtm = sb("eqtm", [128, 32])
        mxp = sb("mxp", [128, 8])
        eqT = sb("eqT", [8, 512])
        v2r = sb("v2r", [1, 512])
        routf = sb("routf", [128, 8 * 512], BF16)
        ytm = sb("ytm", [128, 1024], BF16)
        m2tt = sb("m2tt", [128, 17])
        sltF = sb("sltF", [128, 17])
        sltI = sb("sltI", [128, 17], mybir.dt.int32)
        pb = [cm.enter_context(nc.psum_tensor(f"pb{i}", [128, 512], F32)) for i in range(6)]
        pT = [cm.enter_context(nc.psum_tensor(f"pT{i}", [128, 1024], BF16)) for i in range(2)]

        RG = Reg
        r_xT = RG()
        r_R2 = RG(); r_XIB = RG(); r_O12 = RG(); r_HBF = RG()
        r_SCF_xipre = RG(); r_SCF_xf = RG(); r_SCF_m2b = [RG() for _ in range(5)]
        r_SCF_lnsq = RG()
        r_wup = [RG(), RG()]; r_wgt = [RG(), RG()]; r_wdn = [RG(), RG()]
        r_xbf = RG(); r_hid = [RG() for _ in range(4)]; r_rW = RG(); r_vec = RG()
        r_idf = RG(); r_idb = RG(); r_id2 = RG(); r_onr = RG(); r_onc = RG(); r_triu = RG(); r_sel8 = RG()
        r_qp = [RG() for _ in range(4)]; r_kp = [RG() for _ in range(4)]; r_vp = [RG() for _ in range(4)]
        r_oh = [RG() for _ in range(8)]
        r_sct = [RG(), RG()]; r_et = [RG(), RG()]; r_atb = [RG(), RG()]
        r_atT = [RG(), RG()]; r_vT = [RG(), RG()]; r_nmax = [RG(), RG()]
        r_sc0 = RG(); r_sc1 = RG(); r_sbf = [RG(), RG()]
        r_rowst = RG(); r_m2all = RG(); r_lgF = RG(); r_ltm = RG(); r_eqtm = RG()
        r_mxp = RG(); r_eqT = RG(); r_v2r = RG(); r_ttile = RG()
        r_routf = RG(); r_ytm = RG(); r_m2tt = RG(); r_sltF = RG(); r_sltI = RG()
        r_y1g = RG(); r_y1gf = RG(); r_y1c = RG(); r_rout = RG(); r_posI = RG()
        r_pb = [RG() for _ in range(6)]; r_pT = [RG(), RG()]
        r_bias = RG()
        r_proj = RG()
        r_out = RG()
        r_taps = {t: RG() for t in tap_aps}

        bank_i = [0]
        def nbank():
            b = bank_i[0] % 6
            bank_i[0] += 1
            return pb[b], r_pb[b]

        E = p.emit

        for vi in range(NV):
            E("pool", lambda g, vi=vi: g.dma_start(
                out=vec[:, vi:vi + 1],
                in_=ap["vecs"][vi].rearrange("(p o) -> p o", o=1)),
              writes=[r_vec], dma=True, boot=True)
        E("pool", lambda g: g.dma_start(out=idf[:], in_=ap["ident_f"]), writes=[r_idf], dma=True, boot=True)
        E("pool", lambda g: g.dma_start(out=idb[:], in_=ap["ident_b"]), writes=[r_idb], dma=True, boot=True)
        E("pool", lambda g: g.dma_start(out=onr[:], in_=ap["ones_r"]), writes=[r_onr], dma=True, boot=True)
        E("pool", lambda g: g.dma_start(out=onc[:], in_=ap["ones_c"]), writes=[r_onc], dma=True, boot=True)
        E("pool", lambda g: g.dma_start(out=id2[:], in_=ap["ident2"]), writes=[r_id2], dma=True, boot=True)
        E("pool", lambda g: g.dma_start(out=triu[:], in_=ap["tri_u"]), writes=[r_triu], dma=True, boot=True)
        E("pool", lambda g: g.dma_start(out=sel8[:], in_=ap["sel8"]), writes=[r_sel8], dma=True, boot=True)

        def bias_dst(sl):
            if sl < 4:
                return wup[1][:, 960 * sl:960 * (sl + 1)]
            return wdn[1][:, 960 * (sl - 4):960 * (sl - 3)]
        for sl in range(NSLOT):
            E("pool", lambda g, sl=sl: g.dma_start(out=bias_dst(sl), in_=ap["bias6"][sl]),
              writes=[r_bias], dma=True, boot=True)
        E("pool", lambda g: g.dma_start(out=wup[0][:].rearrange("p (k m) -> p k m", k=8),
                                        in_=ap["w_down"].rearrange("(k p) m -> p k m", p=128)),
          writes=[r_wup[0]], dma=True, boot=True)
        E("pool", lambda g: g.dma_start(out=wdn[0][:].rearrange("p (k m) -> p k m", k=4),
                                        in_=ap["w_up"].rearrange("(k p) m -> p k m", p=128)),
          writes=[r_wdn[0]], dma=True, boot=True)
        for j in range(2):
            E("pool", lambda g, j=j: g.dma_start(
                out=wgt[j][:, 0:1536].rearrange("p (k m) -> p k m", k=2),
                in_=ap["w_qkv" + str(j + 1)].rearrange("(k p) m -> p k m", p=128)),
              writes=[r_wgt[j]], dma=True, boot=True)
            E("pool", lambda g, j=j: g.dma_start(
                out=wgt[j][0:32, 1536:3584].rearrange("p (h m) -> p h m", h=8),
                in_=ap["w_projp"][j].rearrange("(h p) m -> p h m", p=32)),
              writes=[r_proj], dma=True, boot=True)
        E("pool", lambda g: g.dma_start(out=rW[:].rearrange("p (k e) -> p k e", k=8),
                                        in_=ap["w_router"].rearrange("(k p) e -> p k e", p=128)),
          writes=[r_rW], dma=True, boot=True)

        vslice = lambda name, i: vec[:, _vc[name] + i:_vc[name] + i + 1]
        v64 = lambda i: vec[0:64, _vc["qkv_b"] + i:_vc["qkv_b"] + i + 1]

        xT_v = ap["xT"].rearrange("(k p) t -> p k t", p=128)
        marks = {"boot": p.gidx}

        def tap_R2(t):
            if t in tap_aps:
                for mo in range(DK):
                    E("sp", lambda s_, mo=mo, t=t: s_.dma_start(
                        out=tap_aps[t][mo * 128:(mo + 1) * 128, :], in_=R2[:, mo * PW:(mo + 1) * PW]),
                      reads=[r_R2], writes=[r_taps[t]], dma=True)

        def tap_XBA(t, off, rr, nch, width):
            if t in tap_aps:
                for mo in range(nch):
                    for seg in range(0, width, 512):
                        w = min(512, width - seg)
                        E("act", lambda a, mo=mo, off=off, seg=seg, w=w: a.copy(
                            out=sc0[:, 0:w], in_=XBA[:, off + mo * width + seg:off + mo * width + seg + w]),
                          reads=[rr], writes=[r_sc0])
                        E("sp", lambda s_, mo=mo, seg=seg, w=w, t=t: s_.dma_start(
                            out=tap_aps[t][mo * 128:(mo + 1) * 128, seg:seg + w], in_=sc0[:, 0:w]),
                          reads=[r_sc0], writes=[r_taps[t]], dma=True)

        for (t0, tl) in TILES:
            dbanks = [nbank() for _ in range(4)]
            for kh in range(2):
                E("pool", lambda g, kh=kh, t0=t0, tl=tl: g.dma_start(
                    out=XBA[:, O12_OFF:O12_OFF + 4 * tl].rearrange("p (k t) -> p k t", k=4),
                    in_=xT_v[:, 4 * kh:4 * kh + 4, t0:t0 + tl]),
                  reads=[r_xT], writes=[r_xbf], dma=True, sem="sxbf")
                for mo in range(4):
                    bk, rb = dbanks[mo]
                    for k4 in range(4):
                        k = 4 * kh + k4
                        E("te", lambda t, k=k, k4=k4, mo=mo, tl=tl, bk=bk: t.matmul(
                            out=bk[:, 0:tl],
                            lhsT=wup[0][:, (k * 4 + mo) * 128:(k * 4 + mo + 1) * 128],
                            rhs=XBA[:, O12_OFF + k4 * tl:O12_OFF + (k4 + 1) * tl],
                            start=(k == 0), stop=(k == 7)),
                          reads=[r_wup[0], r_xbf], writes=[rb])
            sqb, sqr = nbank()
            for mo in range(4):
                bk, rb = dbanks[mo]
                E("act", lambda a, mo=mo, tl=tl, bk=bk: a.add(
                    out=SCF[:, mo * 512:mo * 512 + tl], in_=bk[:, 0:tl],
                    add=vslice("down_b", mo)),
                  reads=[rb, r_vec], writes=[r_SCF_xipre])
                E("act", lambda a, mo=mo, tl=tl, bk=bk: a.activation(
                    out=sbf[mo % 2][:, 0:tl], in_=bk[:, 0:tl], func=AF.Square,
                    bias=vslice("down_b", mo), scale=1.0),
                  reads=[rb, r_vec], writes=[r_sbf[mo % 2]])
                E("te", lambda t, mo=mo, tl=tl, sqb=sqb: t.matmul(
                    out=sqb[0:1, 0:tl], lhsT=onc[:], rhs=sbf[mo % 2][:, 0:tl],
                    start=(mo == 0), stop=(mo == 3)),
                  reads=[r_sbf[mo % 2], r_onc], writes=[sqr])
            E("act", lambda a, sqb=sqb, tl=tl: a.activation(
                out=rowst[:, 0:tl], in_=sqb[0:1, 0:tl], func=AF.Sqrt,
                bias=vec[0:1, _vc["eps6"]:_vc["eps6"] + 1], scale=1.0 / 512.0),
              reads=[sqr], writes=[r_rowst])
            E("dve", lambda v, tl=tl: v.reciprocal(out=rowst[:, 512:512 + tl], in_=rowst[:, 0:tl]),
              reads=[r_rowst], writes=[r_rowst])
            bb, rbb = nbank()
            E("te", lambda t, tl=tl, bb=bb: t.matmul(
                out=bb[:, 0:tl], lhsT=onr[:], rhs=rowst[:, 512:512 + tl], start=True, stop=True),
              reads=[r_rowst, r_onr], writes=[rbb])
            for mo in range(4):
                E("dve", lambda v, mo=mo, tl=tl, bb=bb: v.tensor_tensor(
                    out=sc0[:, 0:tl], in0=SCF[:, mo * 512:mo * 512 + tl],
                    in1=bb[:, 0:tl], op=ALU.mult),
                  reads=[r_SCF_xipre, rbb], writes=[r_sc0])
                E("dve", lambda v, mo=mo, t0=t0, tl=tl: v.tensor_scalar(
                    out=XBA[:, XIB_OFF + mo * EW + t0:XIB_OFF + mo * EW + t0 + tl],
                    in0=sc0[:, 0:tl], scalar1=vslice("norm1_w", mo), scalar2=None, op0=ALU.mult),
                  reads=[r_sc0, r_vec], writes=[r_XIB])
        for mo in range(4):
            E("pool", lambda g, mo=mo: g.memset(
                XBA[:, XIB_OFF + mo * EW + 2076:XIB_OFF + mo * EW + 2084], 0.0),
              writes=[r_XIB])

        marks["p1"] = p.gidx
        tap_XBA("xi", XIB_OFF, r_XIB, 4, EW)
        Prog.barrier(r_O12, [r_xbf])
        for j in (0, 1):
            for g, (g0, gl) in enumerate(GROUPS[j]):
                sl = _slot(j, g)
                for mc in range(6):
                    bk, rb = nbank()
                    for k in range(2):
                        kc = 2 * j + k
                        E("te", lambda t, mc=mc, k=k, kc=kc, g0=g0, gl=gl, j=j, bk=bk: t.matmul(
                            out=bk[:, 0:gl],
                            lhsT=wgt[j][:, (k * 6 + mc) * 128:(k * 6 + mc + 1) * 128],
                            rhs=XBA[:, XIB_OFF + kc * EW + g0:XIB_OFF + kc * EW + g0 + gl],
                            start=(k == 0), stop=(k == 1)),
                          reads=[r_wgt[j], r_XIB], writes=[rb])
                    kind, sub = divmod(mc, 2)
                    tgt = (qp, kp, vp)[kind]
                    rtg = (r_qp, r_kp, r_vp)[kind]
                    for half in range(2):
                        pi = sub * 2 + half
                        E("dve", lambda v, half=half, pi=pi, gl=gl, j=j, mc=mc, tgt=tgt, bk=bk: v.tensor_scalar(
                            out=tgt[pi][:, 0:gl], in0=bk[64 * half:64 * (half + 1), 0:gl],
                            scalar1=v64(j * 12 + mc * 2 + half), scalar2=None, op0=ALU.add),
                          reads=[rb, r_vec], writes=[rtg[pi]])
                sbanks = []
                def emit_scores(h):
                    pi, su = divmod(h, 2)
                    bk, rb = nbank()
                    E("te", lambda t, pi=pi, su=su, gl=gl, bk=bk: t.matmul(
                        out=bk[0:gl, 0:gl], lhsT=qp[pi][32 * su:32 * (su + 1), 0:gl],
                        rhs=kp[pi][32 * su:32 * (su + 1), 0:gl], start=True, stop=True),
                      reads=[r_qp[pi], r_kp[pi]], writes=[rb])
                    sbanks.append((bk, rb))
                for h in range(4):
                    emit_scores(h)
                for h in range(HEADS):
                    pi, su = divmod(h, 2)
                    pa = h % 2
                    bk, rb = sbanks[h]
                    E("dve", lambda v, gl=gl, sl=sl, h=h, bk=bk, pa=pa: v.scalar_tensor_tensor(
                        out=sct[pa][0:gl, 0:gl], in0=bk[0:gl, 0:gl], scalar=QK_SCALE,
                        in1=bias_dst(sl)[0:gl, 120 * h:120 * h + gl],
                        op0=ALU.mult, op1=ALU.add),
                      reads=[rb, r_bias], writes=[r_sct[pa]])
                    E("dve", lambda v, gl=gl, pa=pa: v.reduce_max(
                        out=nmax[pa][0:gl, 0:1], in_=sct[pa][0:gl, 0:gl], axis=AX.X, negate=True),
                      reads=[r_sct[pa]], writes=[r_nmax[pa]])
                    E("act", lambda a, gl=gl, pa=pa: a.activation(
                        out=et[pa][0:gl, 0:gl], in_=sct[pa][0:gl, 0:gl], func=AF.Exp,
                        bias=nmax[pa][0:gl, 0:1], scale=1.0),
                      reads=[r_sct[pa], r_nmax[pa]], writes=[r_et[pa]])
                    E("dve", lambda v, gl=gl, pa=pa: v.reduce_sum(
                        out=nmax[pa][0:gl, 1:2], in_=et[pa][0:gl, 0:gl], axis=AX.X),
                      reads=[r_et[pa]], writes=[r_nmax[pa]])
                    E("dve", lambda v, gl=gl, pa=pa: v.reciprocal(
                        out=nmax[pa][0:gl, 2:3], in_=nmax[pa][0:gl, 1:2]),
                      reads=[r_nmax[pa]], writes=[r_nmax[pa]])
                    E("dve", lambda v, gl=gl, pa=pa: v.tensor_scalar(
                        out=atb[pa][0:gl, 0:gl], in0=et[pa][0:gl, 0:gl],
                        scalar1=nmax[pa][0:gl, 2:3], scalar2=None, op0=ALU.mult),
                      reads=[r_et[pa], r_nmax[pa]], writes=[r_atb[pa]])
                    E("te", lambda t, gl=gl, pa=pa: t.transpose(
                        out=pT[pa][0:gl, 0:gl], in_=atb[pa][0:gl, 0:gl], identity=idb[0:gl, 0:gl]),
                      reads=[r_atb[pa], r_idb], writes=[r_pT[pa]])
                    E("te", lambda t, pi=pi, su=su, gl=gl, pa=pa: t.transpose(
                        out=pT[pa][0:gl, 512:544], in_=vp[pi][32 * su:32 * (su + 1), 0:gl],
                        identity=id2[32 * su:32 * (su + 1), 0:32]),
                      reads=[r_vp[pi], r_id2], writes=[r_pT[pa]])
                    E("dve", lambda v, gl=gl, pa=pa: v.tensor_copy(
                        out=atT[pa][0:gl, 0:gl], in_=pT[pa][0:gl, 0:gl]),
                      reads=[r_pT[pa]], writes=[r_atT[pa]])
                    E("dve", lambda v, gl=gl, pa=pa: v.tensor_copy(
                        out=vT[pa][0:gl, 0:32], in_=pT[pa][0:gl, 512:544]),
                      reads=[r_pT[pa]], writes=[r_vT[pa]])
                    bk2, rb2 = nbank()
                    E("te", lambda t, gl=gl, bk2=bk2, pa=pa: t.matmul(
                        out=bk2[0:32, 0:gl], lhsT=vT[pa][0:gl, 0:32], rhs=atT[pa][0:gl, 0:gl],
                        start=True, stop=True),
                      reads=[r_vT[pa], r_atT[pa]], writes=[rb2])
                    E("dve", lambda v, h=h, gl=gl, bk2=bk2: v.tensor_copy(
                        out=oh[h][:, 0:gl], in_=bk2[0:32, 0:gl]),
                      reads=[rb2], writes=[r_oh[h]])
                    if h + 4 < HEADS:
                        emit_scores(h + 4)
                for mo in range(2):
                    bk, rb = nbank()
                    for h in range(HEADS):
                        E("te", lambda t, h=h, mo=mo, gl=gl, j=j, bk=bk: t.matmul(
                            out=bk[:, 0:gl],
                            lhsT=wgt[j][0:32, 1536 + h * 256 + mo * 128:1536 + h * 256 + (mo + 1) * 128],
                            rhs=oh[h][:, 0:gl], start=(h == 0), stop=(h == 7)),
                          reads=[r_proj, r_oh[h]], writes=[rb])
                    oc = 2 * j + mo
                    E("dve", lambda v, oc=oc, g0=g0, gl=gl, bk=bk: v.tensor_scalar(
                        out=XBA[:, O12_OFF + oc * EW + g0:O12_OFF + oc * EW + g0 + gl],
                        in0=bk[:, 0:gl], scalar1=vslice("proj_b", oc), scalar2=None, op0=ALU.add),
                      reads=[rb, r_vec], writes=[r_O12])

        marks["p2"] = p.gidx
        tap_XBA("o12", O12_OFF, r_O12, 4, EW)
        Prog.barrier(r_SCF_xf, [r_SCF_xipre])
        for (t0, tl) in TILES:
            for kh in range(2):
                E("sp", lambda s_, kh=kh, t0=t0, tl=tl: s_.dma_start(
                    out=SCF[:, 0:4 * tl].rearrange("p (k t) -> p k t", k=4),
                    in_=xT_v[:, 4 * kh:4 * kh + 4, t0:t0 + tl]),
                  reads=[r_xT], writes=[r_SCF_xf], dma=True)
                for mo4 in range(4):
                    mo = 4 * kh + mo4
                    bk, rb = nbank()
                    for k in range(4):
                        E("te", lambda t, k=k, mo=mo, t0=t0, tl=tl, bk=bk: t.matmul(
                            out=bk[:, 0:tl], lhsT=wdn[0][:, (k * 8 + mo) * 128:(k * 8 + mo + 1) * 128],
                            rhs=XBA[:, O12_OFF + k * EW + t0:O12_OFF + k * EW + t0 + tl],
                            start=(k == 0), stop=(k == 3)),
                          reads=[r_wdn[0], r_O12], writes=[rb])
                    E("dve", lambda v, mo=mo, mo4=mo4, t0=t0, tl=tl, bk=bk: v.scalar_tensor_tensor(
                        out=R2[:, mo * PW + t0:mo * PW + t0 + tl], in0=bk[:, 0:tl],
                        scalar=vslice("up_b", mo), in1=SCF[:, mo4 * tl:(mo4 + 1) * tl],
                        op0=ALU.add, op1=ALU.add),
                      reads=[rb, r_vec, r_SCF_xf], writes=[r_R2])

        marks["p3"] = p.gidx
        tap_R2("x2")
        Prog.barrier(r_HBF, [r_XIB, r_O12])
        for (t0, tl) in TILES:
            s2b, s2r = nbank()
            for mo in range(DK):
                E("act", lambda a, mo=mo, t0=t0, tl=tl: a.activation(
                    out=sbf[mo % 2][:, 0:tl], in_=R2[:, mo * PW + t0:mo * PW + t0 + tl],
                    func=AF.Square),
                  reads=[r_R2], writes=[r_sbf[mo % 2]])
                E("te", lambda t, mo=mo, tl=tl, s2b=s2b: t.matmul(
                    out=s2b[0:1, 0:tl], lhsT=onc[:], rhs=sbf[mo % 2][:, 0:tl],
                    start=(mo == 0), stop=(mo == DK - 1)),
                  reads=[r_sbf[mo % 2], r_onc], writes=[s2r])
            E("act", lambda a, s2b=s2b, tl=tl: a.activation(
                out=rowst[:, 0:tl], in_=s2b[0:1, 0:tl], func=AF.Sqrt,
                bias=vec[0:1, _vc["eps6"]:_vc["eps6"] + 1], scale=1.0 / 1024.0),
              reads=[s2r], writes=[r_rowst])
            E("dve", lambda v, tl=tl: v.reciprocal(out=rowst[:, 512:512 + tl], in_=rowst[:, 0:tl]),
              reads=[r_rowst], writes=[r_rowst])
            bb, rbb = nbank()
            E("te", lambda t, tl=tl, bb=bb: t.matmul(
                out=bb[:, 0:tl], lhsT=onr[:], rhs=rowst[:, 512:512 + tl], start=True, stop=True),
              reads=[r_rowst, r_onr], writes=[rbb])
            for mo in range(DK):
                E("dve", lambda v, mo=mo, t0=t0, tl=tl, bb=bb: v.tensor_tensor(
                    out=sc0[:, 0:tl], in0=R2[:, mo * PW + t0:mo * PW + t0 + tl],
                    in1=bb[:, 0:tl], op=ALU.mult),
                  reads=[r_R2, rbb], writes=[r_sc0])
                E("dve", lambda v, mo=mo, t0=t0, tl=tl: v.tensor_scalar(
                    out=XBA[:, HBF_OFF + mo * PW + t0:HBF_OFF + mo * PW + t0 + tl],
                    in0=sc0[:, 0:tl], scalar1=vslice("norm2_w", mo), scalar2=None, op0=ALU.mult),
                  reads=[r_sc0, r_vec], writes=[r_HBF])

        marks["p4"] = p.gidx
        Prog.barrier(r_wup[1], [r_bias])
        Prog.barrier(r_wdn[1], [r_bias])
        for blk in range(4):
            wb = blk % 2
            E("pool", lambda g, blk=blk, wb=wb: g.dma_start(
                out=wup[wb][:].rearrange("p (k h) -> p k h", k=8),
                in_=ap["w_ffn1"][:, blk * HB:(blk + 1) * HB].rearrange("(k p) h -> p k h", p=128)),
              writes=[r_wup[wb]], dma=True, sem="swup%d" % wb)
            E("pool", lambda g, blk=blk, wb=wb: g.dma_start(
                out=wdn[wb][:].rearrange("p (c d) -> p c d", c=4),
                in_=ap["w_ffn2"][blk * HB:(blk + 1) * HB, :].rearrange("(c p) d -> p c d", p=128)),
              writes=[r_wdn[wb]], dma=True, sem="swdn%d" % wb)
            for (t0, tl) in TILES:
                for m in range(4):
                    bk, rb = nbank()
                    for k in range(DK):
                        E("te", lambda t, k=k, m=m, t0=t0, tl=tl, wb=wb, bk=bk: t.matmul(
                            out=bk[:, 0:tl], lhsT=wup[wb][:, k * HB + m * 128:k * HB + (m + 1) * 128],
                            rhs=XBA[:, HBF_OFF + k * PW + t0:HBF_OFF + k * PW + t0 + tl],
                            start=(k == 0), stop=(k == DK - 1)),
                          reads=[r_wup[wb], r_HBF], writes=[rb])
                    bi = blk * 4 + m
                    E("act", lambda a, bi=bi, m=m, tl=tl, bk=bk: a.activation(
                        out=hid[:, m * 512:m * 512 + tl], in_=bk[:, 0:tl], func=AF.Gelu,
                        bias=vslice("ffn_b1", bi), scale=1.0),
                      reads=[rb, r_vec], writes=[r_hid[m]])
                for mo in range(DK):
                    bk, rb = nbank()
                    for c in range(4):
                        E("te", lambda t, c=c, mo=mo, tl=tl, wb=wb, bk=bk: t.matmul(
                            out=bk[:, 0:tl], lhsT=wdn[wb][:, c * D + mo * 128:c * D + (mo + 1) * 128],
                            rhs=hid[:, c * 512:c * 512 + tl], start=(c == 0), stop=(c == 3)),
                          reads=[r_wdn[wb], r_hid[c]], writes=[rb])
                    if blk == 0:
                        E("dve", lambda v, mo=mo, t0=t0, tl=tl, bk=bk: v.scalar_tensor_tensor(
                            out=R2[:, mo * PW + t0:mo * PW + t0 + tl], in0=bk[:, 0:tl],
                            scalar=vslice("ffn_b2", mo), in1=R2[:, mo * PW + t0:mo * PW + t0 + tl],
                            op0=ALU.add, op1=ALU.add),
                          reads=[rb, r_vec, r_R2], writes=[r_R2])
                    else:
                        E("dve", lambda v, mo=mo, t0=t0, tl=tl, bk=bk: v.tensor_tensor(
                            out=R2[:, mo * PW + t0:mo * PW + t0 + tl], in0=bk[:, 0:tl],
                            in1=R2[:, mo * PW + t0:mo * PW + t0 + tl], op=ALU.add),
                          reads=[rb, r_R2], writes=[r_R2])

        for (t0, tl) in TILES:
            for kh in range(2):
                E("sp", lambda s_, kh=kh, t0=t0, tl=tl: s_.dma_start(
                    out=SCF[:, 0:4 * tl].rearrange("p (k t) -> p k t", k=4),
                    in_=xT_v[:, 4 * kh:4 * kh + 4, t0:t0 + tl]),
                  reads=[r_xT], writes=[r_SCF_xf], dma=True)
                for mo4 in range(4):
                    mo = 4 * kh + mo4
                    E("dve", lambda v, mo=mo, mo4=mo4, t0=t0, tl=tl: v.tensor_tensor(
                        out=R2[:, mo * PW + t0:mo * PW + t0 + tl],
                        in0=R2[:, mo * PW + t0:mo * PW + t0 + tl],
                        in1=SCF[:, mo4 * tl:(mo4 + 1) * tl], op=ALU.add),
                      reads=[r_R2, r_SCF_xf], writes=[r_R2])

        marks["p5"] = p.gidx
        tap_R2("s1")
        for (t0, tl) in TILES:
            s1b, s1r = nbank()
            s2b, s2r = nbank()
            for mo in range(DK):
                E("act", lambda a, mo=mo, t0=t0, tl=tl: a.copy(
                    out=sbf[0][:, 0:tl], in_=R2[:, mo * PW + t0:mo * PW + t0 + tl]),
                  reads=[r_R2], writes=[r_sbf[0]])
                E("te", lambda t, mo=mo, tl=tl, s1b=s1b: t.matmul(
                    out=s1b[0:1, 0:tl], lhsT=onc[:], rhs=sbf[0][:, 0:tl],
                    start=(mo == 0), stop=(mo == DK - 1)),
                  reads=[r_sbf[0], r_onc], writes=[s1r])
                E("act", lambda a, mo=mo, t0=t0, tl=tl: a.activation(
                    out=sbf[1][:, 0:tl], in_=R2[:, mo * PW + t0:mo * PW + t0 + tl], func=AF.Square),
                  reads=[r_R2], writes=[r_sbf[1]])
                E("te", lambda t, mo=mo, tl=tl, s2b=s2b: t.matmul(
                    out=s2b[0:1, 0:tl], lhsT=onc[:], rhs=sbf[1][:, 0:tl],
                    start=(mo == 0), stop=(mo == DK - 1)),
                  reads=[r_sbf[1], r_onc], writes=[s2r])
            E("dve", lambda v, s1b=s1b, tl=tl: v.tensor_scalar(
                out=rowst[:, 0:tl], in0=s1b[0:1, 0:tl], scalar1=1.0 / 1024.0, scalar2=None,
                op0=ALU.mult),
              reads=[s1r], writes=[r_rowst])
            E("act", lambda a, tl=tl: a.activation(
                out=sc0[0:1, 0:tl], in_=rowst[:, 0:tl], func=AF.Square),
              reads=[r_rowst], writes=[r_sc0])
            E("dve", lambda v, s2b=s2b, tl=tl: v.scalar_tensor_tensor(
                out=rowst[:, 512:512 + tl], in0=s2b[0:1, 0:tl], scalar=1.0 / 1024.0,
                in1=sc0[0:1, 0:tl], op0=ALU.mult, op1=ALU.subtract),
              reads=[s2r, r_sc0], writes=[r_rowst])
            E("act", lambda a, tl=tl: a.activation(
                out=sc0[0:1, 0:tl], in_=rowst[:, 512:512 + tl], func=AF.Sqrt,
                bias=vec[0:1, _vc["eps5"]:_vc["eps5"] + 1], scale=1.0),
              reads=[r_rowst], writes=[r_sc0])
            E("dve", lambda v, tl=tl: v.reciprocal(
                out=rowst[:, 512:512 + tl], in_=sc0[0:1, 0:tl]),
              reads=[r_sc0], writes=[r_rowst])
            E("dve", lambda v, tl=tl: v.tensor_tensor(
                out=sc1[0:1, 0:tl], in0=rowst[:, 0:tl], in1=rowst[:, 512:512 + tl],
                op=ALU.mult),
              reads=[r_rowst], writes=[r_sc1])
            rbb_b, rbb_r = nbank()
            mrb_b, mrb_r = nbank()
            E("te", lambda t, tl=tl, rbb_b=rbb_b: t.matmul(
                out=rbb_b[:, 0:tl], lhsT=onr[:], rhs=rowst[:, 512:512 + tl], start=True, stop=True),
              reads=[r_rowst, r_onr], writes=[rbb_r])
            E("te", lambda t, tl=tl, mrb_b=mrb_b: t.matmul(
                out=mrb_b[:, 0:tl], lhsT=onr[:], rhs=sc1[0:1, 0:tl], start=True, stop=True),
              reads=[r_sc1, r_onr], writes=[mrb_r])
            for mo in range(DK):
                E("dve", lambda v, mo=mo, t0=t0, tl=tl, rbb_b=rbb_b: v.tensor_tensor(
                    out=sc0[:, 0:tl], in0=R2[:, mo * PW + t0:mo * PW + t0 + tl],
                    in1=rbb_b[:, 0:tl], op=ALU.mult),
                  reads=[r_R2, rbb_r], writes=[r_sc0])
                E("dve", lambda v, mo=mo, t0=t0, tl=tl, mrb_b=mrb_b: v.tensor_tensor(
                    out=R2[:, mo * PW + t0:mo * PW + t0 + tl], in0=sc0[:, 0:tl],
                    in1=mrb_b[:, 0:tl], op=ALU.subtract),
                  reads=[r_sc0, mrb_r], writes=[r_R2])
                E("act", lambda a, mo=mo, t0=t0, tl=tl: a.copy(
                    out=XBA[:, HBF_OFF + mo * PW + t0:HBF_OFF + mo * PW + t0 + tl],
                    in_=R2[:, mo * PW + t0:mo * PW + t0 + tl]),
                  reads=[r_R2], writes=[r_HBF])

        marks["p6"] = p.gidx
        tap_R2("y1")
        E("pool", lambda g: g.memset(sltF[:], float(BIGIDX)), writes=[r_sltF])
        E("pool", lambda g: g.memset(m2tt[:], 0.0), writes=[r_m2tt])
        for (t0, tl) in TILES:
            lgb, lgr = nbank()
            for k in range(DK):
                E("te", lambda t, k=k, t0=t0, tl=tl, lgb=lgb: t.matmul(
                    out=lgb[0:8, 0:tl], lhsT=rW[:, k * 8:(k + 1) * 8],
                    rhs=XBA[:, HBF_OFF + k * PW + t0:HBF_OFF + k * PW + t0 + tl],
                    start=(k == 0), stop=(k == DK - 1)),
                  reads=[r_rW, r_HBF], writes=[lgr])
            E("dve", lambda v, tl=tl, lgb=lgb: v.tensor_copy(out=lgF[:, 0:tl], in_=lgb[0:8, 0:tl]),
              reads=[lgr], writes=[r_lgF])
            nsub = (tl + 127) // 128
            tpb, tpr = nbank()
            for ji in range(nsub):
                jl = min(128, tl - ji * 128)
                E("te", lambda t, ji=ji, jl=jl, tpb=tpb: t.transpose(
                    out=tpb[0:jl, 8 * ji:8 * ji + 8], in_=lgF[0:8, ji * 128:ji * 128 + jl],
                    identity=idf[0:8, 0:8]),
                  reads=[r_lgF, r_idf], writes=[tpr])
            E("dve", lambda v, nsub=nsub, tl=tl, tpb=tpb: v.tensor_copy(
                out=ltm[0:min(tl, 128), 0:8 * nsub] if tl < 128 else ltm[:, 0:8 * nsub],
                in_=tpb[0:min(tl, 128), 0:8 * nsub] if tl < 128 else tpb[0:128, 0:8 * nsub]),
              reads=[tpr], writes=[r_ltm])
            for ji in range(nsub):
                jl = min(128, tl - ji * 128)
                E("dve", lambda v, ji=ji, jl=jl: v.reduce_max(
                    out=mxp[0:jl, 0:1], in_=ltm[0:jl, 8 * ji:8 * ji + 8], axis=AX.X, negate=True),
                  reads=[r_ltm], writes=[r_mxp])
                E("act", lambda a, ji=ji, jl=jl: a.activation(
                    out=eqtm[0:jl, 8 * ji:8 * ji + 8], in_=ltm[0:jl, 8 * ji:8 * ji + 8],
                    func=AF.Exp, bias=mxp[0:jl, 0:1], scale=1.0),
                  reads=[r_ltm, r_mxp], writes=[r_eqtm])
                E("dve", lambda v, ji=ji, jl=jl: v.reduce_sum(
                    out=mxp[0:jl, 1:2], in_=eqtm[0:jl, 8 * ji:8 * ji + 8], axis=AX.X),
                  reads=[r_eqtm], writes=[r_mxp])
                E("dve", lambda v, jl=jl: v.reciprocal(out=mxp[0:jl, 2:3], in_=mxp[0:jl, 1:2]),
                  reads=[r_mxp], writes=[r_mxp])
                E("dve", lambda v, ji=ji, jl=jl: v.tensor_scalar(
                    out=eqtm[0:jl, 8 * ji:8 * ji + 8], in0=eqtm[0:jl, 8 * ji:8 * ji + 8],
                    scalar1=mxp[0:jl, 2:3], scalar2=None, op0=ALU.mult),
                  reads=[r_eqtm, r_mxp], writes=[r_eqtm])
                E("dve", lambda v, ji=ji, jl=jl: v.reduce_max(
                    out=mxp[0:jl, 3:4], in_=eqtm[0:jl, 8 * ji:8 * ji + 8], axis=AX.X),
                  reads=[r_eqtm], writes=[r_mxp])
                E("dve", lambda v, ji=ji, jl=jl: v.tensor_scalar(
                    out=ltm[0:jl, 8 * ji:8 * ji + 8], in0=eqtm[0:jl, 8 * ji:8 * ji + 8],
                    scalar1=mxp[0:jl, 3:4], scalar2=None, op0=ALU.is_equal),
                  reads=[r_eqtm, r_mxp], writes=[r_ltm])
                E("dve", lambda v, jl=jl: v.tensor_scalar(
                    out=mxp[0:jl, 4:5], in0=mxp[0:jl, 3:4], scalar1=2.0, scalar2=None,
                    op0=ALU.mult),
                  reads=[r_mxp], writes=[r_mxp])
                tb2, tr2 = nbank()
                E("te", lambda t, ji=ji, jl=jl, tb2=tb2: t.transpose(
                    out=tb2[0:8, 0:jl], in_=ltm[0:jl, 8 * ji:8 * ji + 8], identity=idf[0:jl, 0:jl]),
                  reads=[r_ltm, r_idf], writes=[tr2])
                E("dve", lambda v, ji=ji, jl=jl, tb2=tb2: v.tensor_copy(
                    out=eqT[:, ji * 128:ji * 128 + jl], in_=tb2[0:8, 0:jl]),
                  reads=[tr2], writes=[r_eqT])
                tb3, tr3 = nbank()
                E("te", lambda t, ji=ji, jl=jl, tb3=tb3: t.transpose(
                    out=tb3[0:1, 0:jl], in_=mxp[0:jl, 4:5], identity=idf[0:jl, 0:jl]),
                  reads=[r_mxp, r_idf], writes=[tr3])
                E("dve", lambda v, ji=ji, jl=jl, tb3=tb3: v.tensor_copy(
                    out=v2r[:, ji * 128:ji * 128 + jl], in_=tb3[0:1, 0:jl]),
                  reads=[tr3], writes=[r_v2r])
            pfb, pfr = nbank()
            E("te", lambda t, tl=tl, pfb=pfb: t.matmul(
                out=pfb[0:8, 0:tl], lhsT=triu[:], rhs=eqT[:, 0:tl], start=True, stop=True),
              reads=[r_triu, r_eqT], writes=[pfr])
            E("dve", lambda v, tl=tl, pfb=pfb: v.tensor_scalar(
                out=lgF[:, 0:tl], in0=pfb[0:8, 0:tl], scalar1=0.0, scalar2=None, op0=ALU.is_equal),
              reads=[pfr], writes=[r_lgF])
            E("dve", lambda v, tl=tl: v.tensor_tensor(
                out=eqT[:, 0:tl], in0=eqT[:, 0:tl], in1=lgF[:, 0:tl], op=ALU.mult),
              reads=[r_eqT, r_lgF], writes=[r_eqT])
            vbb, vbr = nbank()
            E("te", lambda t, tl=tl, vbb=vbb: t.matmul(
                out=vbb[0:8, 0:tl], lhsT=onr[0:1, 0:8], rhs=v2r[:, 0:tl], start=True, stop=True),
              reads=[r_onr, r_v2r], writes=[vbr])
            E("dve", lambda v, t0=t0, tl=tl, vbb=vbb: v.tensor_tensor(
                out=m2all[:, t0:t0 + tl], in0=eqT[:, 0:tl], in1=vbb[0:8, 0:tl], op=ALU.mult),
              reads=[r_eqT, vbr], writes=[r_m2all])
            mtb, mtr = nbank()
            E("te", lambda t, t0=t0, tl=tl, mtb=mtb: t.matmul(
                out=mtb[0:1, 0:tl], lhsT=vec[0:8, _vc["one8"]:_vc["one8"] + 1],
                rhs=m2all[:, t0:t0 + tl], start=True, stop=True),
              reads=[r_vec, r_m2all], writes=[mtr])
            E("dve", lambda v, tl=tl, mtb=mtb: v.tensor_copy(out=v2r[:, 0:tl], in_=mtb[0:1, 0:tl]),
              reads=[mtr], writes=[r_v2r])
            for ji in range((tl + 127) // 128):
                jl = min(128, tl - ji * 128)
                ub = (t0 + ji * 128) // 128
                mtb2, mtr2 = nbank()
                E("te", lambda t, ji=ji, jl=jl, mtb2=mtb2: t.transpose(
                    out=mtb2[0:jl, 0:1], in_=v2r[0:1, ji * 128:ji * 128 + jl],
                    identity=idf[0:1, 0:1]),
                  reads=[r_v2r, r_idf], writes=[mtr2])
                E("dve", lambda v, jl=jl, ub=ub, mtb2=mtb2: v.tensor_copy(
                    out=m2tt[0:jl, ub:ub + 1], in_=mtb2[0:jl, 0:1]),
                  reads=[mtr2], writes=[r_m2tt])

        marks["p7"] = p.gidx
        if "m2" in tap_aps:
            E("sp", lambda s_: s_.dma_start(out=tap_aps["m2"][:, :], in_=m2all[:]),
              reads=[r_m2all], writes=[r_taps["m2"]], dma=True)
        Prog.barrier(r_posI, [r_SCF_xf] + r_SCF_m2b)
        E("dve", lambda v: v.tensor_scalar(
            out=m2all[:], in0=m2all[:], scalar1=0.0, scalar2=None, op0=ALU.is_gt),
          reads=[r_m2all], writes=[r_m2all])
        E("dve", lambda v: v.tensor_tensor_scan(
            out=SCF[0:8, 0:PW], data0=m2all[:], data1=m2all[:], initial=0.0,
            op0=ALU.add, op1=ALU.max),
          reads=[r_m2all], writes=[r_posI])
        for (t0, tl) in TILES:
            E("dve", lambda v, t0=t0, tl=tl: v.tensor_tensor(
                out=eqT[:, 0:tl], in0=SCF[0:8, t0:t0 + tl], in1=m2all[:, t0:t0 + tl],
                op=ALU.subtract),
              reads=[r_posI, r_m2all], writes=[r_eqT])
            E("dve", lambda v, tl=tl: v.tensor_scalar(
                out=lgF[:, 0:tl], in0=eqT[:, 0:tl], scalar1=float(CAP), scalar2=None,
                op0=ALU.is_ge),
              reads=[r_eqT], writes=[r_lgF])
            E("dve", lambda v, t0=t0, tl=tl: v.tensor_tensor(
                out=lgF[:, 0:tl], in0=lgF[:, 0:tl], in1=m2all[:, t0:t0 + tl], op=ALU.mult),
              reads=[r_lgF, r_m2all], writes=[r_lgF])
            E("dve", lambda v, tl=tl: v.tensor_scalar(
                out=eqT[:, 0:tl], in0=eqT[:, 0:tl], scalar1=vec[0:8, _vc["ecap"]:_vc["ecap"] + 1],
                scalar2=None, op0=ALU.add),
              reads=[r_eqT, r_vec], writes=[r_eqT])
            E("dve", lambda v, t0=t0, tl=tl: v.tensor_tensor(
                out=eqT[:, 0:tl], in0=eqT[:, 0:tl], in1=m2all[:, t0:t0 + tl], op=ALU.mult),
              reads=[r_eqT, r_m2all], writes=[r_eqT])
            sbk, sbr = nbank()
            E("te", lambda t, tl=tl, sbk=sbk: t.matmul(
                out=sbk[0:1, 0:tl], lhsT=vec[0:8, _vc["one8"]:_vc["one8"] + 1],
                rhs=eqT[:, 0:tl], start=True, stop=True),
              reads=[r_vec, r_eqT], writes=[sbr])
            obk, obr = nbank()
            E("te", lambda t, tl=tl, obk=obk: t.matmul(
                out=obk[0:1, 0:tl], lhsT=vec[0:8, _vc["one8"]:_vc["one8"] + 1],
                rhs=lgF[:, 0:tl], start=True, stop=True),
              reads=[r_vec, r_lgF], writes=[obr])
            E("dve", lambda v, tl=tl, sbk=sbk: v.tensor_copy(out=v2r[:, 0:tl], in_=sbk[0:1, 0:tl]),
              reads=[sbr], writes=[r_v2r])
            E("dve", lambda v, tl=tl, obk=obk: v.scalar_tensor_tensor(
                out=lgF[0:1, 0:tl], in0=obk[0:1, 0:tl], scalar=BIGIDX,
                in1=v2r[0:1, 0:tl], op0=ALU.mult, op1=ALU.add),
              reads=[obr, r_v2r], writes=[r_lgF])
            for ji in range((tl + 127) // 128):
                jl = min(128, tl - ji * 128)
                ub = (t0 + ji * 128) // 128
                stb, str_ = nbank()
                E("te", lambda t, ji=ji, jl=jl, stb=stb: t.transpose(
                    out=stb[0:jl, 0:1], in_=lgF[0:1, ji * 128:ji * 128 + jl],
                    identity=idf[0:1, 0:1]),
                  reads=[r_lgF, r_idf], writes=[str_])
                E("dve", lambda v, jl=jl, ub=ub, stb=stb: v.tensor_copy(
                    out=sltF[0:jl, ub:ub + 1], in_=stb[0:jl, 0:1]),
                  reads=[str_], writes=[r_sltF])
        E("dve", lambda v: v.tensor_copy(out=sltI[:], in_=sltF[:]),
          reads=[r_sltF], writes=[r_sltI])
        E("pool", lambda g: g.memset(ytm[:], 0.0), writes=[r_ytm])
        for j4 in range(8 * CAP // 128):
            E("sp", lambda s_, j4=j4: s_.dma_start(
                out=y1c_ap[j4 * 128:(j4 + 1) * 128, :], in_=ytm[:]),
              reads=[r_ytm], writes=[r_y1c], dma=True, sem="szf")
        subs9 = [(ii * 128, 128) for ii in range(16)] + [(2048, 32)]
        for u, (u0, ul) in enumerate(subs9):
            tbA, trA = nbank()
            tbB, trB = nbank()
            for mo in range(DK):
                tb, tr = (tbA, trA) if mo < 4 else (tbB, trB)
                E("te", lambda t, mo=mo, u0=u0, ul=ul, tb=tb: t.transpose(
                    out=tb[0:ul, (mo % 4) * 128:(mo % 4) * 128 + 128],
                    in_=R2[:, mo * PW + u0:mo * PW + u0 + ul], identity=idf[:]),
                  reads=[r_R2, r_idf], writes=[tr])
            E("dve", lambda v, ul=ul, tbA=tbA: v.tensor_copy(
                out=ytm[0:ul, 0:512], in_=tbA[0:ul, 0:512]),
              reads=[trA], writes=[r_ytm])
            E("dve", lambda v, ul=ul, tbB=tbB: v.tensor_copy(
                out=ytm[0:ul, 512:1024], in_=tbB[0:ul, 0:512]),
              reads=[trB], writes=[r_ytm])
            E("pool", lambda g, u=u, ul=ul: g.indirect_dma_start(
                out=y1c_ap, out_offset=bass.IndirectOffsetOnAxis(sltI[0:ul, u:u + 1], 0),
                in_=ytm[0:ul, :], in_offset=None,
                bounds_check=8 * CAP - 1, oob_is_err=False),
              reads=[r_ytm, r_sltI], writes=[r_y1c], dma=True)
        E("pool", lambda g: g.memset(ytm[:], 0.0), reads=[r_y1c], writes=[r_ytm])
        for j4 in range(CAP // 128):
            E("sp", lambda s_, j4=j4: s_.dma_start(
                out=rout_ap[7 * CAP + j4 * 128:7 * CAP + (j4 + 1) * 128, :], in_=ytm[:]),
              reads=[r_ytm], writes=[r_rout], dma=True, sem="szf")

        Prog.barrier(r_wgt[0], [r_proj])
        Prog.barrier(r_wgt[1], [r_proj])
        for hb in range(8):
            wb = hb % 2
            upv = ap["w_sh_up"][:, hb * HB:(hb + 1) * HB]
            gtv = ap["w_sh_gate"][:, hb * HB:(hb + 1) * HB]
            dnv = ap["w_sh_down"][hb * HB:(hb + 1) * HB, :]
            E("pool", lambda g, upv=upv, wb=wb: g.dma_start(
                out=wup[wb][:].rearrange("p (k h) -> p k h", k=8),
                in_=upv.rearrange("(k p) h -> p k h", p=128)),
              writes=[r_wup[wb]], dma=True, sem="swup%d" % wb)
            E("pool", lambda g, gtv=gtv, wb=wb: g.dma_start(
                out=wgt[wb][:].rearrange("p (k h) -> p k h", k=8),
                in_=gtv.rearrange("(k p) h -> p k h", p=128)),
              writes=[r_wgt[wb]], dma=True, sem="swgt%d" % wb)
            E("pool", lambda g, dnv=dnv, wb=wb: g.dma_start(
                out=wdn[wb][:].rearrange("p (c d) -> p c d", c=4),
                in_=dnv.rearrange("(c p) d -> p c d", p=128)),
              writes=[r_wdn[wb]], dma=True, sem="swdn%d" % wb)
            for ti, (t0, tl) in enumerate(TILES):
                for m in range(4):
                    bU, rU = nbank()
                    bG, rG = nbank()
                    for k in range(DK):
                        E("te", lambda t, k=k, m=m, t0=t0, tl=tl, wb=wb, bU=bU: t.matmul(
                            out=bU[:, 0:tl], lhsT=wup[wb][:, k * HB + m * 128:k * HB + (m + 1) * 128],
                            rhs=XBA[:, HBF_OFF + k * PW + t0:HBF_OFF + k * PW + t0 + tl],
                            start=(k == 0), stop=(k == DK - 1)),
                          reads=[r_wup[wb], r_HBF], writes=[rU])
                    for k in range(DK):
                        E("te", lambda t, k=k, m=m, t0=t0, tl=tl, wb=wb, bG=bG: t.matmul(
                            out=bG[:, 0:tl], lhsT=wgt[wb][:, k * HB + m * 128:k * HB + (m + 1) * 128],
                            rhs=XBA[:, HBF_OFF + k * PW + t0:HBF_OFF + k * PW + t0 + tl],
                            start=(k == 0), stop=(k == DK - 1)),
                          reads=[r_wgt[wb], r_HBF], writes=[rG])
                    E("act", lambda a, tl=tl, bG=bG: a.activation(
                        out=sc1[:, 0:tl], in_=bG[:, 0:tl], func=AF.Sigmoid),
                      reads=[rG], writes=[r_sc1])
                    E("dve", lambda v, tl=tl, bG=bG: v.tensor_tensor(
                        out=sc0[:, 0:tl], in0=bG[:, 0:tl], in1=sc1[:, 0:tl], op=ALU.mult),
                      reads=[rG, r_sc1], writes=[r_sc0])
                    E("dve", lambda v, m=m, tl=tl, bU=bU: v.tensor_tensor(
                        out=hid[:, m * 512:m * 512 + tl], in0=bU[:, 0:tl],
                        in1=sc0[:, 0:tl], op=ALU.mult),
                      reads=[rU, r_sc0], writes=[r_hid[m]])
                for mo in range(DK):
                    bk, rb = nbank()
                    for c in range(4):
                        E("te", lambda t, c=c, mo=mo, tl=tl, wb=wb, bk=bk: t.matmul(
                            out=bk[:, 0:tl], lhsT=wdn[wb][:, c * D + mo * 128:c * D + (mo + 1) * 128],
                            rhs=hid[:, c * 512:c * 512 + tl], start=(c == 0), stop=(c == 3)),
                          reads=[r_wdn[wb], r_hid[c]], writes=[rb])
                    E("dve", lambda v, mo=mo, t0=t0, tl=tl, bk=bk: v.tensor_tensor(
                        out=R2[:, mo * PW + t0:mo * PW + t0 + tl], in0=bk[:, 0:tl],
                        in1=R2[:, mo * PW + t0:mo * PW + t0 + tl], op=ALU.add),
                      reads=[rb, r_R2], writes=[r_R2])

        Prog.barrier(r_y1g, [r_XIB, r_O12, r_HBF])
        Y1G_OFF = 0
        Y1GF_OFF = 4096
        for e in range(N_EXP):
            E("sp", lambda s_, e=e: s_.dma_start(
                out=XBA[:, Y1G_OFF:Y1G_OFF + 4096].rearrange("p (s d) -> p s d", s=4),
                in_=y1c_ap[e * CAP:(e + 1) * CAP, :].rearrange("(s p) d -> p s d", p=128)),
              reads=[r_y1c], writes=[r_y1g], dma=True, sem="sy1g")
            for k in range(DK):
                pa = k % 2
                for s4 in range(4):
                    E("te", lambda t, k=k, s4=s4, pa=pa: t.transpose(
                        out=pT[pa][0:128, s4 * 128:(s4 + 1) * 128],
                        in_=XBA[:, Y1G_OFF + s4 * 1024 + k * 128:Y1G_OFF + s4 * 1024 + (k + 1) * 128],
                        identity=idb[:]),
                      reads=[r_y1g, r_idb], writes=[r_pT[pa]])
                E("dve", lambda v, k=k, pa=pa: v.tensor_copy(
                    out=XBA[:, Y1GF_OFF + k * 512:Y1GF_OFF + (k + 1) * 512],
                    in_=pT[pa][0:128, 0:512]),
                  reads=[r_pT[pa]], writes=[r_y1gf])
            for hb in range(8):
                wb = hb % 2
                upv = ap["w_moe_up"][e, :, hb * HB:(hb + 1) * HB]
                gtv = ap["w_moe_gate"][e, :, hb * HB:(hb + 1) * HB]
                dnv = ap["w_moe_down"][e, hb * HB:(hb + 1) * HB, :]
                E("pool", lambda g, upv=upv, wb=wb: g.dma_start(
                    out=wup[wb][:].rearrange("p (k h) -> p k h", k=8),
                    in_=upv.rearrange("(k p) h -> p k h", p=128)),
                  writes=[r_wup[wb]], dma=True, sem="swup%d" % wb)
                E("pool", lambda g, gtv=gtv, wb=wb: g.dma_start(
                    out=wgt[wb][:].rearrange("p (k h) -> p k h", k=8),
                    in_=gtv.rearrange("(k p) h -> p k h", p=128)),
                  writes=[r_wgt[wb]], dma=True, sem="swgt%d" % wb)
                E("pool", lambda g, dnv=dnv, wb=wb: g.dma_start(
                    out=wdn[wb][:].rearrange("p (c d) -> p c d", c=4),
                    in_=dnv.rearrange("(c p) d -> p c d", p=128)),
                  writes=[r_wdn[wb]], dma=True, sem="swdn%d" % wb)
                for m in range(4):
                    bU, rU = nbank()
                    bG, rG = nbank()
                    for k in range(DK):
                        E("te", lambda t, k=k, m=m, wb=wb, bU=bU: t.matmul(
                            out=bU[:, 0:CAP], lhsT=wup[wb][:, k * HB + m * 128:k * HB + (m + 1) * 128],
                            rhs=XBA[:, Y1GF_OFF + k * 512:Y1GF_OFF + (k + 1) * 512],
                            start=(k == 0), stop=(k == DK - 1)),
                          reads=[r_wup[wb], r_y1gf], writes=[rU])
                    for k in range(DK):
                        E("te", lambda t, k=k, m=m, wb=wb, bG=bG: t.matmul(
                            out=bG[:, 0:CAP], lhsT=wgt[wb][:, k * HB + m * 128:k * HB + (m + 1) * 128],
                            rhs=XBA[:, Y1GF_OFF + k * 512:Y1GF_OFF + (k + 1) * 512],
                            start=(k == 0), stop=(k == DK - 1)),
                          reads=[r_wgt[wb], r_y1gf], writes=[rG])
                    E("act", lambda a, bG=bG: a.activation(
                        out=sc1[:, 0:CAP], in_=bG[:, 0:CAP], func=AF.Sigmoid),
                      reads=[rG], writes=[r_sc1])
                    E("dve", lambda v, bG=bG: v.tensor_tensor(
                        out=sc0[:, 0:CAP], in0=bG[:, 0:CAP], in1=sc1[:, 0:CAP], op=ALU.mult),
                      reads=[rG, r_sc1], writes=[r_sc0])
                    E("dve", lambda v, m=m, bU=bU: v.tensor_tensor(
                        out=hid[:, m * 512:m * 512 + CAP], in0=bU[:, 0:CAP],
                        in1=sc0[:, 0:CAP], op=ALU.mult),
                      reads=[rU, r_sc0], writes=[r_hid[m]])
                for mo in range(DK):
                    bk, rb = nbank()
                    for c in range(4):
                        E("te", lambda t, c=c, mo=mo, wb=wb, bk=bk: t.matmul(
                            out=bk[:, 0:CAP], lhsT=wdn[wb][:, c * D + mo * 128:c * D + (mo + 1) * 128],
                            rhs=hid[:, c * 512:c * 512 + CAP], start=(c == 0), stop=(c == 3)),
                          reads=[r_wdn[wb], r_hid[c]], writes=[rb])
                    if hb == 0:
                        E("dve", lambda v, mo=mo, bk=bk: v.tensor_copy(
                            out=routf[:, mo * 512:(mo + 1) * 512], in_=bk[:, 0:CAP]),
                          reads=[rb], writes=[r_routf])
                    else:
                        E("dve", lambda v, mo=mo, bk=bk: v.tensor_tensor(
                            out=routf[:, mo * 512:(mo + 1) * 512], in0=bk[:, 0:CAP],
                            in1=routf[:, mo * 512:(mo + 1) * 512], op=ALU.add),
                          reads=[rb, r_routf], writes=[r_routf])
            for s4 in range(4):
                pa = s4 % 2
                for mo in range(DK):
                    E("te", lambda t, mo=mo, s4=s4, pa=pa: t.transpose(
                        out=pT[pa][0:128, mo * 128:(mo + 1) * 128],
                        in_=routf[:, mo * 512 + s4 * 128:mo * 512 + (s4 + 1) * 128],
                        identity=idb[:]),
                      reads=[r_routf, r_idb], writes=[r_pT[pa]])
                E("dve", lambda v, pa=pa: v.tensor_copy(out=ytm[:], in_=pT[pa][0:128, 0:1024]),
                  reads=[r_pT[pa]], writes=[r_ytm])
                E("sp", lambda s_, e=e, s4=s4: s_.dma_start(
                    out=rout_ap[e * CAP + s4 * 128:e * CAP + (s4 + 1) * 128, :], in_=ytm[:]),
                  reads=[r_ytm], writes=[r_rout], dma=True, sem="srt")

        marks["p8"] = p.gidx
        tap_R2("racc")
        Prog.barrier(r_SCF_lnsq, r_SCF_m2b)
        subs = [(i * 128, 128) for i in range(16)] + [(2048, 32)]
        for u, (u0, ul) in enumerate(subs):
            tb0, tr0 = nbank()
            tb1, tr1 = nbank()
            for mo in range(DK):
                tb, tr = (tb0, tr0) if mo < 4 else (tb1, tr1)
                E("te", lambda t, mo=mo, u0=u0, ul=ul, tb=tb: t.transpose(
                    out=tb[0:ul, (mo % 4) * 128:(mo % 4) * 128 + 128],
                    in_=R2[:, mo * PW + u0:mo * PW + u0 + ul], identity=idf[:]),
                  reads=[r_R2, r_idf], writes=[tr])
            E("dve", lambda v, ul=ul, tb0=tb0: v.tensor_copy(
                out=SCF[0:ul, 1024:1536], in_=tb0[0:ul, 0:512]),
              reads=[tr0], writes=[r_ttile])
            E("dve", lambda v, ul=ul, tb1=tb1: v.tensor_copy(
                out=SCF[0:ul, 1536:2048], in_=tb1[0:ul, 0:512]),
              reads=[tr1], writes=[r_ttile])
            E("pool", lambda g, ul=ul: g.memset(ytm[0:ul, :], 0.0),
              reads=[], writes=[r_ytm])
            E("pool", lambda g, u=u, ul=ul: g.indirect_dma_start(
                out=ytm[0:ul, :], out_offset=None,
                in_=rout_ap, in_offset=bass.IndirectOffsetOnAxis(sltI[0:ul, u:u + 1], 0),
                bounds_check=8 * CAP - 1, oob_is_err=False),
              reads=[r_rout, r_sltI], writes=[r_ytm], dma=True, sem="sgrt")
            E("dve", lambda v, u=u, ul=ul: v.scalar_tensor_tensor(
                out=SCF[0:ul, 1024:2048], in0=ytm[0:ul, :],
                scalar=m2tt[0:ul, u:u + 1], in1=SCF[0:ul, 1024:2048],
                op0=ALU.mult, op1=ALU.add),
              reads=[r_ytm, r_m2tt, r_ttile], writes=[r_ttile])
            E("dve", lambda v, ul=ul: v.reduce_sum(
                out=mxp[0:ul, 0:1], in_=SCF[0:ul, 1024:2048], axis=AX.X),
              reads=[r_ttile], writes=[r_mxp])
            E("dve", lambda v, ul=ul: v.tensor_scalar(
                out=mxp[0:ul, 1:2], in0=mxp[0:ul, 0:1], scalar1=1.0 / 1024.0, scalar2=None,
                op0=ALU.mult),
              reads=[r_mxp], writes=[r_mxp])
            E("act", lambda a, ul=ul: a.activation(
                out=SCF[0:ul, 0:1024], in_=SCF[0:ul, 1024:2048], func=AF.Square),
              reads=[r_ttile], writes=[r_SCF_lnsq])
            E("dve", lambda v, ul=ul: v.reduce_sum(
                out=mxp[0:ul, 2:3], in_=SCF[0:ul, 0:1024], axis=AX.X),
              reads=[r_SCF_lnsq], writes=[r_mxp])
            E("act", lambda a, ul=ul: a.activation(
                out=mxp[0:ul, 3:4], in_=mxp[0:ul, 1:2], func=AF.Square),
              reads=[r_mxp], writes=[r_mxp])
            E("dve", lambda v, ul=ul: v.scalar_tensor_tensor(
                out=mxp[0:ul, 4:5], in0=mxp[0:ul, 2:3], scalar=1.0 / 1024.0,
                in1=mxp[0:ul, 3:4], op0=ALU.mult, op1=ALU.subtract),
              reads=[r_mxp], writes=[r_mxp])
            E("act", lambda a, ul=ul: a.activation(
                out=mxp[0:ul, 5:6], in_=mxp[0:ul, 4:5], func=AF.Sqrt,
                bias=vec[0:ul, _vc["eps5"]:_vc["eps5"] + 1], scale=1.0),
              reads=[r_mxp], writes=[r_mxp])
            E("dve", lambda v, ul=ul: v.reciprocal(out=mxp[0:ul, 6:7], in_=mxp[0:ul, 5:6]),
              reads=[r_mxp], writes=[r_mxp])
            E("dve", lambda v, ul=ul: v.tensor_scalar(
                out=SCF[0:ul, 1024:2048], in0=SCF[0:ul, 1024:2048], scalar1=mxp[0:ul, 1:2],
                scalar2=mxp[0:ul, 6:7], op0=ALU.subtract, op1=ALU.mult),
              reads=[r_ttile, r_mxp], writes=[r_ttile])
            E("sp", lambda s_, u0=u0, ul=ul: s_.dma_start(
                out=out_ap[u0:u0 + ul, :], in_=SCF[0:ul, 1024:2048]),
              reads=[r_ttile], writes=[r_out], dma=True)

        marks["p9"] = p.gidx
        fin = [r_out] + list(r_taps.values())
        E("pool", lambda g: g.engine_nop(), reads=fin)

        semnames = set()
        for e in ENGINES:
            for op in p.ops[e]:
                semnames.add(op.sem)
        sems = {}
        for name in sorted(semnames):
            sems[name] = cm.enter_context(nc.semaphore("sem_" + name))
        block = cm.enter_context(nc.Block())
        p.finalize(block, sems, trunc=trunc)
    return nc, marks


def _toeplitz(rpb, w, m, h):
    idx = (np.arange(m)[:, None] - np.arange(m)[None, :]) + w - 1
    return rpb[idx, h]

def _core_blocks(hh, w, s):
    blocks = []
    if hh == 0:
        blocks.append((0, s, True))
        for o in range(s, 2048, w):
            blocks.append((o, w, True))
        pos = blocks[-1][0] + w
        while pos < EW:
            ln = min(w, EW - pos)
            blocks.append((pos, ln, False))
            pos += ln
    else:
        first = s + ((28 - s) // w) * w
        pos = 0
        while pos < first:
            ln = min(w, first - pos)
            blocks.append((pos, ln, False))
            pos += ln
        o = first
        while o + w <= EW - 4:
            blocks.append((o, w, True))
            o += w
        blocks.append((o, (EW - 4) - o, True))
        blocks.append((EW - 4, 4, False))
    return blocks

def _bias_for_core(hh, rpb1, rpb2):
    out = np.zeros((NSLOT, 128, BIAS_W), np.float32)
    for j, (w, s) in enumerate(BR):
        rpb = rpb1 if j == 0 else rpb2
        blocks = _core_blocks(hh, w, s)
        for g, (g0, gl) in enumerate(GROUPS[j]):
            sl = _slot(j, g)
            for h in range(HEADS):
                th = np.full((gl, gl), -100.0, np.float32)
                for (b0, bl, real) in blocks:
                    if b0 >= g0 + gl or b0 + bl <= g0:
                        continue
                    assert b0 >= g0 and b0 + bl <= g0 + gl, (j, g, b0, bl, g0, gl)
                    lo = b0 - g0
                    if real:
                        th[lo:lo + bl, lo:lo + bl] = _toeplitz(rpb, w, bl, h)
                    else:
                        th[lo:lo + bl, lo:lo + bl] = 0.0
                out[sl, 0:gl, 120 * h:120 * h + gl] = th
    return out

def _vecs_host(i):
    v = np.zeros((NV, 128), np.float32)
    def put(name, arr):
        a = np.asarray(arr, np.float32).reshape(-1, 128)
        v[_vc[name]:_vc[name] + a.shape[0], :] = a
    put("down_b", i["L0_attn_down_b"])
    put("up_b", i["L0_attn_up_b"])
    put("norm1_w", i["L0_attn_norm1_w"])
    put("norm2_w", i["L0_attn_norm2_w"])
    put("ffn_b1", i["L0_attn_ffn_b1"])
    put("ffn_b2", i["L0_attn_ffn_b2"])
    put("proj_b", np.concatenate([np.asarray(i["L0_attn_proj1_b"]),
                                  np.asarray(i["L0_attn_proj2_b"])]))
    qb = np.concatenate([np.asarray(i["L0_attn_qkv1_b"]), np.asarray(i["L0_attn_qkv2_b"])])
    v[_vc["qkv_b"]:_vc["qkv_b"] + 24, 0:64] = qb.reshape(24, 64)
    v[_vc["eps6"], :] = 1e-6
    v[_vc["eps5"], :] = 1e-5
    v[_vc["ecap"], 0:8] = np.arange(8, dtype=np.float32) * CAP
    v[_vc["one8"], :] = 1.0
    return v

_NC_CACHE = {}

def _in_maps(i, taps=()):
    x = i["L0_attn_x"]
    projp = np.stack([
        np.ascontiguousarray(np.asarray(i["L0_attn_proj1_w"], np.float32)),
        np.ascontiguousarray(np.asarray(i["L0_attn_proj2_w"], np.float32))])
    shared = dict(
        w_down=np.ascontiguousarray(np.asarray(i["L0_attn_down_w"], np.float32)),
        w_up=np.ascontiguousarray(np.asarray(i["L0_attn_up_w"], np.float32)),
        w_qkv1=np.ascontiguousarray(np.asarray(i["L0_attn_qkv1_w"], np.float32)),
        w_qkv2=np.ascontiguousarray(np.asarray(i["L0_attn_qkv2_w"], np.float32)),
        w_projp=projp,
        w_ffn1=np.ascontiguousarray(np.asarray(i["L0_attn_ffn_w1"], np.float32)),
        w_ffn2=np.ascontiguousarray(np.asarray(i["L0_attn_ffn_w2"], np.float32)),
        w_router=np.ascontiguousarray(np.asarray(i["L0_ffn_router_w"], np.float32)),
        w_moe_up=np.ascontiguousarray(np.asarray(i["L0_ffn_up_w"], np.float32)),
        w_moe_gate=np.ascontiguousarray(np.asarray(i["L0_ffn_gate_w"], np.float32)),
        w_moe_down=np.ascontiguousarray(np.asarray(i["L0_ffn_down_w"], np.float32)),
        w_sh_up=np.ascontiguousarray(np.asarray(i["L0_ffn_shared_up_w"], np.float32)),
        w_sh_gate=np.ascontiguousarray(np.asarray(i["L0_ffn_shared_gate_w"], np.float32)),
        w_sh_down=np.ascontiguousarray(np.asarray(i["L0_ffn_shared_down_w"], np.float32)),
        ident_f=np.eye(128, dtype=np.float32),
        ident_b=np.eye(128, dtype=ml_dtypes.bfloat16),
        ident2=np.ascontiguousarray(np.vstack([np.eye(32), np.eye(32)]).astype(ml_dtypes.bfloat16)),
        ones_r=np.ones((1, 128), np.float32),
        ones_c=np.ones((128, 1), ml_dtypes.bfloat16),
        tri_u=np.ascontiguousarray(np.triu(np.ones((8, 8), np.float32), 1)),
        sel8=np.ascontiguousarray(
            np.stack([np.tile((np.arange(8) == e).astype(np.float32)[:, None], (1, 128))
                      for e in range(N_EXP)], axis=1).reshape(8, N_EXP * 128)),
        vecs=_vecs_host(i),
    )
    bias_h = [_bias_for_core(hh, np.asarray(i["L0_attn_rpb1"], np.float32),
                             np.asarray(i["L0_attn_rpb2"], np.float32)) for hh in (0, 1)]
    maps = []
    for c in range(8):
        b, hh = divmod(c, 2)
        if hh == 0:
            xs = np.asarray(x[b, 0:2080], np.float32)
        else:
            xs = np.concatenate([np.asarray(x[b, 2020:4096], np.float32),
                                 np.zeros((4, D), np.float32)], axis=0)
        m = dict(shared)
        m["xT"] = np.ascontiguousarray(xs.T)
        m["bias6"] = bias_h[hh]
        maps.append(m)
    return maps

def kernel(**inputs):
    i = {k: np.asarray(v) for k, v in inputs.items()}
    if "main" not in _NC_CACHE:
        _NC_CACHE["main"] = build_program()[0]
    nc = _NC_CACHE["main"]
    res = run_bass_kernel_spmd(nc, _in_maps(i), core_ids=list(range(8)))
    out = np.empty((4, 4096, D), np.float32)
    for c in range(8):
        b, hh = divmod(c, 2)
        o = np.asarray(res.results[c]["out"])
        if hh == 0:
            out[b, 0:2048] = o[0:2048]
        else:
            out[b, 2048:4096] = o[28:28 + 2048]
    return out
